# version 04
# baseline (speedup 1.0000x reference)
import contextlib
import numpy as np
import concourse.bass as bass
import concourse.mybir as mybir
from concourse.bass_utils import run_bass_kernel_spmd

F32 = mybir.dt.float32
BF16 = mybir.dt.bfloat16
AF = mybir.ActivationFunctionType
ALU = mybir.AluOpType
AX = mybir.AxisListType

D = 1024
S = 4096
DT = D // 128
TT = S // 128
MT = 1024
E = 8
I_FF = 4096
IB = 1024
NEGBIG = -1.0e5
KB = 1024

C_INV32, C_EPS5, C_EPS6, C_NEG1, C_INV1024, C_HALF, C_ZERO, C_ONE = range(8)
CONST_VALS = [1.0 / 32, 1e-5, 1e-6, -1.0, 1.0 / 1024, 0.5, 0.0, 1.0]


def ts(i, n=128):
    return slice(i * n, (i + 1) * n)


class Ctx:
    def __init__(self, nc):
        self.nc = nc
        self.cnt = {}
        self.waited = {}
        self._h = {}

    def sem(self, name, stack):
        self._h[name] = stack.enter_context(self.nc.semaphore(name))
        self.cnt[name] = 0

    def inc(self, inst, name, n=1):
        inst.then_inc(self._h[name], n)
        self.cnt[name] += n
        return (name, self.cnt[name])

    def tok(self, name):
        return (name, self.cnt[name])

    def all_toks(self):
        return [self.tok(s) for s in self.cnt]

    def wait(self, eng, ename, *tokens):
        w = self.waited.setdefault(ename, {})
        for name, val in tokens:
            if val <= 0 or w.get(name, 0) >= val:
                continue
            eng.wait_ge(self._h[name], val)
            w[name] = val


def build_l0(dbg=False):
    nc = bass.Bass("TRN2", target_bir_lowering=False)
    ap = {}
    for nm, shp in [("x0", [S, D]), ("x1", [S, D]), ("x_moe", [MT, D]),
                    ("wqT", [D, D]), ("wkT", [D, D]), ("wvT", [D, D]),
                    ("woT", [D, D]), ("router_wT", [D, E]),
                    ("router_b", [1, E]), ("ident", [128, 128]),
                    ("consts", [128, 8])]:
        ap[nm] = nc.dram_tensor(nm, shp, F32, kind="ExternalInput").ap()
    ap["masks"] = nc.dram_tensor("masks", [4, 128, 512], BF16,
                                 kind="ExternalInput").ap()
    for nm, shp in [("wfr", [D, D]), ("wfi", [D, D]), ("Ffr", [D, D]),
                    ("Ffi", [D, D]), ("fcol", [D, 128]), ("gcol", [D, 128]),
                    ("ngcol", [D, 128])]:
        ap[nm] = nc.dram_tensor(nm, shp, F32, kind="ExternalInput").ap()
    ap["m_slice"] = nc.dram_tensor("m_slice", [128, D], F32,
                                   kind="ExternalOutput").ap()
    if dbg:
        for nm in ["dbg_arc", "dbg_brc", "dbg_atr", "dbg_btr"]:
            ap[nm] = nc.dram_tensor(nm, [128, D], F32,
                                    kind="ExternalOutput").ap()
    ap["ei"] = nc.dram_tensor("ei", [E, D, I_FF], F32, kind="ExternalInput").ap()
    ap["eo"] = nc.dram_tensor("eo", [E, I_FF, D], F32, kind="ExternalInput").ap()
    ap["y0"] = nc.dram_tensor("y0", [S, D], F32, kind="ExternalOutput").ap()
    ap["y1o"] = nc.dram_tensor("y1o", [S, D], F32, kind="ExternalOutput").ap()
    ap["moe_out"] = nc.dram_tensor("moe_out", [MT, D], F32,
                                   kind="ExternalOutput").ap()
    kind_dbg = "ExternalOutput" if dbg else "Internal"
    sc_q = nc.dram_tensor("sc_q", [128, DT * S], BF16, kind=kind_dbg).ap()
    sc_k = nc.dram_tensor("sc_k", [128, DT * S], BF16, kind=kind_dbg).ap()
    sc_v = nc.dram_tensor("sc_v", [S, D], BF16, kind=kind_dbg).ap()
    if dbg:
        ap["dbg_xnTm"] = nc.dram_tensor("dbg_xnTm", [128, DT * MT], BF16,
                                        kind="ExternalOutput").ap()
        ap["dbg_wrt"] = nc.dram_tensor("dbg_wrt", [128, 8 * E], F32,
                                       kind="ExternalOutput").ap()
        ap["dbg_int"] = nc.dram_tensor("dbg_int", [128, (IB // 128) * MT],
                                       BF16, kind="ExternalOutput").ap()
        ap["dbg_xnT"] = nc.dram_tensor("dbg_xnT", [128, DT * S], BF16,
                                       kind="ExternalOutput").ap()
        ap["dbg_scrow"] = nc.dram_tensor("dbg_scrow", [128, S], F32,
                                         kind="ExternalOutput").ap()
        ap["dbg_prow"] = nc.dram_tensor("dbg_prow", [128, S], BF16,
                                        kind="ExternalOutput").ap()
        ap["dbg_ctx"] = nc.dram_tensor("dbg_ctx", [128, D], BF16,
                                       kind="ExternalOutput").ap()

    sb = {}

    def A(name, shape, dtype):
        sb[name] = nc.alloc_sbuf_tensor(name, shape, dtype)
        return sb[name]

    A("cst", [128, 8], F32)
    A("idn", [128, 128], BF16)
    A("mask", [128, 4 * 512], BF16)
    A("ones1", [1, 128], BF16)
    A("rb_b", [128, E], F32)
    A("red", [128, 64], F32)
    A("rw", [128, DT * E], BF16)
    A("wrt", [128, 8 * E], F32)
    A("pt", [128, 8 * 128], BF16)
    A("ctxq", [128, D], BF16)
    A("ctxT", [128, DT * 128], BF16)
    A("idnf", [128, 128], F32)

    BASE = 31 * KB
    assert nc.bytes_allocated(sb["cst"].space) <= BASE, \
        nc.bytes_allocated(sb["cst"].space)
    OA = BASE
    OB = BASE + 64 * KB
    OC = BASE + 128 * KB

    def AT(name, shape, dtype, off):
        sb[name] = nc.alloc_sbuf_tensor_at(name, shape, dtype, offset=off)
        return sb[name]

    AT("xnT", [128, DT * S], BF16, OA)
    AT("ktA", [128, DT * S], BF16, OA)
    AT("xnTm", [128, DT * MT], BF16, OA)
    AT("wib", [128, DT * IB], BF16, OA + 16 * KB)
    AT("wob", [128, (IB // 128) * D], BF16, OA + 32 * KB)
    AT("intT", [128, (IB // 128) * MT], BF16, OA + 48 * KB)
    AT("vt", [128, TT * D], BF16, OB)
    AT("xw", [128, 4 * D], F32, OB)
    AT("xnw", [128, 4 * D], BF16, OB + 16 * KB)
    AT("lnscr", [128, D], F32, OB + 24 * KB)
    AT("stg", [128, 4 * 512], BF16, OB + 28 * KB)
    AT("acc", [128, 8 * D], F32, OB + 16 * KB)
    AT("wq", [128, DT * D], BF16, OC)
    AT("wk", [128, DT * D], BF16, OC + 16 * KB)
    AT("wv", [128, DT * D], BF16, OC + 32 * KB)
    AT("wo", [128, DT * D], BF16, OC + 48 * KB)
    AT("Frf", [128, DT * D], F32, OA)
    AT("Fif", [128, DT * D], F32, OA + 32 * KB)
    AT("Wrf", [128, DT * D], F32, OC)
    AT("Wif", [128, DT * D], F32, OC + 32 * KB)
    AT("fcl", [128, DT * 128], F32, OB + 32 * KB)
    AT("gcl", [128, DT * 128], F32, OB + 36 * KB)
    AT("ngc", [128, DT * 128], F32, OB + 40 * KB)
    AT("arc", [128, D], F32, OB + 44 * KB)
    AT("brc", [128, D], F32, OB + 48 * KB)
    AT("atr", [128, DT * 128], F32, OB + 52 * KB)
    AT("btr", [128, DT * 128], F32, OB + 56 * KB)
    AT("mou", [128, D], F32, OB + 60 * KB)
    AT("scrow", [128, S], F32, OC)
    AT("prow", [128, S], BF16, OC + 16 * KB)
    AT("qblk", [128, DT * 512], BF16, OC + 24 * KB)
    AT("qblk2", [128, DT * 512], BF16, OC + 32 * KB)
    AT("xres", [128, D], F32, OC + 40 * KB)
    AT("y1s", [128, D], F32, OC + 44 * KB)

    p512a = nc.alloc_psum_tensor("p512a", [128, 512], F32)
    p512b = nc.alloc_psum_tensor("p512b", [128, 512], F32)
    p1024 = nc.alloc_psum_tensor("p1024", [128, 1024], F32)
    pTa = nc.alloc_psum_tensor("pTa", [128, 512], BF16)
    pTb = nc.alloc_psum_tensor("pTb", [128, 512], BF16)
    pS = nc.alloc_psum_tensor("pS", [128, 512], F32)

    stack = contextlib.ExitStack()
    ctx = Ctx(nc)
    for s_ in ["dg", "ds", "pe", "ac", "ve", "gp"]:
        ctx.sem(s_, stack)
    blk = stack.enter_context(nc.Block())

    state = {"psum_free": {}}

    def pfree(p):
        return state["psum_free"].get(p.name, ("pe", 0))

    def set_pfree(p, tok):
        state["psum_free"][p.name] = tok

    def cstc(i):
        return sb["cst"][:, i:i + 1]

    @blk.gpsimd
    def _(g):
        ctx.inc(g.dma_start(
            out=sb["rw"][:].rearrange("p (a o) -> p a o", a=DT),
            in_=ap["router_wT"].rearrange("(a p) o -> p a o", p=128)), "dg", 16)
        ctx.inc(g.dma_start(out=sb["idn"][:], in_=ap["ident"]), "dg", 16)
    t_wload0 = ctx.tok("dg")

    @blk.sync
    def _(s):
        ctx.inc(s.dma_start(out=sb["cst"][:], in_=ap["consts"]), "ds", 16)
        ctx.inc(s.dma_start(
            out=sb["mask"][:].rearrange("p (a o) -> p a o", a=4),
            in_=ap["masks"].rearrange("a p o -> p a o")), "ds", 16)
        ctx.inc(s.dma_start(out=sb["red"][0:1, 0:E], in_=ap["router_b"]),
                "ds", 16)
    t_cload = ctx.tok("ds")

    @blk.sync
    def _(s):
        for nm, dst in [("Ffr", "Frf"), ("Ffi", "Fif"), ("wfr", "Wrf"),
                        ("wfi", "Wif")]:
            ctx.inc(s.dma_start(
                out=sb[dst][:].rearrange("p (a o) -> p a o", a=DT),
                in_=ap[nm].rearrange("(a p) o -> p a o", p=128)), "ds", 16)
        for nm, dst in [("fcol", "fcl"), ("gcol", "gcl"), ("ngcol", "ngc")]:
            ctx.inc(s.dma_start(
                out=sb[dst][:].rearrange("p (a o) -> p a o", a=DT),
                in_=ap[nm].rearrange("(a p) o -> p a o", p=128)), "ds", 16)
        ctx.inc(s.dma_start(out=sb["idnf"][:], in_=ap["ident"]), "ds", 16)
    t_mload = ctx.tok("ds")

    def mm_mrow(lhs1, rhs1, lhs2, rhs2, out_s, scale_ap=None):
        for nb in range(2):
            p = p512a if nb == 0 else p512b

            @blk.tensor
            def _(t, nb=nb, p=p):
                ctx.wait(t, "tensor", t_mload, pfree(p))
                for k in range(DT):
                    t.matmul(p[:],
                             sb[lhs1][:, k * 128:(k + 1) * 128],
                             sb[rhs1][:, k * D + nb * 512:
                                      k * D + nb * 512 + 512],
                             start=(k == 0), stop=False)
                for k in range(DT):
                    last = t.matmul(
                        p[:], sb[lhs2][:, k * 128:(k + 1) * 128],
                        sb[rhs2][:, k * D + nb * 512: k * D + nb * 512 + 512],
                        start=False, stop=(k == DT - 1))
                ctx.inc(last, "pe")
            tk = ctx.tok("pe")

            @blk.scalar
            def _(sc, nb=nb, p=p, tk=tk):
                ctx.wait(sc, "scalar", tk)
                if scale_ap is not None:
                    ctx.inc(sc.activation(sb[out_s][:, nb * 512:
                                                    nb * 512 + 512],
                                          p[:], AF.Copy, scale=scale_ap),
                            "ac")
                else:
                    ctx.inc(sc.copy(sb[out_s][:, nb * 512: nb * 512 + 512],
                                    p[:]), "ac")
            set_pfree(p, ctx.tok("ac"))

    mm_mrow("fcl", "Wrf", "ngc", "Wif", "arc")
    mm_mrow("fcl", "Wif", "gcl", "Wrf", "brc")
    t_ab = ctx.tok("ac")

    for src, dst in [("arc", "atr"), ("brc", "btr")]:
        for half in range(2):
            @blk.tensor
            def _(t, src=src, half=half):
                ctx.wait(t, "tensor", t_ab, pfree(pS))
                for q in range(4):
                    a = half * 4 + q
                    last = t.transpose(pS[:, ts(q)],
                                       sb[src][:, ts(a)], sb["idnf"][:])
                ctx.inc(last, "pe")
            tk = ctx.tok("pe")

            @blk.scalar
            def _(sc, dst=dst, half=half, tk=tk):
                ctx.wait(sc, "scalar", tk)
                for q in range(4):
                    a = half * 4 + q
                    last = sc.copy(sb[dst][:, ts(a)], pS[:, ts(q)])
                ctx.inc(last, "ac")
            set_pfree(pS, ctx.tok("ac"))
    t_abt = ctx.tok("ac")
    if dbg:
        @blk.sync
        def _(s):
            ctx.wait(s, "sync", t_abt, t_ab)
            for nm, src in [("dbg_arc", "arc"), ("dbg_brc", "brc"),
                            ("dbg_atr", "atr"), ("dbg_btr", "btr")]:
                ctx.inc(s.dma_start(out=ap[nm], in_=sb[src][:]), "ds", 16)

    @blk.tensor
    def _(t):
        ctx.wait(t, "tensor", t_abt)
    mm_mrow("atr", "Frf", "btr", "Fif", "mou", scale_ap=cstc(C_INV1024))
    t_msc = ctx.tok("ac")

    @blk.sync
    def _(s):
        ctx.wait(s, "sync", t_msc)
        ctx.inc(s.dma_start(out=ap["m_slice"], in_=sb["mou"][:]), "ds", 16)
    t_mdone = ctx.tok("ds")

    @blk.gpsimd
    def _(g):
        ctx.wait(g, "gpsimd", ctx.tok("pe"))
        for nm, dst in [("wqT", "wq"), ("wkT", "wk"), ("wvT", "wv"),
                        ("woT", "wo")]:
            ctx.inc(g.dma_start(
                out=sb[dst][:].rearrange("p (a o) -> p a o", a=DT),
                in_=ap[nm].rearrange("(a p) o -> p a o", p=128)), "dg", 16)
    t_wload = ctx.tok("dg")


    @blk.vector
    def _(v):
        ctx.wait(v, "vector", t_cload)
        v.memset(sb["ones1"][:], 1.0)
        ctx.inc(v.tensor_copy(sb["stg"][0:1, 0:E], sb["red"][0:1, 0:E]), "ve")
    t_ones = ctx.tok("ve")

    @blk.tensor
    def _(t):
        ctx.wait(t, "tensor", t_ones)
        ctx.inc(t.matmul(pS[:, 0:E], sb["ones1"][:], sb["stg"][0:1, 0:E],
                         start=True, stop=True), "pe")
    t_rbm = ctx.tok("pe")

    @blk.scalar
    def _(sc):
        ctx.wait(sc, "scalar", t_rbm)
        ctx.inc(sc.copy(sb["rb_b"][:], pS[:, 0:E]), "ac")
    set_pfree(pS, ctx.tok("ac"))

    def emit_ln_wave(x_in_view, deps):
        @blk.sync
        def _(s):
            ctx.wait(s, "sync", *deps, state.get("m_guard", ("ds", 0)))
            ctx.inc(s.dma_start(out=sb["xw"][:].rearrange(
                "p (j d) -> p j d", j=4), in_=x_in_view), "ds", 16)
        t_x = ctx.tok("ds")

        @blk.scalar
        def _(sc):
            ctx.wait(sc, "scalar", t_cload, t_x, *deps)
            for j in range(4):
                sc.activation(sb["lnscr"][:], sb["xw"][:, ts(j, D)],
                              AF.Square, accum_out=sb["red"][:, j:j + 1])
            sc.drain()
            ctx.inc(sc.copy(sb["red"][:, 4:8], sb["red"][:, 0:4]), "ac")
        t_s2 = ctx.tok("ac")

        @blk.vector
        def _(v):
            ctx.wait(v, "vector", t_x, *deps)
            for j in range(4):
                v.tensor_scalar(sb["xnw"][:, ts(j, D)], sb["xw"][:, ts(j, D)],
                                1.0, 0.0, ALU.mult, ALU.add,
                                accum_out=sb["red"][:, 8 + j:9 + j])
            ctx.wait(v, "vector", t_s2)
            v.tensor_scalar(sb["red"][:, 16:20], sb["red"][:, 8:12],
                            1.0 / D, None, ALU.mult)
            v.drain()
            v.tensor_tensor(sb["red"][:, 24:28], sb["red"][:, 16:20],
                            sb["red"][:, 16:20], ALU.mult)
            v.tensor_scalar(sb["red"][:, 4:8], sb["red"][:, 4:8],
                            1.0 / D, 1e-5, ALU.mult, ALU.add)
            v.drain()
            ctx.inc(v.tensor_sub(sb["red"][:, 24:28], sb["red"][:, 4:8],
                                 sb["red"][:, 24:28]), "ve")
        t_var = ctx.tok("ve")

        @blk.scalar
        def _(sc):
            ctx.wait(sc, "scalar", t_var)
            ctx.inc(sc.activation(sb["red"][:, 32:36], sb["red"][:, 24:28],
                                  AF.Sqrt), "ac")
        t_sd = ctx.tok("ac")

        @blk.vector
        def _(v):
            ctx.wait(v, "vector", t_sd)
            v.reciprocal(sb["red"][:, 32:36], sb["red"][:, 32:36])
            v.drain()
            for j in range(4):
                last = v.tensor_scalar(sb["xnw"][:, ts(j, D)],
                                       sb["xw"][:, ts(j, D)],
                                       sb["red"][:, 16 + j:17 + j],
                                       sb["red"][:, 32 + j:33 + j],
                                       ALU.subtract, ALU.mult)
            ctx.inc(last, "ve")
        return ctx.tok("ve")

    def emit_transpose_wave(src, src_col0, n_tiles, dst, dst_T, dst_tcol,
                            deps):
        for j in range(n_tiles):
            for half, p in enumerate([pTa, pTb]):
                @blk.tensor
                def _(t, j=j, half=half, p=p):
                    ctx.wait(t, "tensor", t_wload, pfree(p), *deps)
                    for q in range(4):
                        a = half * 4 + q
                        last = t.transpose(
                            p[:, ts(q)],
                            src[:, src_col0 + j * D + a * 128:
                                src_col0 + j * D + a * 128 + 128],
                            sb["idn"][:])
                    ctx.inc(last, "pe")
                t_tr = ctx.tok("pe")

                @blk.scalar
                def _(sc, j=j, half=half, p=p, t_tr=t_tr):
                    ctx.wait(sc, "scalar", t_tr)
                    tc = dst_tcol + j * 128
                    for q in range(4):
                        a = half * 4 + q
                        last = sc.copy(
                            dst[:, a * dst_T + tc: a * dst_T + tc + 128],
                            p[:, ts(q)])
                    ctx.inc(last, "ac")
                set_pfree(p, ctx.tok("ac"))
        return ctx.tok("ac")

    def emit_mm(lhsT_of, rhs_of, n_m, n_n, n_k, evac, deps=(), n_free=512):
        for mi in range(n_m):
            for ni in range(n_n):
                p = p512a if (mi * n_n + ni) % 2 == 0 else p512b

                @blk.tensor
                def _(t, mi=mi, ni=ni, p=p):
                    ctx.wait(t, "tensor", t_wload, pfree(p), *deps)
                    for k in range(n_k):
                        last = t.matmul(p[:, 0:n_free], lhsT_of(k, mi),
                                        rhs_of(k, ni), start=(k == 0),
                                        stop=(k == n_k - 1))
                    ctx.inc(last, "pe")
                set_pfree(p, evac(mi, ni, p, ctx.tok("pe")))

    state["m_guard"] = t_mdone
    t_prev = ctx.tok("ac")
    xmoe_v = ap["x_moe"].rearrange("(j p) d -> p j d", p=128)
    for w in range(MT // 512):
        t_ln = emit_ln_wave(xmoe_v[:, 4 * w:4 * w + 4, :], [t_prev])
        t_prev = emit_transpose_wave(sb["xnw"], 0, 4, sb["xnTm"], MT,
                                     w * 512, [t_ln])
    t_xnTm = t_prev

    for tt2 in range(MT // 128):
        @blk.tensor
        def _(t, tt2=tt2):
            ctx.wait(t, "tensor", t_xnTm, pfree(pS))
            for k in range(DT):
                last = t.matmul(
                    pS[:, 0:E],
                    sb["xnTm"][:, k * MT + tt2 * 128: k * MT + tt2 * 128 + 128],
                    sb["rw"][:, ts(tt2 * 0 + k, E)], start=(k == 0),
                    stop=(k == DT - 1))
            ctx.inc(last, "pe")
        t_lg = ctx.tok("pe")

        @blk.vector
        def _(v, t_lg=t_lg):
            ctx.wait(v, "vector", t_lg)
            v.tensor_tensor(sb["red"][:, 40:40 + E], pS[:, 0:E],
                            sb["rb_b"][:], ALU.add)
            v.drain()
            ctx.inc(v.reduce_max(sb["red"][:, 48:49], sb["red"][:, 40:40 + E],
                                 axis=AX.X, negate=True), "ve")
        set_pfree(pS, ctx.tok("ve"))
        t_mx = ctx.tok("ve")

        @blk.scalar
        def _(sc, t_mx=t_mx):
            ctx.wait(sc, "scalar", t_mx)
            sc.activation(sb["red"][:, 40:40 + E],
                          sb["red"][:, 40:40 + E], AF.Exp,
                          bias=sb["red"][:, 48:49],
                          accum_out=sb["red"][:, 49:50])
            sc.drain()
            ctx.inc(sc.copy(sb["red"][:, 50:51], sb["red"][:, 49:50]), "ac")
        t_exp = ctx.tok("ac")

        @blk.vector
        def _(v, tt2=tt2, t_exp=t_exp):
            ctx.wait(v, "vector", t_exp)
            v.reciprocal(sb["red"][:, 50:51], sb["red"][:, 50:51])
            v.drain()
            ctx.inc(v.tensor_scalar(sb["wrt"][:, ts(tt2, E)],
                                    sb["red"][:, 40:40 + E],
                                    sb["red"][:, 50:51], None, ALU.mult), "ve")
    t_wrt = ctx.tok("ve")

    if dbg:
        @blk.sync
        def _(s):
            ctx.wait(s, "sync", t_xnTm, t_wrt)
            ctx.inc(s.dma_start(out=ap["dbg_xnTm"], in_=sb["xnTm"][:]),
                    "ds", 16)
            ctx.inc(s.dma_start(out=ap["dbg_wrt"], in_=sb["wrt"][:]),
                    "ds", 16)

    @blk.gpsimd
    def _(g):
        ctx.wait(g, "gpsimd", t_xnTm)
        ctx.inc(g.memset(sb["acc"][:], 0.0), "gp")
    t_accz = ctx.tok("gp")

    t_int_used = ("pe", 0)
    for e in range(E):
        for ib in range(I_FF // IB):
            @blk.gpsimd
            def _(g, e=e, ib=ib, t_int_used=t_int_used):
                ctx.wait(g, "gpsimd", t_int_used)
                ctx.inc(g.dma_start(
                    out=sb["wib"][:].rearrange("p (a i) -> p a i", a=DT),
                    in_=ap["ei"][e].rearrange("(a p) i -> p a i", p=128)
                    [:, :, ib * IB:(ib + 1) * IB]), "dg", 16)
                ctx.inc(g.dma_start(
                    out=sb["wob"][:].rearrange("p (a o) -> p a o",
                                               a=IB // 128),
                    in_=ap["eo"][e].rearrange("(a p) o -> p a o", p=128)
                    [:, 8 * ib:8 * ib + 8, :]), "dg", 16)
            t_w = ctx.tok("dg")

            def ev_int(mi, ni, p, tok):
                @blk.scalar
                def _(sc, mi=mi, ni=ni, p=p, tok=tok):
                    ctx.wait(sc, "scalar", tok)
                    ctx.inc(sc.activation(
                        sb["intT"][:, mi * MT + ni * 512:
                                   mi * MT + ni * 512 + 512],
                        p[:], AF.Relu), "ac")
                return ctx.tok("ac")

            emit_mm(
                lhsT_of=lambda k, mi: sb["wib"][:, k * IB + mi * 128:
                                                k * IB + mi * 128 + 128],
                rhs_of=lambda k, ni: sb["xnTm"][:, k * MT + ni * 512:
                                               k * MT + ni * 512 + 512],
                n_m=IB // 128, n_n=MT // 512, n_k=DT, evac=ev_int,
                deps=[t_w, t_xnTm])
            t_int = ctx.tok("ac")

            def ev_out(mi, ni, p, tok, e=e):
                @blk.vector
                def _(v, mi=mi, ni=ni, p=p, tok=tok, e=e):
                    ctx.wait(v, "vector", tok, t_wrt, t_accz)
                    ctx.inc(v.scalar_tensor_tensor(
                        sb["acc"][:, mi * D + ni * 512:
                                  mi * D + ni * 512 + 512],
                        p[:], sb["wrt"][:, mi * E + e: mi * E + e + 1],
                        sb["acc"][:, mi * D + ni * 512:
                                  mi * D + ni * 512 + 512],
                        ALU.mult, ALU.add), "ve")
                return ctx.tok("ve")

            if dbg and e == 0 and ib == 0:
                @blk.sync
                def _(s, t_int=t_int):
                    ctx.wait(s, "sync", t_int)
                    ctx.inc(s.dma_start(out=ap["dbg_int"], in_=sb["intT"][:]),
                            "ds", 16)

            emit_mm(
                lhsT_of=lambda k, mi: sb["intT"][:, k * MT + mi * 128:
                                                 k * MT + mi * 128 + 128],
                rhs_of=lambda k, ni: sb["wob"][:, k * D + ni * 512:
                                               k * D + ni * 512 + 512],
                n_m=MT // 128, n_n=D // 512, n_k=IB // 128, evac=ev_out,
                deps=[t_int])
            t_int_used = ctx.tok("pe")
    t_acc_last = ctx.tok("ve")

    @blk.sync
    def _(s):
        ctx.wait(s, "sync", t_acc_last)
        ctx.inc(s.dma_start(
            out=ap["moe_out"].rearrange("(a p) o -> p a o", p=128),
            in_=sb["acc"][:].rearrange("p (a o) -> p a o", a=8)), "ds", 16)
    t_moe_out = ctx.tok("ds")

    for b, (x_in, y_out) in enumerate([(ap["x0"], ap["y0"]),
                                       (ap["x1"], ap["y1o"])]):
        dep_A = [t_int_used, t_moe_out] if b == 0 else ctx.all_toks()
        if b == 1:
            @blk.gpsimd
            def _(g):
                ctx.wait(g, "gpsimd", *dep_A)
                for nm, dst in [("wqT", "wq"), ("wkT", "wk"), ("wvT", "wv")]:
                    ctx.inc(g.dma_start(
                        out=sb[dst][:].rearrange("p (a o) -> p a o", a=DT),
                        in_=ap[nm].rearrange("(a p) o -> p a o", p=128)),
                        "dg", 16)
            dep_A = [ctx.tok("dg")]

        x_v = x_in.rearrange("(j p) d -> p j d", p=128)
        t_prev = ctx.tok("ac")
        for w in range(S // 512):
            t_ln = emit_ln_wave(x_v[:, 4 * w:4 * w + 4, :], [t_prev] + dep_A)
            t_prev = emit_transpose_wave(sb["xnw"], 0, 4, sb["xnT"], S,
                                         w * 512, [t_ln])
        t_xnT = t_prev
        if dbg and b == 0:
            @blk.sync
            def _(s):
                ctx.wait(s, "sync", t_xnT)
                ctx.inc(s.dma_start(out=ap["dbg_xnT"], in_=sb["xnT"][:]),
                        "ds", 16)

        stg_free = [ctx.tok("ds")] * 4
        stg_i = [0]

        def ev_spill(col_of):
            def ev(mi, ni, p, tok):
                sl = stg_i[0] % 4
                stg_i[0] += 1

                @blk.scalar
                def _(sc, sl=sl, p=p, tok=tok):
                    ctx.wait(sc, "scalar", tok, stg_free[sl])
                    ctx.inc(sc.copy(sb["stg"][:, ts(sl, 512)], p[:]), "ac")
                t_e = ctx.tok("ac")

                @blk.sync
                def _(s, sl=sl, mi=mi, ni=ni, t_e=t_e):
                    ctx.wait(s, "sync", t_e)
                    ctx.inc(s.dma_start(out=col_of(mi, ni),
                                        in_=sb["stg"][:, ts(sl, 512)]),
                            "ds", 16)
                stg_free[sl] = ctx.tok("ds")
                return t_e
            return ev

        for wname, dst in [("wk", sc_k), ("wq", sc_q)]:
            emit_mm(
                lhsT_of=lambda k, mi, wname=wname: sb[wname][
                    :, k * D + mi * 128: k * D + mi * 128 + 128],
                rhs_of=lambda k, ni: sb["xnT"][:, k * S + ni * 512:
                                               k * S + ni * 512 + 512],
                n_m=DT, n_n=S // 512, n_k=DT,
                evac=ev_spill(lambda mi, ni, dst=dst: dst[
                    :, mi * S + ni * 512: mi * S + ni * 512 + 512]),
                deps=[t_xnT])
        emit_mm(
            lhsT_of=lambda k, mi: sb["xnT"][:, k * S + mi * 128:
                                            k * S + mi * 128 + 128],
            rhs_of=lambda k, ni: sb["wv"][:, k * D + ni * 512:
                                          k * D + ni * 512 + 512],
            n_m=TT, n_n=D // 512, n_k=DT,
            evac=ev_spill(lambda mi, ni: sc_v[ts(mi),
                                              ni * 512:(ni + 1) * 512]),
            deps=[t_xnT])
        t_proj_pe = ctx.tok("pe")
        t_proj_st = ctx.tok("ds")

        @blk.sync
        def _(s):
            ctx.wait(s, "sync", t_proj_pe, t_proj_st, ctx.tok("ve"),
                     ctx.tok("ac"))
            ctx.inc(s.dma_start(out=sb["ktA"][:], in_=sc_k), "ds", 16)
            ctx.inc(s.dma_start(
                out=sb["vt"][:].rearrange("p (a o) -> p a o", a=TT),
                in_=sc_v.rearrange("(a p) o -> p a o", p=128)), "ds", 16)
        t_kv = ctx.tok("ds")

        qb_bufs = [sb["qblk"], sb["qblk2"]]
        qb_tok = [("ds", 0), ("ds", 0)]
        qb_used = [("pe", 0), ("pe", 0)]
        t_exp_prev = ("ac", 0)
        t_prowread_prev = ("pe", 0)
        t_y1_ve = ("ve", 0)
        t_y1_st = ("ds", 0)
        for qb in range(S // 512):
            bi = qb % 2

            @blk.sync
            def _(s, qb=qb, bi=bi):
                ctx.wait(s, "sync", t_proj_st, qb_used[bi], t_kv)
                ctx.inc(s.dma_start(
                    out=qb_bufs[bi][:].rearrange("p (a t) -> p a t", a=DT),
                    in_=sc_q.rearrange("p (a t) -> p a t", a=DT)
                    [:, :, qb * 512:(qb + 1) * 512]), "ds", 16)
            qb_tok[bi] = ctx.tok("ds")

            for qq in range(4):
                qt = qb * 4 + qq
                nkb = qt // 4 + 1
                par = qt % 2
                cF = 16 if par == 0 else 20
                for kb2 in range(nkb):
                    p = p512a if kb2 % 2 == 0 else p512b

                    @blk.tensor
                    def _(t, qq=qq, kb2=kb2, p=p, bi=bi):
                        ctx.wait(t, "tensor", qb_tok[bi], t_kv, pfree(p))
                        for k in range(DT):
                            last = t.matmul(
                                p[:],
                                qb_bufs[bi][:, k * 512 + qq * 128:
                                            k * 512 + qq * 128 + 128],
                                sb["ktA"][:, k * S + kb2 * 512:
                                          k * S + kb2 * 512 + 512],
                                start=(k == 0), stop=(k == DT - 1))
                        ctx.inc(last, "pe")
                    t_sc = ctx.tok("pe")

                    @blk.vector
                    def _(v, qt=qt, kb2=kb2, p=p, t_sc=t_sc, nkb=nkb):
                        ctx.wait(v, "vector", t_sc, t_cload, t_exp_prev)
                        if kb2 == nkb - 1:
                            v.tensor_tensor(sb["scrow"][:, ts(kb2, 512)],
                                            p[:],
                                            sb["mask"][:, ts(qt % 4, 512)],
                                            ALU.add)
                        else:
                            v.tensor_copy(sb["scrow"][:, ts(kb2, 512)], p[:])
                        ctx.inc(v.reduce_max(
                            sb["red"][:, kb2:kb2 + 1],
                            p[:], axis=AX.X), "ve")
                    set_pfree(p, ctx.tok("ve"))

                @blk.vector
                def _(v, nkb=nkb, cF=cF):
                    v.drain()
                    if nkb > 1:
                        v.reduce_max(sb["red"][:, cF:cF + 1],
                                     sb["red"][:, 0:nkb], axis=AX.X)
                    else:
                        v.tensor_copy(sb["red"][:, cF:cF + 1],
                                      sb["red"][:, 0:1])
                    v.drain()
                    ctx.inc(v.tensor_scalar(sb["red"][:, cF + 1:cF + 2],
                                            sb["red"][:, cF:cF + 1],
                                            -1.0 / 32, None, ALU.mult), "ve")
                t_max = ctx.tok("ve")

                @blk.scalar
                def _(sc, nkb=nkb, cF=cF, t_max=t_max,
                      t_prowread_prev=t_prowread_prev):
                    ctx.wait(sc, "scalar", t_max, t_prowread_prev)
                    cP = 32 + 8 * (cF == 20)
                    for kb2 in range(nkb):
                        sc.activation(
                            sb["prow"][:, ts(kb2, 512)],
                            sb["scrow"][:, ts(kb2, 512)], AF.Exp,
                            bias=sb["red"][:, cF + 1:cF + 2],
                            scale=cstc(C_INV32),
                            accum_out=sb["red"][:, 24 + kb2:25 + kb2])
                    sc.drain()
                    ctx.inc(sc.copy(sb["red"][:, cP:cP + nkb],
                                    sb["red"][:, 24:24 + nkb]), "ac")
                t_exp_prev = ctx.tok("ac")
                t_exp = t_exp_prev

                @blk.vector
                def _(v, nkb=nkb, cF=cF, t_exp=t_exp):
                    ctx.wait(v, "vector", t_exp)
                    cP = 32 + 8 * (cF == 20)
                    v.reduce_sum(sb["red"][:, cF + 2:cF + 3],
                                 sb["red"][:, cP:cP + nkb], axis=AX.X)
                    v.drain()
                    ctx.inc(v.reciprocal(sb["red"][:, cF + 3:cF + 4],
                                         sb["red"][:, cF + 2:cF + 3]), "ve")
                t_linv = ctx.tok("ve")

                nkt = nkb * 4
                for g0 in range(0, nkt, 4):
                    p = pTa if (g0 // 4) % 2 == 0 else pTb
                    sl0 = ((g0 // 4) % 2) * 4

                    @blk.tensor
                    def _(t, g0=g0, p=p, t_exp=t_exp, nkt=nkt):
                        ctx.wait(t, "tensor", t_exp, pfree(p))
                        for q in range(min(4, nkt - g0)):
                            last = t.transpose(p[:, ts(q)],
                                               sb["prow"][:, ts(g0 + q)],
                                               sb["idn"][:])
                        ctx.inc(last, "pe")
                    t_tr = ctx.tok("pe")

                    @blk.scalar
                    def _(sc, g0=g0, p=p, t_tr=t_tr, sl0=sl0, nkt=nkt):
                        ctx.wait(sc, "scalar", t_tr)
                        for q in range(min(4, nkt - g0)):
                            last = sc.copy(sb["pt"][:, ts(sl0 + q)],
                                           p[:, ts(q)])
                        ctx.inc(last, "ac")
                    t_pt = ctx.tok("ac")
                    set_pfree(p, t_pt)

                    @blk.tensor
                    def _(t, g0=g0, t_pt=t_pt, sl0=sl0, nkt=nkt):
                        ctx.wait(t, "tensor", t_pt,
                                 pfree(p1024) if g0 == 0 else ("pe", 0))
                        for q in range(min(4, nkt - g0)):
                            kt2 = g0 + q
                            for oh in range(2):
                                last = t.matmul(
                                    p1024[:, ts(oh, 512)],
                                    sb["pt"][:, ts(sl0 + q)],
                                    sb["vt"][:, kt2 * D + oh * 512:
                                             kt2 * D + oh * 512 + 512],
                                    start=(kt2 == 0), stop=(kt2 == nkt - 1))
                        ctx.inc(last, "pe")
                t_pv = ctx.tok("pe")
                t_prowread_prev = t_pv

                @blk.scalar
                def _(sc, t_pv=t_pv, t_linv=t_linv, cF=cF):
                    ctx.wait(sc, "scalar", t_pv, t_linv)
                    ctx.inc(sc.activation(sb["ctxq"][:], p1024[:], AF.Copy,
                                          scale=sb["red"][:, cF + 3:cF + 4]),
                            "ac")
                set_pfree(p1024, ctx.tok("ac"))
                t_ctx = ctx.tok("ac")
                if dbg and b == 0 and qt == 31:
                    @blk.sync
                    def _(s, t_exp=t_exp, t_ctx=t_ctx):
                        ctx.wait(s, "sync", t_exp, t_ctx)
                        ctx.inc(s.dma_start(out=ap["dbg_scrow"],
                                            in_=sb["scrow"][:]), "ds", 16)
                        ctx.inc(s.dma_start(out=ap["dbg_prow"],
                                            in_=sb["prow"][:]), "ds", 16)
                        ctx.inc(s.dma_start(out=ap["dbg_ctx"],
                                            in_=sb["ctxq"][:]), "ds", 16)
                t_ctxT = emit_transpose_wave(sb["ctxq"], 0, 1, sb["ctxT"],
                                             128, 0, [t_ctx])

                @blk.sync
                def _(s, qt=qt, t_y1_ve=t_y1_ve):
                    ctx.wait(s, "sync", t_y1_ve)
                    ctx.inc(s.dma_start(out=sb["xres"][:],
                                        in_=x_in[ts(qt), :]), "ds", 16)
                t_xr = ctx.tok("ds")

                for oh in range(2):
                    p = p512a if oh == 0 else p512b

                    @blk.tensor
                    def _(t, oh=oh, p=p, t_ctxT=t_ctxT):
                        ctx.wait(t, "tensor", t_ctxT, pfree(p))
                        for k in range(DT):
                            last = t.matmul(
                                p[:], sb["ctxT"][:, ts(k)],
                                sb["wo"][:, k * D + oh * 512:
                                         k * D + oh * 512 + 512],
                                start=(k == 0), stop=(k == DT - 1))
                        ctx.inc(last, "pe")
                    t_o = ctx.tok("pe")

                    @blk.vector
                    def _(v, oh=oh, p=p, t_o=t_o, t_xr=t_xr,
                          t_y1_st=t_y1_st):
                        ctx.wait(v, "vector", t_o, t_xr, t_y1_st)
                        ctx.inc(v.tensor_tensor(
                            sb["y1s"][:, ts(oh, 512)], p[:],
                            sb["xres"][:, ts(oh, 512)], ALU.add), "ve")
                    set_pfree(p, ctx.tok("ve"))
                t_y1_ve = ctx.tok("ve")

                @blk.sync
                def _(s, qt=qt, t_y1_ve=t_y1_ve):
                    ctx.wait(s, "sync", t_y1_ve)
                    ctx.inc(s.dma_start(out=y_out[ts(qt), :],
                                        in_=sb["y1s"][:]), "ds", 16)
                t_y1_st = ctx.tok("ds")
            qb_used[bi] = ctx.tok("pe")

    @blk.sync
    def _(s):
        ctx.wait(s, "sync", ctx.tok("ds"))

    stack.close()
    return nc


_CACHE = {}


def _masks_np():
    import ml_dtypes
    m = np.zeros((4, 128, 512), np.float32)
    for v in range(4):
        for i in range(128):
            m[v, i, 128 * v + i + 1:] = NEGBIG
    return m.astype(ml_dtypes.bfloat16)


def _consts_np():
    c = np.zeros((128, 8), np.float32)
    for i, val in enumerate(CONST_VALS):
        c[:, i] = val
    return c


def l0_in_maps(inputs):
    x = np.ascontiguousarray(inputs["L0_attn_in_features"], np.float32)
    wqT = np.ascontiguousarray(np.asarray(inputs["L0_attn_w_q"]).T)
    wkT = np.ascontiguousarray(np.asarray(inputs["L0_attn_w_k"]).T)
    wvT = np.ascontiguousarray(np.asarray(inputs["L0_attn_w_v"]).T)
    woT = np.ascontiguousarray(np.asarray(inputs["L0_attn_w_o"]).T)
    rwT = np.ascontiguousarray(np.asarray(inputs["L0_ffn_router_w"]).T)
    rb = np.ascontiguousarray(np.asarray(inputs["L0_ffn_router_b"])).reshape(1, E)
    ei = np.ascontiguousarray(inputs["L0_ffn_experts_inter"], np.float32)
    eo = np.ascontiguousarray(inputs["L0_ffn_experts_out"], np.float32)
    masks = _masks_np()
    consts = _consts_np()
    ident = np.eye(128, dtype=np.float32)
    xflat = x.reshape(-1, D)
    idx = np.arange(D)
    F = np.exp(-2j * np.pi * np.outer(idx, idx) / D)
    Ffr = np.ascontiguousarray(np.real(F).astype(np.float32))
    Ffi = np.ascontiguousarray(np.imag(F).astype(np.float32))
    wfr = np.ascontiguousarray(np.asarray(inputs["L1_ffn_w_freq_real"],
                                          np.float32))
    wfi = np.ascontiguousarray(np.asarray(inputs["L1_ffn_w_freq_imag"],
                                          np.float32))
    maps = []
    for c in range(8):
        rc = slice(128 * c, 128 * (c + 1))
        maps.append({
            "x0": np.ascontiguousarray(x[c]),
            "x1": np.ascontiguousarray(x[c + 8]),
            "x_moe": np.ascontiguousarray(xflat[1024 * c:1024 * (c + 1)]),
            "wqT": wqT, "wkT": wkT, "wvT": wvT, "woT": woT,
            "router_wT": rwT, "router_b": rb, "ei": ei, "eo": eo,
            "ident": ident, "masks": masks, "consts": consts,
            "wfr": wfr, "wfi": wfi, "Ffr": Ffr, "Ffi": Ffi,
            "fcol": np.ascontiguousarray(Ffr[:, rc]),
            "gcol": np.ascontiguousarray(Ffi[:, rc]),
            "ngcol": np.ascontiguousarray(-Ffi[:, rc]),
        })
    return maps


def _kernel_2launch(**inputs):
    if "l0" not in _CACHE:
        _CACHE["l0"] = build_l0()
        _CACHE["l1"] = build_l1()
    maps = l0_in_maps(inputs)
    res = run_bass_kernel_spmd(_CACHE["l0"], maps, core_ids=list(range(8)))
    y_full = np.zeros((16, S, D), np.float32)
    moe_full = np.zeros((2, S, D), np.float32)
    Mfull = np.zeros((D, D), np.float32)
    for c in range(8):
        y_full[c] = res.results[c]["y0"]
        y_full[c + 8] = res.results[c]["y1o"]
        b, r0 = divmod(1024 * c, S)
        moe_full[b, r0:r0 + 1024] = res.results[c]["moe_out"]
        Mfull[128 * c:128 * (c + 1)] = res.results[c]["m_slice"]
    maps2 = l1_in_maps(inputs, y_full, moe_full, Mfull)
    res2 = run_bass_kernel_spmd(_CACHE["l1"], maps2, core_ids=list(range(8)))
    out = np.zeros((16, S, D), np.float32)
    for c in range(8):
        out[c] = res2.results[c]["outa"]
        out[c + 8] = res2.results[c]["outb"]
    return out



F32 = mybir.dt.float32
BF16 = mybir.dt.bfloat16
AF = mybir.ActivationFunctionType
ALU = mybir.AluOpType
AX = mybir.AxisListType

D = 1024
S = 4096
DT = 8
H = 4
KB = 1024
TB = 256
NBLK = S // TB
CH = 128
NCH = S // CH
FIRS = [1, 3, 7, 31]
NTAP = sum(FIRS)

L1_CONSTS = [1.0 / 32, 1e-5, 1e-6, -1.0, 1.0 / 1024, 1.0 / 256, 0.0, 1.0,
             1.0 / 255, -256.0 / 255, 0.5, 1.0 / 64, 0, 0, 0, 0]
(LC_INV32, LC_EPS5, LC_EPS6, LC_NEG1, LC_INV1024, LC_INV256, LC_ZERO,
 LC_ONE, LC_INV255, LC_N256_255, LC_HALF, LC_INV64) = range(12)


def build_l1(dbg=False, dbg_blk=1):
    nc = bass.Bass("TRN2", target_bir_lowering=False)
    ap = {}

    def IN(nm, shp, dt=F32):
        ap[nm] = nc.dram_tensor(nm, shp, dt, kind="ExternalInput").ap()

    IN("ya", [S, D]); IN("yb", [S, D]); IN("moe_a", [S, D])
    IN("qw", [D, D]); IN("kw", [D, D]); IN("vw", [D, D]); IN("bw", [D, H])
    IN("ow", [D, D])
    IN("qcw", [D, 4]); IN("kcw", [D, 4]); IN("vcw", [D, 4])
    IN("firw", [D, NTAP])
    IN("w1h", [D, D])
    IN("w1m", [24, D]); IN("w1s", [24, D]); IN("w1a", [24, D]); IN("w1l", [24, D])
    IN("b1", [128, DT]); IN("w2", [D, 24]); IN("b2", [1, 24]); IN("glt", [1, H])
    IN("onw", [128, 2]); IN("fbias", [1, D])
    IN("Mfull", [D, D])
    IN("ident", [128, 128]); IN("consts", [128, 16])
    ap["masksb"] = nc.dram_tensor("masksb",
                                  [128, 3 * 128 + 47 + 95 + 7 + 512 + 3072],
                                  BF16, kind="ExternalInput").ap()
    ap["outa"] = nc.dram_tensor("outa", [S, D], F32, kind="ExternalOutput").ap()
    ap["outb"] = nc.dram_tensor("outb", [S, D], F32, kind="ExternalOutput").ap()
    if dbg:
        for nm, shp, dt in [
                ("dbg_xn1T", [128, DT * TB], BF16),
                ("dbg_qn", [128, DT * TB], BF16),
                ("dbg_kn", [128, DT * TB], BF16),
                ("dbg_vt", [128, DT * (TB + 32)], BF16),
                ("dbg_beta", [4, TB], F32),
                ("dbg_br", [128, 5 * DT * TB], BF16),
                ("dbg_stats", [24, 4 * TB], BF16),
                ("dbg_hg", [128, DT * TB], BF16),
                ("dbg_wgt", [24, TB], BF16),
                ("dbg_fsd", [128, DT * TB], BF16),
                ("dbg_M", [128, DT * D], BF16),
                ("dbg_dmx", [128, 24 * 128], BF16),
                ("dbg_bb", [128, CH], BF16)]:
            ap[nm] = nc.dram_tensor(nm, shp, dt, kind="ExternalOutput").ap()

    sb = {}

    def A(name, shape, dtype):
        sb[name] = nc.alloc_sbuf_tensor(name, shape, dtype)
        return sb[name]

    A("cst", [128, 16], F32)
    A("idn", [128, 128], BF16)
    A("msk", [128, 3 * 128 + 47 + 95 + 7 + 512 + 3072], BF16)
    A("ones1", [1, 128], BF16)
    A("ones1f", [1, 128], F32)
    A("red", [128, 64], F32)
    A("bw_s", [128, DT * H], BF16)
    A("qcw_s", [128, DT * 4], F32)
    A("kcw_s", [128, DT * 4], F32)
    A("vcw_s", [128, DT * 4], F32)
    A("fir_s", [128, DT * NTAP], F32)
    A("b1_s", [128, DT], F32)
    A("w2_s", [128, DT * 24], BF16)
    A("b2_b", [128, 24], F32)
    A("it_b", [128, 24], F32)
    A("onw_s", [128, 2], F32)
    A("fb_b", [128, D], BF16)
    A("glt_r", [1, 32], F32)
    A("w1m_s", [24, D], BF16)
    A("w1s_s", [24, D], BF16)
    A("w1a_s", [24, D], BF16)
    A("w1l_s", [24, D], BF16)
    A("stm", [24, TB], BF16)
    A("sts", [24, TB], BF16)
    A("sta", [24, TB], BF16)
    A("stl", [24, TB], BF16)
    A("stf", [24, TB], F32)
    A("s_sb", [128, H * 512], BF16)
    A("wgt", [128, 24], BF16)
    A("wgf", [128, 24], F32)
    A("wTt", [24, TB], BF16)
    A("bcv", [128, 2 * 512], BF16)
    A("sq1", [128, 256], BF16)
    A("sq2", [128, 256], BF16)
    A("beta", [4, TB], F32)
    A("qfac", [4, TB], F32)
    A("kfac", [4, TB], F32)
    A("rowbf", [4, TB], BF16)
    A("bb", [128, CH], BF16)
    A("gtmp", [128, 2 * TB], BF16)
    A("tls", [128, 3 * DT * 3], BF16)

    MSK_A, MSK_AT, MSK_U = 0, 128, 256
    IND_M, IND_2, IND_H = 384, 384 + 47, 384 + 47 + 95
    SEL = 384 + 47 + 95 + 7
    SEL24 = SEL + 512

    def msk(off, w):
        return sb["msk"][:, off:off + w]

    BASE = 54 * KB
    used = nc.bytes_allocated(sb["cst"].space)
    assert used <= BASE, used

    def AT(name, shape, dtype, off):
        sb[name] = nc.alloc_sbuf_tensor_at(name, shape, dtype, offset=off)
        return sb[name]

    OW = BASE
    AT("qw_s", [128, DT * D], BF16, OW)
    AT("kw_s", [128, DT * D], BF16, OW + 16 * KB)
    AT("vw_s", [128, DT * D], BF16, OW + 32 * KB)
    AT("ow_s", [128, DT * D], BF16, OW + 48 * KB)
    AT("w1h_s", [128, DT * D], BF16, OW + 64 * KB)
    AT("M_s", [128, DT * D], BF16, OW + 80 * KB)
    AT("M_l", [128, DT * D], BF16, OW + 96 * KB)
    OX = OW + 112 * KB
    AT("Mstg", [128, DT * D // 2], F32, OX)
    AT("xn1T", [128, DT * TB], BF16, OX)
    AT("raw", [128, DT * (TB + 4)], BF16, OX + 4 * KB)
    AT("hgT", [128, DT * TB], BF16, OX + 4 * KB)
    AT("qn", [128, DT * TB], BF16, OX + 9 * KB)
    AT("fsd", [128, DT * TB], BF16, OX + 9 * KB)
    AT("kn", [128, DT * TB], BF16, OX + 13 * KB)
    AT("vt", [128, DT * (TB + 32)], BF16, OX + 17 * KB)
    AT("brs", [128, 5 * DT * TB], BF16, OX + 22 * KB)
    AT("dmx", [128, 24 * 128], BF16, OX + 42 * KB)
    AT("x1t", [128, D], F32, OX + 48 * KB)
    AT("outs", [128, D], F32, OX + 52 * KB)
    AT("x2t", [128, D], F32, OX + 52 * KB)
    assert OX + 56 * KB <= 223 * KB, OX

    BR_F1, BR_F3, BR_F7, BR_F31, BR_DL = range(5)
    BR_SRC = [BR_F1, BR_F3, BR_F7, BR_F31, BR_DL, None]

    def brs_ap(br, a, c0, w):
        base = (br * DT + a) * TB
        return sb["brs"][:, base + c0: base + c0 + w]

    def vt_ap(a, c0, w):
        return sb["vt"][:, a * (TB + 32) + 32 + c0: a * (TB + 32) + 32 + c0 + w]

    DM = {}
    _dm = [0]

    def dm_alloc(name, n=1):
        DM[name] = _dm[0]
        _dm[0] += n

    for nm, n in [("ktok", 2), ("kbtok", 2), ("vbtok", 2), ("kbT", 2),
                  ("vbT", 2), ("A", 1), ("ATr", 1), ("X", 1), ("XT", 1),
                  ("P", 1), ("P2", 1), ("Tt", 1), ("wn", 2), ("uh", 2),
                  ("at", 1)]:
        dm_alloc(nm, n)
    assert _dm[0] <= 24, _dm

    def dm(name, i=0, w=1):
        c = (DM[name] + i) * 128
        return sb["dmx"][:, c:c + w * 128]

    pSh = [nc.alloc_psum_tensor(f"pS{h}", [128, 512], F32) for h in range(H)]
    p512a = nc.alloc_psum_tensor("p512a", [128, 512], F32)
    p512b = nc.alloc_psum_tensor("p512b", [128, 512], F32)
    pT = nc.alloc_psum_tensor("pT", [128, 512], BF16)
    pU = nc.alloc_psum_tensor("pU", [128, 512], F32)

    stack = contextlib.ExitStack()
    ctx = Ctx(nc)
    for s_ in ["dg", "ds", "pe", "ac", "ve", "gp"]:
        ctx.sem(s_, stack)
    blk = stack.enter_context(nc.Block())
    state = {"psum_free": {}}

    def pfree(p):
        return state["psum_free"].get(p.name, ("pe", 0))

    def set_pfree(p, tok):
        state["psum_free"][p.name] = tok

    def cstc(i):
        return sb["cst"][:, i:i + 1]

    @blk.gpsimd
    def _(g):
        for nm, dst in [("qw", "qw_s"), ("kw", "kw_s"), ("vw", "vw_s"),
                        ("ow", "ow_s"), ("w1h", "w1h_s"),
                        ("Mfull", "M_s")]:
            ctx.inc(g.dma_start(
                out=sb[dst][:].rearrange("p (a o) -> p a o", a=DT),
                in_=ap[nm].rearrange("(a p) o -> p a o", p=128)), "dg", 16)
        ctx.inc(g.dma_start(
            out=sb["bw_s"][:].rearrange("p (a o) -> p a o", a=DT),
            in_=ap["bw"].rearrange("(a p) o -> p a o", p=128)), "dg", 16)
        ctx.inc(g.dma_start(
            out=sb["w2_s"][:].rearrange("p (a o) -> p a o", a=DT),
            in_=ap["w2"].rearrange("(a p) o -> p a o", p=128)), "dg", 16)
        for nm in ["w1m", "w1s", "w1a", "w1l"]:
            ctx.inc(g.dma_start(out=sb[nm + "_s"][:], in_=ap[nm]), "dg", 16)
        ctx.inc(g.dma_start(out=sb["idn"][:], in_=ap["ident"]), "dg", 16)
        ctx.inc(g.dma_start(out=sb["msk"][:], in_=ap["masksb"]), "dg", 16)
    t_wload = ctx.tok("dg")

    @blk.sync
    def _(s):
        ctx.inc(s.dma_start(out=sb["cst"][:], in_=ap["consts"]), "ds", 16)
        for nm in ["qcw", "kcw", "vcw"]:
            ctx.inc(s.dma_start(
                out=sb[nm + "_s"][:].rearrange("p (a o) -> p a o", a=DT),
                in_=ap[nm].rearrange("(a p) o -> p a o", p=128)), "ds", 16)
        ctx.inc(s.dma_start(
            out=sb["fir_s"][:].rearrange("p (a o) -> p a o", a=DT),
            in_=ap["firw"].rearrange("(a p) o -> p a o", p=128)), "ds", 16)
        ctx.inc(s.dma_start(out=sb["b1_s"][:], in_=ap["b1"]), "ds", 16)
        ctx.inc(s.dma_start(out=sb["onw_s"][:], in_=ap["onw"]), "ds", 16)
        ctx.inc(s.dma_start(out=sb["red"][0:1, 0:24], in_=ap["b2"]), "ds", 16)
        ctx.inc(s.dma_start(out=sb["glt_r"][0:1, 0:H], in_=ap["glt"]),
                "ds", 16)
    t_cload = ctx.tok("ds")

    @blk.gpsimd
    def _(g):
        ctx.inc(g.dma_start(out=sb["fb_b"][0:1, :], in_=ap["fbias"]),
                "dg", 16)
    t_fbrow = ctx.tok("dg")

    @blk.vector
    def _(v):
        ctx.wait(v, "vector", t_cload, t_wload)
        v.memset(sb["ones1"][:], 1.0)
        ctx.inc(v.memset(sb["ones1f"][:], 1.0), "ve")
    t_ones = ctx.tok("ve")

    @blk.vector
    def _(v):
        ctx.wait(v, "vector", t_cload)
        g = sb["glt_r"]
        v.scalar_tensor_tensor(g[0:1, 4:8], g[0:1, 0:H], -1.0, g[0:1, 0:H],
                               ALU.mult, ALU.max)
        ctx.inc(v.tensor_scalar_max(g[0:1, 8:12], g[0:1, 0:H], 0.0), "ve")
    t_g1 = ctx.tok("ve")

    @blk.scalar
    def _(sc):
        ctx.wait(sc, "scalar", t_g1)
        g = sb["glt_r"]
        sc.activation(g[0:1, 12:16], g[0:1, 4:8], AF.Exp,
                      scale=sb["cst"][0:1, LC_NEG1:LC_NEG1 + 1])
        sc.drain()
        ctx.inc(sc.activation(g[0:1, 16:20], g[0:1, 12:16], AF.Ln,
                              bias=sb["cst"][0:1, LC_ONE:LC_ONE + 1]), "ac")
    t_g2 = ctx.tok("ac")

    @blk.vector
    def _(v):
        ctx.wait(v, "vector", t_g2)
        g = sb["glt_r"]
        v.tensor_add(g[0:1, 20:24], g[0:1, 8:12], g[0:1, 16:20])
        v.drain()
        v.tensor_scalar_add(g[0:1, 20:24], g[0:1, 20:24], 0.5)
        v.drain()
        v.reciprocal(g[0:1, 24:28], g[0:1, 20:24])
        v.drain()
        ctx.inc(v.tensor_copy(
            g[0:1, 0:24].rearrange("q (a b) -> q a b", a=4),
            g[0:1, 24:28].unsqueeze(2).broadcast_to([1, 4, 6])),
            "ve")
    t_it = ctx.tok("ve")

    def emit_bcast_row_f32(row_ap, dst_ap, n, deps):
        @blk.tensor
        def _(t):
            ctx.wait(t, "tensor", t_ones, pfree(p512b), *deps)
            ctx.inc(t.matmul(p512b[:, 0:n], sb["ones1f"][:], row_ap,
                             start=True, stop=True), "pe")
        tk = ctx.tok("pe")

        @blk.scalar
        def _(sc):
            ctx.wait(sc, "scalar", tk)
            ctx.inc(sc.copy(dst_ap, p512b[:, 0:n]), "ac")
        set_pfree(p512b, ctx.tok("ac"))
        return ctx.tok("ac")

    emit_bcast_row_f32(sb["red"][0:1, 0:24], sb["b2_b"][:], 24, [t_cload])
    emit_bcast_row_f32(sb["glt_r"][0:1, 0:24], sb["it_b"][:], 24, [t_it])
    def emit_bcast_row_bf(row_ap, dst_ap, n, deps):
        @blk.tensor
        def _(t):
            ctx.wait(t, "tensor", t_ones, pfree(p512b), *deps)
            ctx.inc(t.matmul(p512b[:, 0:n], sb["ones1"][:], row_ap,
                             start=True, stop=True), "pe")
        tk = ctx.tok("pe")

        @blk.scalar
        def _(sc):
            ctx.wait(sc, "scalar", tk)
            ctx.inc(sc.copy(dst_ap, p512b[:, 0:n]), "ac")
        set_pfree(p512b, ctx.tok("ac"))
        return ctx.tok("ac")

    emit_bcast_row_bf(sb["fb_b"][0:1, 0:512], sb["fb_b"][:, 0:512], 512,
                      [t_fbrow])
    t_pre = emit_bcast_row_bf(sb["fb_b"][0:1, 512:1024],
                              sb["fb_b"][:, 512:1024], 512, [t_fbrow])

    t_wload2 = t_wload
    t_ml_prev = ("ve", 0)
    for half in range(2):
        @blk.sync
        def _(s, half=half, t_ml_prev=t_ml_prev):
            ctx.wait(s, "sync", t_ml_prev)
            ctx.inc(s.dma_start(
                out=sb["Mstg"][:].rearrange("p (a o) -> p a o", a=DT // 2),
                in_=ap["Mfull"].rearrange("(a p) o -> p a o", p=128)
                [:, half * 4:half * 4 + 4, :]), "ds", 16)
        t_mst = ctx.tok("ds")

        @blk.vector
        def _(v, half=half, t_mst=t_mst):
            ctx.wait(v, "vector", t_mst, t_wload)
            for a4 in range(DT // 2):
                a = half * 4 + a4
                last = v.tensor_sub(sb["M_l"][:, a * D:(a + 1) * D],
                                    sb["Mstg"][:, a4 * D:(a4 + 1) * D],
                                    sb["M_s"][:, a * D:(a + 1) * D])
            ctx.inc(last, "ve")
        t_ml_prev = ctx.tok("ve")
    t_M = t_ml_prev
    if dbg:
        @blk.sync
        def _(s):
            ctx.wait(s, "sync", t_M)
            ctx.inc(s.dma_start(out=ap["dbg_M"], in_=sb["M_s"][:]), "ds", 16)

    def emit_head_bcast(src_rows, h, c0, w, dst_bf, deps, copy_rows=True):
        if copy_rows:
            @blk.vector
            def _(v):
                ctx.wait(v, "vector", *deps,
                         state.get("t_rowbf_used", ("pe", 0)))
                v.drain()
                ctx.inc(v.tensor_copy(sb["rowbf"][:], src_rows), "ve")
            state["t_rowbf"] = ctx.tok("ve")
        tk0 = state["t_rowbf"]

        @blk.tensor
        def _(t):
            ctx.wait(t, "tensor", tk0, pfree(p512b), t_wload, *deps)
            ctx.inc(t.matmul(p512b[:, 0:w],
                             msk(SEL + h * 128, 128)[0:4, :],
                             sb["rowbf"][0:4, c0:c0 + w],
                             start=True, stop=True), "pe")
        tk = ctx.tok("pe")
        state["t_rowbf_used"] = tk

        @blk.scalar
        def _(sc):
            ctx.wait(sc, "scalar", tk)
            ctx.inc(sc.copy(dst_bf, p512b[:, 0:w]), "ac")
        set_pfree(p512b, ctx.tok("ac"))
        return ctx.tok("ac")

    for b in range(2):
        x_in = ap["ya"] if b == 0 else ap["yb"]
        out_t = ap["outa"] if b == 0 else ap["outb"]
        barrier = ctx.all_toks()

        @blk.vector
        def _(v):
            ctx.wait(v, "vector", *barrier)
            v.memset(sb["tls"][:], 0.0)
            ctx.inc(v.memset(sb["vt"][:].rearrange(
                "p (a o) -> p a o", a=DT)[:, :, 0:32], 0.0), "ve")
        t_tails = ctx.tok("ve")
        state["t_beta_used"] = t_tails

        for ib in range(NBLK):
            t0 = ib * TB

            for j in range(TB // 128):
                tt = t0 // 128 + j

                @blk.sync
                def _(s, tt=tt):
                    ctx.wait(s, "sync", state.get("t_x1_used", ("ve", 0)),
                             t_M)
                    ctx.inc(s.dma_start(out=sb["x1t"][:],
                                        in_=x_in[ts(tt), :]), "ds", 16)
                    if b == 0:
                        ctx.inc(s.dma_start(out=sb["x2t"][:],
                                            in_=ap["moe_a"][ts(tt), :]),
                                "ds", 16)
                t_x = ctx.tok("ds")

                @blk.vector
                def _(v, t_x=t_x):
                    ctx.wait(v, "vector", t_x, t_tails)
                    if b == 0:
                        v.tensor_add(sb["x1t"][:], sb["x1t"][:], sb["x2t"][:])
                        v.drain()
                    v.tensor_scalar(sb["x2t"][:], sb["x1t"][:], 1.0, 0.0,
                                    ALU.mult, ALU.add,
                                    accum_out=sb["red"][:, 8:9])
                    ctx.inc(v.drain(), "ve")
                t_s1 = ctx.tok("ve")

                @blk.scalar
                def _(sc, t_s1=t_s1):
                    ctx.wait(sc, "scalar", t_s1, t_cload)
                    sc.activation(sb["x2t"][:], sb["x1t"][:], AF.Square,
                                  accum_out=sb["red"][:, 0:1])
                    sc.drain()
                    ctx.inc(sc.copy(sb["red"][:, 4:5], sb["red"][:, 0:1]),
                            "ac")
                t_s2 = ctx.tok("ac")

                @blk.vector
                def _(v, t_s2=t_s2):
                    ctx.wait(v, "vector", t_s2)
                    v.tensor_scalar(sb["red"][:, 16:17], sb["red"][:, 8:9],
                                    1.0 / D, None, ALU.mult)
                    v.drain()
                    v.tensor_tensor(sb["red"][:, 24:25], sb["red"][:, 16:17],
                                    sb["red"][:, 16:17], ALU.mult)
                    v.tensor_scalar(sb["red"][:, 4:5], sb["red"][:, 4:5],
                                    1.0 / D, 1e-5, ALU.mult, ALU.add)
                    v.drain()
                    ctx.inc(v.tensor_sub(sb["red"][:, 24:25],
                                         sb["red"][:, 4:5],
                                         sb["red"][:, 24:25]), "ve")
                t_var = ctx.tok("ve")

                @blk.scalar
                def _(sc, t_var=t_var):
                    ctx.wait(sc, "scalar", t_var)
                    ctx.inc(sc.activation(sb["red"][:, 32:33],
                                          sb["red"][:, 24:25], AF.Sqrt), "ac")
                t_sd = ctx.tok("ac")

                @blk.vector
                def _(v, t_sd=t_sd):
                    ctx.wait(v, "vector", t_sd,
                             state.get("t_bcv_used", ("pe", 0)))
                    v.reciprocal(sb["red"][:, 33:34], sb["red"][:, 32:33])
                    v.drain()
                    v.tensor_scalar(sb["bcv"][:], sb["x1t"][:],
                                    sb["red"][:, 16:17],
                                    sb["red"][:, 33:34],
                                    ALU.subtract, ALU.mult)
                    ctx.inc(v.drain(), "ve")
                t_xn = ctx.tok("ve")
                state["t_x1_used"] = t_xn

                for half in range(2):
                    @blk.tensor
                    def _(t, j=j, half=half, t_xn=t_xn):
                        ctx.wait(t, "tensor", t_xn, pfree(pT), t_wload,
                                 state.get("t_xnT_used", ("ac", 0)))
                        for q in range(4):
                            a = half * 4 + q
                            last = t.transpose(pT[:, ts(q)],
                                               sb["bcv"][:, ts(a)],
                                               sb["idn"][:])
                        ctx.inc(last, "pe")
                    t_tr = ctx.tok("pe")
                    state["t_bcv_used"] = t_tr

                    @blk.scalar
                    def _(sc, j=j, half=half, t_tr=t_tr):
                        ctx.wait(sc, "scalar", t_tr)
                        for q in range(4):
                            a = half * 4 + q
                            last = sc.copy(
                                sb["xn1T"][:, a * TB + j * 128:
                                           a * TB + j * 128 + 128],
                                pT[:, ts(q)])
                        ctx.inc(last, "ac")
                    set_pfree(pT, ctx.tok("ac"))
            t_xn1T = ctx.tok("ac")

            def emit_proj_conv(w_s, cw_s, dst, dst_off, dstW, deps, tix=0):
                for mi in range(DT):
                    p = p512a if mi % 2 == 0 else p512b

                    @blk.tensor
                    def _(t, mi=mi, p=p):
                        ctx.wait(t, "tensor", t_wload2, pfree(p), *deps)
                        for k in range(DT):
                            last = t.matmul(
                                p[:, 0:TB],
                                sb[w_s][:, k * D + mi * 128:
                                        k * D + mi * 128 + 128],
                                sb["xn1T"][:, k * TB:(k + 1) * TB],
                                start=(k == 0), stop=(k == DT - 1))
                        ctx.inc(last, "pe")
                    tk = ctx.tok("pe")

                    @blk.scalar
                    def _(sc, mi=mi, p=p, tk=tk):
                        ctx.wait(sc, "scalar", tk,
                                 state.get("t_raw_used", ("ve", 0)))
                        ctx.inc(sc.copy(
                            sb["raw"][:, mi * (TB + 4) + 3:
                                      mi * (TB + 4) + 3 + TB], p[:, 0:TB]),
                            "ac")
                    set_pfree(p, ctx.tok("ac"))
                t_raw = ctx.tok("ac")

                @blk.vector
                def _(v, t_raw=t_raw, tix=tix):
                    ctx.wait(v, "vector", t_raw, t_tails,
                             state.get("t_sil_used", ("ac", 0)))
                    for a in range(DT):
                        rb = a * (TB + 4)
                        v.tensor_copy(sb["raw"][:, rb:rb + 3],
                                      sb["tls"][:, (tix * DT + a) * 3:
                                                (tix * DT + a) * 3 + 3])
                    v.drain()
                    for tap in range(4):
                        for a in range(DT):
                            rb = a * (TB + 4)
                            src = sb["raw"][:, rb + tap: rb + tap + TB]
                            wcol = sb[cw_s][:, a * 4 + tap: a * 4 + tap + 1]
                            if tap == 0:
                                v.tensor_scalar(
                                    dst[:, a * dstW + dst_off:
                                        a * dstW + dst_off + TB],
                                    src, wcol, None, ALU.mult)
                            else:
                                v.scalar_tensor_tensor(
                                    dst[:, a * dstW + dst_off:
                                        a * dstW + dst_off + TB],
                                    src, wcol,
                                    dst[:, a * dstW + dst_off:
                                        a * dstW + dst_off + TB],
                                    ALU.mult, ALU.add)
                    v.drain()
                    for a in range(DT):
                        rb = a * (TB + 4)
                        v.tensor_copy(sb["tls"][:, (tix * DT + a) * 3:
                                                (tix * DT + a) * 3 + 3],
                                      sb["raw"][:, rb + TB:rb + TB + 3])
                    ctx.inc(v.drain(), "ve")
                t_conv = ctx.tok("ve")
                state["t_raw_used"] = t_conv

                @blk.scalar
                def _(sc, t_conv=t_conv):
                    ctx.wait(sc, "scalar", t_conv)
                    for a in range(DT):
                        last = sc.activation(
                            dst[:, a * dstW + dst_off:
                                a * dstW + dst_off + TB],
                            dst[:, a * dstW + dst_off:
                                a * dstW + dst_off + TB], AF.Silu)
                    ctx.inc(last, "ac")
                state["t_sil_used"] = ctx.tok("ac")
                return ctx.tok("ac")

            t_q = emit_proj_conv("qw_s", "qcw_s", sb["qn"][:], 0, TB,
                                 [t_xn1T, state.get("t_qn_used", ("pe", 0))],
                                 tix=0)
            t_k = emit_proj_conv("kw_s", "kcw_s", sb["kn"][:], 0, TB,
                                 [t_xn1T, state.get("t_kn_used", ("pe", 0))],
                                 tix=1)
            t_v = emit_proj_conv("vw_s", "vcw_s", sb["vt"][:], 32, TB + 32,
                                 [t_xn1T, state.get("t_vt_used", ("pe", 0)),
                                  state.get("t_vtl", ("ve", 0))], tix=2)

            @blk.tensor
            def _(t):
                ctx.wait(t, "tensor", t_xn1T, pfree(pU), t_wload,
                         state.get("t_beta_used", ("ve", 0)))
                for k in range(DT):
                    last = t.matmul(pU[0:H, 0:TB],
                                    sb["bw_s"][:, ts(k, H)],
                                    sb["xn1T"][:, k * TB:(k + 1) * TB],
                                    start=(k == 0), stop=(k == DT - 1))
                ctx.inc(last, "pe")
            t_bl = ctx.tok("pe")

            @blk.scalar
            def _(sc, t_bl=t_bl):
                ctx.wait(sc, "scalar", t_bl)
                ctx.inc(sc.activation(sb["beta"][:], pU[0:H, 0:TB],
                                      AF.Sigmoid), "ac")
            set_pfree(pU, ctx.tok("ac"))
            t_beta = ctx.tok("ac")

            def emit_l2fac(src, dst_rows, tok_in):
                for a in range(DT):
                    sqb = sb["sq1"] if a % 2 == 0 else sb["sq2"]

                    @blk.scalar
                    def _(sc, a=a, sqb=sqb):
                        ctx.wait(sc, "scalar", tok_in,
                                 state.get(f"t_sqf{a % 2}", ("pe", 0)))
                        ctx.inc(sc.activation(
                            sqb[:], src[:, a * TB:(a + 1) * TB], AF.Square),
                            "ac")
                    t_sq = ctx.tok("ac")

                    @blk.tensor
                    def _(t, a=a, sqb=sqb, t_sq=t_sq):
                        ctx.wait(t, "tensor", t_sq, t_wload,
                                 pfree(p512a) if a == 0 else ("pe", 0))
                        h = a // 2
                        ctx.inc(t.matmul(
                            p512a[0:H, 0:TB],
                            msk(IND_H + 3 - h, H), sqb[:],
                            start=(a == 0), stop=(a == DT - 1)), "pe")
                    state[f"t_sqf{a % 2}"] = ctx.tok("pe")
                t_sum = ctx.tok("pe")

                @blk.scalar
                def _(sc, t_sum=t_sum):
                    ctx.wait(sc, "scalar", t_sum)
                    ctx.inc(sc.activation(dst_rows, p512a[0:H, 0:TB],
                                          AF.Sqrt,
                                          bias=sb["cst"][0:H, LC_EPS6:
                                                         LC_EPS6 + 1]), "ac")
                set_pfree(p512a, ctx.tok("ac"))
                t_r = ctx.tok("ac")

                @blk.vector
                def _(v, t_r=t_r):
                    ctx.wait(v, "vector", t_r)
                    v.reciprocal(dst_rows, dst_rows)
                    ctx.inc(v.drain(), "ve")
                return ctx.tok("ve")

            t_qf = emit_l2fac(sb["qn"][:], sb["qfac"][:], t_q)
            t_kf = emit_l2fac(sb["kn"][:], sb["kfac"][:], t_k)

            def emit_apply_rows(dstbuf, rows, tok_rows, tok_data):
                last = ("ve", 0)
                for h in range(H):
                    t_b = emit_head_bcast(rows, h, 0, TB,
                                          sb["bcv"][:, 0:TB],
                                          [tok_rows, tok_data,
                                           state.get("t_bcv_used", ("pe", 0))])
                    state["t_bcv_used"] = ("pe", 0)

                    @blk.vector
                    def _(v, h=h, t_b=t_b):
                        ctx.wait(v, "vector", t_b)
                        for a in (2 * h, 2 * h + 1):
                            v.tensor_tensor(
                                dstbuf[:, a * TB:(a + 1) * TB],
                                dstbuf[:, a * TB:(a + 1) * TB],
                                sb["bcv"][:, 0:TB], ALU.mult)
                            v.drain()
                        ctx.inc(v.drain(), "ve")
                    state["t_bcv_used"] = ctx.tok("ve")
                    last = ctx.tok("ve")
                return last

            t_qn = emit_apply_rows(sb["qn"][:], sb["qfac"][:], t_qf, t_q)
            t_kn = emit_apply_rows(sb["kn"][:], sb["kfac"][:], t_kf, t_k)

            @blk.vector
            def _(v, t_v=t_v):
                ctx.wait(v, "vector", t_v,
                         state.get("t_brs_used", ("pe", 0)))
                col = 0
                for fi, kw_ in enumerate(FIRS[:3]):
                    brj = [BR_F1, BR_F3, BR_F7][fi]
                    for tap in range(kw_):
                        for a in range(DT):
                            src = sb["vt"][:, a * (TB + 32) + 32 - (kw_ - 1)
                                           + tap: a * (TB + 32) + 32 -
                                           (kw_ - 1) + tap + TB]
                            wcol = sb["fir_s"][:, a * NTAP + col + tap:
                                               a * NTAP + col + tap + 1]
                            dstap = brs_ap(brj, a, 0, TB)
                            if tap == 0:
                                v.tensor_scalar(dstap, src, wcol, None,
                                                ALU.mult)
                            else:
                                v.scalar_tensor_tensor(dstap, src, wcol,
                                                       dstap, ALU.mult,
                                                       ALU.add)
                    col += kw_
                ctx.inc(v.drain(), "ve")
            t_fir_a = ctx.tok("ve")

            @blk.gpsimd
            def _(g, t_v=t_v):
                ctx.wait(g, "gpsimd", t_v,
                         state.get("t_brs_used", ("pe", 0)))
                col = 11
                for tap in range(31):
                    for a in range(DT):
                        src = sb["vt"][:, a * (TB + 32) + 2 + tap:
                                       a * (TB + 32) + 2 + tap + TB]
                        wb = sb["fir_s"][:, a * NTAP + col + tap:
                                         a * NTAP + col + tap + 1] \
                            .broadcast_to([128, TB])
                        dstap = brs_ap(BR_F31, a, 0, TB)
                        if tap == 0:
                            g.tensor_tensor(dstap, src, wb, ALU.mult)
                        else:
                            tmp = sb["gtmp"][:, (a % 2) * TB:
                                             (a % 2) * TB + TB]
                            g.tensor_tensor(tmp, src, wb, ALU.mult)
                            g.tensor_tensor(dstap, dstap, tmp, ALU.add)
                ctx.inc(g.drain(), "gp")
            t_fir_b = ctx.tok("gp")

            for cc in range(TB // CH):
                gchunk = ib * (TB // CH) + cc
                first = gchunk == 0
                ccol = cc * CH

                for h in range(H):
                    a0, a1 = 2 * h, 2 * h + 1
                    ktok = dm("ktok", 0, 2)
                    kbtok = dm("kbtok", 0, 2)
                    vbtok = dm("vbtok", 0, 2)
                    ssb = sb["s_sb"][:, h * 512:(h + 1) * 512]

                    def kns(a):
                        return sb["kn"][:, a * TB + ccol: a * TB + ccol + CH]

                    def qns(a):
                        return sb["qn"][:, a * TB + ccol: a * TB + ccol + CH]

                    t_bb = emit_head_bcast(
                        sb["beta"][:], h, ccol, CH, sb["bb"][:],
                        [t_beta, state.get("t_bb_used", ("ve", 0))])

                    @blk.vector
                    def _(v, h=h, a0=a0, a1=a1, t_bb=t_bb, t_kn=t_kn,
                          t_v=t_v):
                        ctx.wait(v, "vector", t_bb, t_kn, t_v,
                                 state.get("t_dm_used", ("pe", 0)))
                        for i, a in enumerate((a0, a1)):
                            v.tensor_tensor(dm("kbT", i), kns(a),
                                            sb["bb"][:], ALU.mult)
                            v.tensor_tensor(dm("vbT", i),
                                            vt_ap(a, ccol, CH),
                                            sb["bb"][:], ALU.mult)
                        ctx.inc(v.drain(), "ve")
                    t_kbv = ctx.tok("ve")
                    state["t_bb_used"] = t_kbv

                    @blk.tensor
                    def _(t, a0=a0, a1=a1, t_kbv=t_kbv, t_qn=t_qn):
                        ctx.wait(t, "tensor", t_kbv, t_qn, pfree(p512a))
                        t.matmul(p512a[:, 0:128], dm("kbT", 0), kns(a0),
                                 start=True, stop=False)
                        t.matmul(p512a[:, 0:128], dm("kbT", 1), kns(a1),
                                 start=False, stop=True)
                        t.matmul(p512a[:, 128:256], kns(a0), dm("kbT", 0),
                                 start=True, stop=False)
                        ctx.inc(t.matmul(p512a[:, 128:256], kns(a1),
                                         dm("kbT", 1), start=False,
                                         stop=True), "pe")
                    t_Araw = ctx.tok("pe")

                    @blk.tensor
                    def _(t, a0=a0, a1=a1, t_kbv=t_kbv):
                        ctx.wait(t, "tensor", t_kbv, pfree(pT))
                        t.transpose(pT[:, 0:128], kns(a0), sb["idn"][:])
                        t.transpose(pT[:, 128:256], kns(a1), sb["idn"][:])
                        t.transpose(pT[:, 256:384], dm("kbT", 0),
                                    sb["idn"][:])
                        ctx.inc(t.transpose(pT[:, 384:512], dm("kbT", 1),
                                            sb["idn"][:]), "pe")
                    t_tr1 = ctx.tok("pe")

                    @blk.scalar
                    def _(sc, t_tr1=t_tr1):
                        ctx.wait(sc, "scalar", t_tr1)
                        sc.copy(ktok, pT[:, 0:256])
                        ctx.inc(sc.copy(kbtok, pT[:, 256:512]), "ac")
                    set_pfree(pT, ctx.tok("ac"))

                    @blk.tensor
                    def _(t, t_kbv=t_kbv):
                        ctx.wait(t, "tensor", pfree(pT))
                        t.transpose(pT[:, 0:128], dm("vbT", 0), sb["idn"][:])
                        ctx.inc(t.transpose(pT[:, 128:256], dm("vbT", 1),
                                            sb["idn"][:]), "pe")
                    t_tr2 = ctx.tok("pe")
                    state["t_dm_used"] = t_tr2

                    @blk.scalar
                    def _(sc, t_tr2=t_tr2):
                        ctx.wait(sc, "scalar", t_tr2)
                        ctx.inc(sc.copy(vbtok, pT[:, 0:256]), "ac")
                    set_pfree(pT, ctx.tok("ac"))
                    t_toks = ctx.tok("ac")

                    @blk.vector
                    def _(v, t_Araw=t_Araw):
                        ctx.wait(v, "vector", t_Araw)
                        v.tensor_tensor(dm("A"), p512a[:, 0:128],
                                        msk(MSK_A, 128), ALU.mult)
                        v.tensor_tensor(dm("ATr"), p512a[:, 128:256],
                                        msk(MSK_AT, 128), ALU.mult)
                        v.drain()
                        ctx.inc(v.tensor_tensor(dm("P"), dm("A"),
                                                sb["idn"][:], ALU.add), "ve")
                    set_pfree(p512a, ctx.tok("ve"))
                    t_P0 = ctx.tok("ve")

                    t_lvl = t_P0
                    for lvl in range(6):
                        @blk.tensor
                        def _(t, t_lvl=t_lvl):
                            ctx.wait(t, "tensor", t_lvl, pfree(p512b))
                            t.matmul(p512b[:, 0:128], dm("ATr"), dm("A"),
                                     start=True, stop=True)
                            ctx.inc(t.matmul(p512b[:, 128:256], dm("A"),
                                             dm("ATr"), start=True,
                                             stop=True), "pe")
                        t_mm = ctx.tok("pe")

                        @blk.vector
                        def _(v, t_mm=t_mm):
                            ctx.wait(v, "vector", t_mm)
                            v.tensor_copy(dm("X"), p512b[:, 0:128])
                            ctx.inc(v.tensor_copy(dm("XT"),
                                                  p512b[:, 128:256]), "ve")
                        t_x = ctx.tok("ve")

                        @blk.tensor
                        def _(t, t_x=t_x):
                            ctx.wait(t, "tensor", t_x)
                            ctx.inc(t.matmul(p512b[:, 256:384], dm("XT"),
                                             dm("P"), start=True,
                                             stop=True), "pe")
                        t_pm = ctx.tok("pe")

                        @blk.vector
                        def _(v, t_pm=t_pm):
                            ctx.wait(v, "vector", t_pm)
                            ctx.inc(v.tensor_tensor(
                                dm("P2"), p512b[:, 256:384], dm("P"),
                                ALU.add), "ve")
                        set_pfree(p512b, ctx.tok("ve"))
                        t_lvl = ctx.tok("ve")
                        DM["A"], DM["X"] = DM["X"], DM["A"]
                        DM["ATr"], DM["XT"] = DM["XT"], DM["ATr"]
                        DM["P"], DM["P2"] = DM["P2"], DM["P"]
                    t_chain = t_lvl

                    @blk.tensor
                    def _(t, t_chain=t_chain):
                        ctx.wait(t, "tensor", t_chain, pfree(pT))
                        ctx.inc(t.transpose(pT[:, 0:128], dm("P"),
                                            sb["idn"][:]), "pe")
                    t_ttr = ctx.tok("pe")

                    @blk.scalar
                    def _(sc, t_ttr=t_ttr):
                        ctx.wait(sc, "scalar", t_ttr)
                        ctx.inc(sc.copy(dm("Tt"), pT[:, 0:128]), "ac")
                    set_pfree(pT, ctx.tok("ac"))
                    t_Tt = ctx.tok("ac")

                    if not first:
                        @blk.tensor
                        def _(t, t_Tt=t_Tt, t_toks=t_toks):
                            ctx.wait(t, "tensor", t_Tt, t_toks,
                                     pfree(p512b))
                            t.matmul(p512b[:, 0:128], kbtok[:, 0:128],
                                     dm("Tt"), start=True, stop=True)
                            ctx.inc(t.matmul(p512b[:, 128:256],
                                             kbtok[:, 128:256], dm("Tt"),
                                             start=True, stop=True), "pe")
                        t_w = ctx.tok("pe")

                        @blk.scalar
                        def _(sc, t_w=t_w):
                            ctx.wait(sc, "scalar", t_w)
                            sc.activation(dm("wn", 0), p512b[:, 0:128],
                                          AF.Copy, scale=cstc(LC_NEG1))
                            ctx.inc(sc.activation(dm("wn", 1),
                                                  p512b[:, 128:256],
                                                  AF.Copy,
                                                  scale=cstc(LC_NEG1)), "ac")
                        set_pfree(p512b, ctx.tok("ac"))
                        t_wn = ctx.tok("ac")
                    else:
                        t_wn = t_Tt

                    @blk.tensor
                    def _(t, t_Tt=t_Tt, t_toks=t_toks, t_wn=t_wn,
                          first=first, h=h):
                        ctx.wait(t, "tensor", t_Tt, t_toks, t_wn, pfree(pU))
                        t.matmul(pU[:, 0:256], dm("Tt"), vbtok,
                                 start=True, stop=first)
                        if not first:
                            t.matmul(pU[:, 0:256], dm("wn", 0),
                                     ssb[:, 0:256], start=False, stop=False)
                            ctx.inc(t.matmul(pU[:, 0:256], dm("wn", 1),
                                             ssb[:, 256:512], start=False,
                                             stop=True), "pe")
                        else:
                            ctx.inc(t.matmul(pU[:, 256:384], sb["idn"][:],
                                             sb["idn"][:], start=True,
                                             stop=True), "pe")
                    t_uh_mm = ctx.tok("pe")

                    @blk.scalar
                    def _(sc, t_uh_mm=t_uh_mm):
                        ctx.wait(sc, "scalar", t_uh_mm)
                        ctx.inc(sc.copy(dm("uh", 0, 2), pU[:, 0:256]), "ac")
                    t_uh = ctx.tok("ac")
                    set_pfree(pU, t_uh)

                    @blk.tensor
                    def _(t, a0=a0, a1=a1, t_qn=t_qn, t_toks=t_toks):
                        ctx.wait(t, "tensor", t_qn, pfree(p512b))
                        t.matmul(p512b[:, 0:128], kns(a0), qns(a0),
                                 start=True, stop=False)
                        ctx.inc(t.matmul(p512b[:, 0:128], kns(a1), qns(a1),
                                         start=False, stop=True), "pe")
                    t_at_mm = ctx.tok("pe")

                    @blk.vector
                    def _(v, t_at_mm=t_at_mm):
                        ctx.wait(v, "vector", t_at_mm)
                        ctx.inc(v.tensor_tensor(dm("at"), p512b[:, 0:128],
                                                msk(MSK_U, 128), ALU.mult),
                                "ve")
                    set_pfree(p512b, ctx.tok("ve"))
                    t_at = ctx.tok("ve")

                    for dt2 in range(2):
                        @blk.tensor
                        def _(t, h=h, dt2=dt2, a0=a0, a1=a1, t_uh=t_uh,
                              t_at=t_at, first=first):
                            ctx.wait(t, "tensor", t_uh, t_at, pfree(pU))
                            if not first:
                                t.matmul(pU[:, 256 + dt2 * 128:
                                            384 + dt2 * 128],
                                         ssb[:, 0 * 256 + dt2 * 128:
                                             0 * 256 + dt2 * 128 + 128],
                                         qns(a0), start=True, stop=False)
                                t.matmul(pU[:, 256 + dt2 * 128:
                                            384 + dt2 * 128],
                                         ssb[:, 1 * 256 + dt2 * 128:
                                             1 * 256 + dt2 * 128 + 128],
                                         qns(a1), start=False, stop=False)
                                ctx.inc(t.matmul(
                                    pU[:, 256 + dt2 * 128:384 + dt2 * 128],
                                    dm("uh", dt2), dm("at"),
                                    start=False, stop=True), "pe")
                            else:
                                ctx.inc(t.matmul(
                                    pU[:, 256 + dt2 * 128:384 + dt2 * 128],
                                    dm("uh", dt2), dm("at"),
                                    start=True, stop=True), "pe")
                        t_o = ctx.tok("pe")

                        @blk.scalar
                        def _(sc, h=h, dt2=dt2, t_o=t_o):
                            ctx.wait(sc, "scalar", t_o)
                            ctx.inc(sc.copy(
                                brs_ap(BR_DL, 2 * h + dt2, ccol, CH),
                                pU[:, 256 + dt2 * 128:384 + dt2 * 128]),
                                "ac")
                    set_pfree(pU, ctx.tok("ac"))

                    @blk.tensor
                    def _(t, h=h, t_uh=t_uh, first=first,
                          gchunk=gchunk):
                        ctx.wait(t, "tensor", t_uh,
                                 state.get(f"t_scp{h}", ("ac", 0)))
                        for i in range(2):
                            last = t.matmul(
                                pSh[h][:, i * 256:(i + 1) * 256],
                                ktok[:, i * 128:(i + 1) * 128],
                                dm("uh", 0, 2),
                                start=(first and i == 0),
                                stop=(gchunk == NCH - 1 and i == 1))
                        ctx.inc(last, "pe")
                    t_su = ctx.tok("pe")

                    @blk.scalar
                    def _(sc, h=h, t_su=t_su):
                        ctx.wait(sc, "scalar", t_su)
                        ctx.inc(sc.copy(ssb, pSh[h][:]), "ac")
                    state[f"t_scp{h}"] = ctx.tok("ac")
                    if dbg and b == 0 and gchunk == 0 and h == 0:
                        @blk.sync
                        def _(s):
                            ctx.wait(s, "sync", ctx.tok("ac"), ctx.tok("ve"))
                            ctx.inc(s.dma_start(out=ap["dbg_dmx"],
                                                in_=sb["dmx"][:]), "ds", 16)
                            ctx.inc(s.dma_start(out=ap["dbg_bb"],
                                                in_=sb["bb"][:]), "ds", 16)
            t_delta = ctx.tok("ac")

            def branch_tile(s, a, w=TB):
                if BR_SRC[s] is None:
                    return vt_ap(a, 0, w)
                return brs_ap(BR_SRC[s], a, 0, w)

            t_br_ready = [t_fir_a, t_fir_a, t_fir_a, t_fir_b, t_delta, t_v]

            @blk.tensor
            def _(t):
                ctx.wait(t, "tensor", pfree(p512a), t_wload, *t_br_ready)
                n = 0
                for s_ in range(6):
                    for a in range(DT):
                        r = s_ * 4 + a // 2
                        last = t.matmul(p512a[0:24, 0:TB],
                                        msk(IND_M + 23 - r, 24),
                                        branch_tile(s_, a),
                                        start=(n == 0), stop=(n == 47))
                        n += 1
                ctx.inc(last, "pe")
            t_s1k = ctx.tok("pe")

            @blk.scalar
            def _(sc, t_s1k=t_s1k):
                ctx.wait(sc, "scalar", t_s1k)
                ctx.inc(sc.copy(sb["stm"][:], p512a[0:24, 0:TB]), "ac")
            t_stm = ctx.tok("ac")

            n = [0]
            for s_ in range(6):
                for a in range(DT):
                    sqb = sb["sq1"] if n[0] % 2 == 0 else sb["sq2"]

                    @blk.scalar
                    def _(sc, s_=s_, a=a, sqb=sqb):
                        ctx.wait(sc, "scalar",
                                 state.get(f"t_sqs{n[0] % 2}", ("pe", 0)),
                                 *t_br_ready)
                        ctx.inc(sc.activation(sqb[:], branch_tile(s_, a),
                                              AF.Square), "ac")
                    t_sq = ctx.tok("ac")

                    @blk.tensor
                    def _(t, s_=s_, a=a, sqb=sqb, t_sq=t_sq, n0=n[0],
                          t_stm=t_stm):
                        r = s_ * 4 + a // 2
                        ctx.wait(t, "tensor", t_sq,
                                 pfree(p512b) if n0 == 0 else ("pe", 0),
                                 t_stm if n0 == 0 else ("pe", 0))
                        ctx.inc(t.matmul(p512b[0:64, 0:TB],
                                         msk(IND_2 + 31 - r, 64), sqb[:],
                                         start=(n0 == 0), stop=(n0 == 47)),
                                "pe")
                    state[f"t_sqs{n[0] % 2}"] = ctx.tok("pe")
                    n[0] += 1
            t_s2k = ctx.tok("pe")

            @blk.scalar
            def _(sc, t_s2k=t_s2k):
                ctx.wait(sc, "scalar", t_s2k)
                ctx.inc(sc.activation(sb["stl"][:], p512b[32:56, 0:TB],
                                      AF.Sqrt), "ac")
            t_stl = ctx.tok("ac")

            @blk.vector
            def _(v, t_s2k=t_s2k, t_stm=t_stm):
                ctx.wait(v, "vector", t_s2k, t_stm)
                v.tensor_tensor(sb["stf"][:], sb["stm"][:], sb["stm"][:],
                                ALU.mult)
                v.drain()
                ctx.inc(v.scalar_tensor_tensor(
                    sb["stf"][:], sb["stf"][:], -256.0 / 255.0,
                    p512b[0:24, 0:TB], ALU.mult, ALU.add), "ve")
            set_pfree(p512b, ctx.tok("ve"))
            t_var2 = ctx.tok("ve")

            @blk.scalar
            def _(sc, t_var2=t_var2):
                ctx.wait(sc, "scalar", t_var2)
                ctx.inc(sc.activation(sb["sts"][:], sb["stf"][:], AF.Sqrt),
                        "ac")
            t_sts = ctx.tok("ac")

            n = [0]
            for s_ in range(6):
                for a in range(DT):
                    sqb = sb["sq1"] if n[0] % 2 == 0 else sb["sq2"]

                    @blk.vector
                    def _(v, s_=s_, a=a, sqb=sqb):
                        ctx.wait(v, "vector",
                                 state.get(f"t_sqa{n[0] % 2}", ("pe", 0)),
                                 *t_br_ready)
                        ctx.inc(v.scalar_tensor_tensor(
                            sqb[:], branch_tile(s_, a), -1.0,
                            branch_tile(s_, a), ALU.mult, ALU.max), "ve")
                    t_ab = ctx.tok("ve")

                    @blk.tensor
                    def _(t, s_=s_, a=a, sqb=sqb, t_ab=t_ab, n0=n[0]):
                        r = s_ * 4 + a // 2
                        ctx.wait(t, "tensor", t_ab,
                                 pfree(p512a) if n0 == 0 else ("pe", 0))
                        ctx.inc(t.matmul(p512a[0:24, 0:TB],
                                         msk(IND_M + 23 - r, 24), sqb[:],
                                         start=(n0 == 0), stop=(n0 == 47)),
                                "pe")
                    state[f"t_sqa{n[0] % 2}"] = ctx.tok("pe")
                    n[0] += 1
            t_sak = ctx.tok("pe")

            @blk.scalar
            def _(sc, t_sak=t_sak):
                ctx.wait(sc, "scalar", t_sak)
                ctx.inc(sc.copy(sb["sta"][:], p512a[0:24, 0:TB]), "ac")
            set_pfree(p512a, ctx.tok("ac"))
            t_sta = ctx.tok("ac")

            if dbg and ib == dbg_blk and b == 0:
                @blk.sync
                def _(s):
                    ctx.wait(s, "sync", t_stm, t_sts, t_sta, t_stl)
                    ctx.inc(s.dma_start(out=ap["dbg_stats"][:, 0:TB],
                                        in_=sb["stm"][:]), "ds", 16)
                    ctx.inc(s.dma_start(out=ap["dbg_stats"][:, TB:2 * TB],
                                        in_=sb["sts"][:]), "ds", 16)
                    ctx.inc(s.dma_start(out=ap["dbg_stats"][:, 2 * TB:3 * TB],
                                        in_=sb["sta"][:]), "ds", 16)
                    ctx.inc(s.dma_start(out=ap["dbg_stats"][:, 3 * TB:4 * TB],
                                        in_=sb["stl"][:]), "ds", 16)

            for gt in range(DT):
                p = p512a if gt % 2 == 0 else p512b

                @blk.tensor
                def _(t, gt=gt, p=p):
                    ctx.wait(t, "tensor", pfree(p), t_xn1T, t_stm, t_sts,
                             t_sta, t_stl, t_wload2)
                    for k in range(DT):
                        t.matmul(p[:, 0:TB],
                                 sb["w1h_s"][:, k * D + gt * 128:
                                             k * D + gt * 128 + 128],
                                 sb["xn1T"][:, k * TB:(k + 1) * TB],
                                 start=(k == 0), stop=False)
                    t.matmul(p[:, 0:TB], sb["w1m_s"][:, gt * 128:
                                                     gt * 128 + 128],
                             sb["stm"][:], start=False, stop=False)
                    t.matmul(p[:, 0:TB], sb["w1s_s"][:, gt * 128:
                                                     gt * 128 + 128],
                             sb["sts"][:], start=False, stop=False)
                    t.matmul(p[:, 0:TB], sb["w1a_s"][:, gt * 128:
                                                     gt * 128 + 128],
                             sb["sta"][:], start=False, stop=False)
                    ctx.inc(t.matmul(p[:, 0:TB],
                                     sb["w1l_s"][:, gt * 128: gt * 128 + 128],
                                     sb["stl"][:], start=False, stop=True),
                            "pe")
                tk = ctx.tok("pe")

                @blk.scalar
                def _(sc, gt=gt, p=p, tk=tk):
                    ctx.wait(sc, "scalar", tk, t_v,
                             state.get("t_hg_used", ("pe", 0)))
                    ctx.inc(sc.activation(
                        sb["hgT"][:, gt * TB:(gt + 1) * TB], p[:, 0:TB],
                        AF.Gelu, bias=sb["b1_s"][:, gt:gt + 1]), "ac")
                set_pfree(p, ctx.tok("ac"))
            t_hg = ctx.tok("ac")

            for j in range(TB // 128):
                @blk.tensor
                def _(t, j=j, t_hg=t_hg):
                    ctx.wait(t, "tensor", t_hg, pfree(pU))
                    for k in range(DT):
                        last = t.matmul(
                            pU[:, 0:24],
                            sb["hgT"][:, k * TB + j * 128:
                                      k * TB + j * 128 + 128],
                            sb["w2_s"][:, ts(k, 24)],
                            start=(k == 0), stop=(k == DT - 1))
                    ctx.inc(last, "pe")
                t_lg = ctx.tok("pe")

                @blk.vector
                def _(v, j=j, t_lg=t_lg):
                    ctx.wait(v, "vector", t_lg, t_cload,
                             state.get("t_wgt_used", ("pe", 0)))
                    v.tensor_tensor(sb["wgf"][:], pU[:, 0:24], sb["b2_b"][:],
                                    ALU.add)
                    v.drain()
                    v.tensor_tensor(sb["wgf"][:], sb["wgf"][:], sb["it_b"][:],
                                    ALU.mult)
                    v.drain()
                    v.reduce_max(sb["red"][:, 40:44],
                                 sb["wgf"][:].rearrange(
                                     "p (a s) -> p a s", a=4), axis=AX.X)
                    v.drain()
                    ctx.inc(v.tensor_tensor(
                        sb["wgf"][:].rearrange("p (a s) -> p a s", a=4),
                        sb["wgf"][:].rearrange("p (a s) -> p a s", a=4),
                        sb["red"][:, 40:44].unsqueeze(2)
                        .broadcast_to([128, 4, 6]), ALU.subtract), "ve")
                set_pfree(pU, ctx.tok("ve"))
                t_lgx = ctx.tok("ve")

                @blk.scalar
                def _(sc, t_lgx=t_lgx):
                    ctx.wait(sc, "scalar", t_lgx)
                    ctx.inc(sc.activation(sb["wgf"][:], sb["wgf"][:],
                                          AF.Exp), "ac")
                t_we = ctx.tok("ac")

                @blk.vector
                def _(v, t_we=t_we):
                    ctx.wait(v, "vector", t_we)
                    v.reduce_sum(sb["red"][:, 44:48],
                                 sb["wgf"][:].rearrange(
                                     "p (a s) -> p a s", a=4), axis=AX.X)
                    v.drain()
                    v.reciprocal(sb["red"][:, 44:48], sb["red"][:, 44:48])
                    v.drain()
                    ctx.inc(v.tensor_tensor(
                        sb["wgt"][:].rearrange("p (a s) -> p a s", a=4),
                        sb["wgf"][:].rearrange("p (a s) -> p a s", a=4),
                        sb["red"][:, 44:48].unsqueeze(2)
                        .broadcast_to([128, 4, 6]), ALU.mult), "ve")
                t_wg = ctx.tok("ve")

                @blk.tensor
                def _(t, j=j, t_wg=t_wg):
                    ctx.wait(t, "tensor", t_wg, pfree(pT))
                    ctx.inc(t.transpose(pT[0:24, 0:128], sb["wgt"][:],
                                        sb["idn"][:]), "pe")
                t_wtr = ctx.tok("pe")
                state["t_wgt_used"] = t_wtr

                @blk.scalar
                def _(sc, j=j, t_wtr=t_wtr):
                    ctx.wait(sc, "scalar", t_wtr)
                    ctx.inc(sc.copy(sb["wTt"][0:24, j * 128:j * 128 + 128],
                                    pT[0:24, 0:128]), "ac")
                set_pfree(pT, ctx.tok("ac"))
            t_wT = ctx.tok("ac")

            for s_ in range(6):
                for h in range(H):
                    r = h * 6 + s_
                    a0, a1 = 2 * h, 2 * h + 1

                    @blk.tensor
                    def _(t, r=r, t_wT=t_wT):
                        ctx.wait(t, "tensor", t_wT, pfree(p512b), t_wload)
                        ctx.inc(t.matmul(p512b[:, 0:TB],
                                         msk(SEL24 + r * 128, 128)[0:24, :],
                                         sb["wTt"][0:24, 0:TB],
                                         start=True, stop=True), "pe")
                    t_bw = ctx.tok("pe")

                    @blk.scalar
                    def _(sc, t_bw=t_bw):
                        ctx.wait(sc, "scalar", t_bw,
                                 state.get("t_bcw_used", ("ve", 0)))
                        ctx.inc(sc.copy(sb["bcv"][:, 512:512 + TB],
                                        p512b[:, 0:TB]), "ac")
                    set_pfree(p512b, ctx.tok("ac"))
                    t_bc = ctx.tok("ac")

                    eng = "vector" if (s_ * H + h) % 2 == 0 else "gpsimd"

                    def mixop(v, s_=s_, a0=a0, a1=a1, t_bc=t_bc, eng=eng):
                        ctx.wait(v, eng, t_bc, t_delta, t_fir_a, t_fir_b,
                                 t_v, state.get("t_fsd_used", ("pe", 0)))
                        if s_ == 0:
                            v.tensor_tensor(
                                sb["fsd"][:, a0 * TB:(a0 + 1) * TB],
                                branch_tile(s_, a0),
                                sb["bcv"][:, 512:512 + TB], ALU.mult)
                            v.tensor_tensor(
                                sb["fsd"][:, a1 * TB:(a1 + 1) * TB],
                                branch_tile(s_, a1),
                                sb["bcv"][:, 512:512 + TB], ALU.mult)
                        else:
                            tmp0 = sb["sq1"] if eng == "vector" else sb["sq2"]
                            v.tensor_tensor(tmp0[:], branch_tile(s_, a0),
                                            sb["bcv"][:, 512:512 + TB],
                                            ALU.mult)
                            v.tensor_tensor(sb["bcv"][:, 0:TB],
                                            branch_tile(s_, a1),
                                            sb["bcv"][:, 512:512 + TB],
                                            ALU.mult)
                            v.tensor_tensor(
                                sb["fsd"][:, a0 * TB:(a0 + 1) * TB],
                                sb["fsd"][:, a0 * TB:(a0 + 1) * TB],
                                tmp0[:], ALU.add)
                            v.tensor_tensor(
                                sb["fsd"][:, a1 * TB:(a1 + 1) * TB],
                                sb["fsd"][:, a1 * TB:(a1 + 1) * TB],
                                sb["bcv"][:, 0:TB], ALU.add)
                        tk = v.drain()
                        ctx.inc(tk, "ve" if eng == "vector" else "gp")

                    if eng == "vector":
                        @blk.vector
                        def _(v, f=mixop):
                            f(v)
                    else:
                        @blk.gpsimd
                        def _(g, f=mixop):
                            f(g)
                    state["t_bcw_used"] = (ctx.tok("ve") if eng == "vector"
                                           else ctx.tok("gp"))
                    state["t_fsd_used"] = state["t_bcw_used"]
            t_mixv = ctx.tok("ve")
            t_mixg = ctx.tok("gp")

            for a in range(DT):
                sqb = sb["sq1"] if a % 2 == 0 else sb["sq2"]

                @blk.scalar
                def _(sc, a=a, sqb=sqb, t_mixv=t_mixv, t_mixg=t_mixg):
                    ctx.wait(sc, "scalar", t_mixv, t_mixg,
                             state.get(f"t_sqr{a % 2}", ("pe", 0)))
                    ctx.inc(sc.activation(
                        sqb[:], sb["fsd"][:, a * TB:(a + 1) * TB],
                        AF.Square), "ac")
                t_sq = ctx.tok("ac")

                @blk.tensor
                def _(t, a=a, sqb=sqb, t_sq=t_sq):
                    ctx.wait(t, "tensor", t_sq,
                             pfree(p512a) if a == 0 else ("pe", 0))
                    h = a // 2
                    ctx.inc(t.matmul(p512a[0:H, 0:TB],
                                     msk(IND_H + 3 - h, H), sqb[:],
                                     start=(a == 0), stop=(a == DT - 1)),
                            "pe")
                state[f"t_sqr{a % 2}"] = ctx.tok("pe")
            t_rsum = ctx.tok("pe")

            @blk.scalar
            def _(sc, t_rsum=t_rsum):
                ctx.wait(sc, "scalar", t_rsum)
                ctx.inc(sc.activation(
                    sb["qfac"][:], p512a[0:H, 0:TB], AF.Sqrt,
                    bias=sb["cst"][0:H, LC_EPS5:LC_EPS5 + 1],
                    scale=sb["cst"][0:H, LC_INV256:LC_INV256 + 1]), "ac")
            set_pfree(p512a, ctx.tok("ac"))
            t_rr = ctx.tok("ac")

            @blk.vector
            def _(v, t_rr=t_rr):
                ctx.wait(v, "vector", t_rr)
                v.reciprocal(sb["qfac"][:], sb["qfac"][:])
                ctx.inc(v.drain(), "ve")
            t_rinv = ctx.tok("ve")

            for h in range(H):
                t_b = emit_head_bcast(sb["qfac"][:], h, 0, TB,
                                      sb["bcv"][:, 0:TB],
                                      [t_rinv,
                                       state.get("t_bcv_used", ("pe", 0))])

                @blk.vector
                def _(v, h=h, t_b=t_b):
                    ctx.wait(v, "vector", t_b, t_cload)
                    for a in (2 * h, 2 * h + 1):
                        v.scalar_tensor_tensor(
                            sb["fsd"][:, a * TB:(a + 1) * TB],
                            sb["fsd"][:, a * TB:(a + 1) * TB],
                            sb["onw_s"][:, a % 2:a % 2 + 1],
                            sb["bcv"][:, 0:TB], ALU.mult, ALU.mult)
                        v.drain()
                    ctx.inc(v.drain(), "ve")
                state["t_bcv_used"] = ctx.tok("ve")
            t_fsdn = ctx.tok("ve")

            if dbg and ib == dbg_blk and b == 0:
                @blk.sync
                def _(s):
                    ctx.wait(s, "sync", t_xn1T, t_qn, t_kn, t_v, t_beta,
                             t_delta, t_hg, t_wT, t_fsdn, t_fir_a, t_fir_b)
                    for nm, buf in [("dbg_xn1T", "xn1T"), ("dbg_qn", "qn"),
                                    ("dbg_kn", "kn"), ("dbg_vt", "vt"),
                                    ("dbg_br", "brs"), ("dbg_hg", "hgT"),
                                    ("dbg_fsd", "fsd")]:
                        ctx.inc(s.dma_start(out=ap[nm], in_=sb[buf][:]),
                                "ds", 16)
                    ctx.inc(s.dma_start(out=ap["dbg_beta"], in_=sb["beta"][:]),
                            "ds", 16)
                    ctx.inc(s.dma_start(out=ap["dbg_wgt"], in_=sb["wTt"][:]),
                            "ds", 16)

            for j in range(TB // 128):
                tt = t0 // 128 + j

                @blk.sync
                def _(s, tt=tt):
                    ctx.wait(s, "sync", state.get("t_out_used", ("ve", 0)),
                             state.get("t_x1_used", ("ve", 0)))
                    ctx.inc(s.dma_start(out=sb["x1t"][:],
                                        in_=x_in[ts(tt), :]), "ds", 16)
                    if b == 0:
                        ctx.inc(s.dma_start(out=sb["x2t"][:],
                                            in_=ap["moe_a"][ts(tt), :]),
                                "ds", 16)
                t_xo = ctx.tok("ds")
                state["t_x1_used"] = t_xo

                @blk.vector
                def _(v, t_xo=t_xo):
                    ctx.wait(v, "vector", t_xo)
                    if b == 0:
                        v.tensor_add(sb["x1t"][:], sb["x1t"][:], sb["x2t"][:])
                    ctx.inc(v.drain(), "ve")
                t_x1r = ctx.tok("ve")

                for oh in range(2):
                    p = p512a if oh == 0 else p512b

                    @blk.tensor
                    def _(t, j=j, oh=oh, p=p, t_fsdn=t_fsdn, t_xn1T=t_xn1T):
                        ctx.wait(t, "tensor", t_fsdn, t_xn1T, t_M, pfree(p),
                                 t_wload2)
                        for k in range(DT):
                            t.matmul(p[:],
                                     sb["fsd"][:, k * TB + j * 128:
                                               k * TB + j * 128 + 128],
                                     sb["ow_s"][:, k * D + oh * 512:
                                                k * D + oh * 512 + 512],
                                     start=(k == 0), stop=False)
                        for k in range(DT):
                            t.matmul(
                                p[:],
                                sb["xn1T"][:, k * TB + j * 128:
                                           k * TB + j * 128 + 128],
                                sb["M_s"][:, k * D + oh * 512:
                                          k * D + oh * 512 + 512],
                                start=False, stop=False)
                        for k in range(DT):
                            last = t.matmul(
                                p[:],
                                sb["xn1T"][:, k * TB + j * 128:
                                           k * TB + j * 128 + 128],
                                sb["M_l"][:, k * D + oh * 512:
                                          k * D + oh * 512 + 512],
                                start=False, stop=(k == DT - 1))
                        ctx.inc(last, "pe")
                    t_om = ctx.tok("pe")

                    @blk.vector
                    def _(v, oh=oh, p=p, t_om=t_om, t_x1r=t_x1r):
                        ctx.wait(v, "vector", t_om, t_x1r, t_cload,
                                 state.get("t_st_done", ("ds", 0)))
                        v.tensor_tensor(sb["outs"][:, ts(oh, 512)], p[:],
                                        sb["fb_b"][:, ts(oh, 512)], ALU.add)
                        v.drain()
                        ctx.inc(v.tensor_tensor(
                            sb["outs"][:, ts(oh, 512)],
                            sb["outs"][:, ts(oh, 512)],
                            sb["x1t"][:, ts(oh, 512)], ALU.add), "ve")
                    set_pfree(p, ctx.tok("ve"))
                t_oe = ctx.tok("ve")
                state["t_x1_used"] = t_oe

                @blk.sync
                def _(s, tt=tt, t_oe=t_oe):
                    ctx.wait(s, "sync", t_oe)
                    ctx.inc(s.dma_start(out=out_t[ts(tt), :],
                                        in_=sb["outs"][:]), "ds", 16)
                state["t_st_done"] = ctx.tok("ds")
                state["t_out_used"] = ctx.tok("ve")
            state["t_xnT_used"] = ctx.tok("pe")
            state["t_hg_used"] = ctx.tok("pe")
            state["t_raw_used"] = ctx.tok("pe")
            state["t_qn_used"] = ctx.tok("pe")
            state["t_kn_used"] = ctx.tok("pe")
            state["t_vt_used"] = ctx.tok("pe")
            state["t_brs_used"] = ctx.tok("pe")

            @blk.vector
            def _(v):
                ctx.wait(v, "vector", ctx.tok("pe"))
                for a in range(DT):
                    vb0 = a * (TB + 32)
                    v.tensor_copy(sb["vt"][:, vb0:vb0 + 32],
                                  sb["vt"][:, vb0 + TB:vb0 + TB + 32])
                ctx.inc(v.drain(), "ve")
            state["t_vtl"] = ctx.tok("ve")

    @blk.sync
    def _(s):
        ctx.wait(s, "sync", ctx.tok("ds"))

    stack.close()
    return nc


def l1_masks_np():
    import ml_dtypes
    w = 3 * 128 + 47 + 95 + 7 + 512 + 3072
    m = np.zeros((128, w), np.float32)
    for i in range(128):
        m[i, 0:i] = -1.0
    for i in range(128):
        m[i, 128 + i + 1:128 + 128] = -1.0
    for i in range(128):
        m[i, 256 + i:256 + 128] = 1.0
    m[:, 384 + 23] = 1.0 / 256
    m[:, 384 + 47 + 31] = 1.0 / 255
    m[:, 384 + 47 + 63] = 1.0
    m[:, 384 + 47 + 95 + 3] = 1.0
    base = 384 + 47 + 95 + 7
    for h in range(4):
        m[h, base + h * 128: base + (h + 1) * 128] = 1.0
    base24 = base + 512
    for r in range(24):
        m[r, base24 + r * 128: base24 + (r + 1) * 128] = 1.0
    return m.astype(ml_dtypes.bfloat16)


def l1_consts_np():
    c = np.zeros((128, 16), np.float32)
    for i, val in enumerate(L1_CONSTS):
        c[:, i] = val
    return c


def l1_in_maps(inputs, y_full, moe_full, Mfull):
    n = np.asarray
    qc = np.ascontiguousarray(n(inputs["L1_attn_qc_w"]).reshape(D, 4))
    kc = np.ascontiguousarray(n(inputs["L1_attn_kc_w"]).reshape(D, 4))
    vc = np.ascontiguousarray(n(inputs["L1_attn_vc_w"]).reshape(D, 4))
    firw = np.concatenate([n(inputs[f"L1_attn_fir_w{k}"]).reshape(D, k)
                           for k in FIRS], axis=1)
    w1 = n(inputs["L1_attn_mlp_w1"])
    w1h = np.ascontiguousarray(w1[:D])
    stats_rows = {0: [], 1: [], 2: [], 3: []}
    for kind in range(4):
        rows = np.zeros((24, D), np.float32)
        for s_ in range(6):
            for h in range(H):
                rows[s_ * 4 + h] = w1[D + h * 24 + s_ * 4 + kind]
        stats_rows[kind] = np.ascontiguousarray(rows)
    base = {
        "qw": n(inputs["L1_attn_q_w"]), "kw": n(inputs["L1_attn_k_w"]),
        "vw": n(inputs["L1_attn_v_w"]), "bw": n(inputs["L1_attn_b_w"]),
        "ow": n(inputs["L1_attn_o_w"]),
        "qcw": qc, "kcw": kc, "vcw": vc, "firw": firw,
        "w1h": w1h, "w1m": stats_rows[0], "w1s": stats_rows[1],
        "w1a": stats_rows[2], "w1l": stats_rows[3],
        "b1": np.ascontiguousarray(
            n(inputs["L1_attn_mlp_b1"]).reshape(DT, 128).T),
        "w2": n(inputs["L1_attn_mlp_w2"]),
        "b2": n(inputs["L1_attn_mlp_b2"]).reshape(1, 24),
        "glt": n(inputs["L1_attn_gate_log_temp"]).reshape(1, H),
        "onw": np.ascontiguousarray(
            n(inputs["L1_attn_onorm_w"]).reshape(2, 128).T),
        "fbias": n(inputs["L1_ffn_bias"]).reshape(1, D),
        "Mfull": np.ascontiguousarray(Mfull, np.float32),
        "ident": np.eye(128, dtype=np.float32),
        "consts": l1_consts_np(), "masksb": l1_masks_np(),
    }
    base = {k: np.ascontiguousarray(np.asarray(v, base[k].dtype
                                               if hasattr(v, 'dtype')
                                               else np.float32))
            for k, v in base.items()}
    zeros = np.zeros((S, D), np.float32)
    maps = []
    for c in range(8):
        m = dict(base)
        m["ya"] = np.ascontiguousarray(y_full[c])
        m["yb"] = np.ascontiguousarray(y_full[c + 8])
        m["moe_a"] = np.ascontiguousarray(moe_full[c]) if c < 2 else zeros
        maps.append(m)
    return maps


NPOLLS = 64
POLL_K = 2000
MT2 = 2048


def build_fused(dbg=False):
    nc = bass.Bass("TRN2", target_bir_lowering=False)
    ap = {}
    for nm, shp in [("x0", [S, D]), ("x1", [S, D]), ("x_moe", [MT2, D]),
                    ("wqT", [D, D]), ("wkT", [D, D]), ("wvT", [D, D]),
                    ("woT", [D, D]), ("router_wT", [D, E]),
                    ("router_b", [1, E]), ("ident", [128, 128]),
                    ("consts", [128, 16])]:
        ap[nm] = nc.dram_tensor(nm, shp, F32, kind="ExternalInput").ap()
    ap["masks"] = nc.dram_tensor("masks", [4, 128, 512], BF16,
                                 kind="ExternalInput").ap()
    for nm, shp in [("wfr", [D, D]), ("wfi", [D, D]), ("Ffr", [D, D]),
                    ("Ffi", [D, D]), ("fcol", [D, 512]), ("gcol", [D, 512]),
                    ("ngcol", [D, 512])]:
        ap[nm] = nc.dram_tensor(nm, shp, F32, kind="ExternalInput").ap()
    ap["ei"] = nc.dram_tensor("ei", [E, D, I_FF], F32, kind="ExternalInput").ap()
    ap["eo"] = nc.dram_tensor("eo", [E, I_FF, D], F32, kind="ExternalInput").ap()

    def IN(nm, shp, dt=F32):
        ap[nm] = nc.dram_tensor(nm, shp, dt, kind="ExternalInput").ap()

    IN("qw", [D, D]); IN("kw", [D, D]); IN("vw", [D, D]); IN("bw", [D, H])
    IN("ow", [D, D])
    IN("qcw", [D, 4]); IN("kcw", [D, 4]); IN("vcw", [D, 4])
    IN("firw", [D, NTAP])
    IN("w1h", [D, D])
    IN("w1m", [24, D]); IN("w1s", [24, D]); IN("w1a", [24, D]); IN("w1l", [24, D])
    IN("b1", [128, DT]); IN("w2", [D, 24]); IN("b2", [1, 24]); IN("glt", [1, H])
    IN("onw", [128, 2]); IN("fbias", [1, D])
    IN("alpha", [128, 1])
    ap["cfg"] = nc.dram_tensor("cfg", [1, 16], mybir.dt.int32,
                               kind="ExternalInput").ap()
    ap["masksb"] = nc.dram_tensor("masksb",
                                  [128, 3 * 128 + 47 + 95 + 7 + 512 + 3072],
                                  BF16, kind="ExternalInput").ap()
    ap["outa"] = nc.dram_tensor("outa", [S, D], F32, kind="ExternalOutput").ap()
    ap["outb"] = nc.dram_tensor("outb", [S, D], F32, kind="ExternalOutput").ap()

    sc_q = nc.dram_tensor("sc_q", [128, DT * S], BF16, kind="Internal").ap()
    sc_k = nc.dram_tensor("sc_k", [128, DT * S], BF16, kind="Internal").ap()
    sc_v = nc.dram_tensor("sc_v", [S, D], BF16, kind="Internal").ap()
    sc_ya = nc.dram_tensor("sc_ya", [S, D], F32, kind="Internal").ap()
    sc_yb = nc.dram_tensor("sc_yb", [S, D], F32, kind="Internal").ap()
    privM_t = nc.dram_tensor("privM", [512, D], F32, kind="Internal")
    priv_M = privM_t.ap()
    privK_t = nc.dram_tensor("privK", [MT2, D], F32, kind="Internal")
    priv_moe = privK_t.ap()
    shmM_t = nc.dram_tensor("shm_M", [D, D], F32, kind="Internal",
                            addr_space="Shared")
    shm_M = shmM_t.ap()
    shmK_t = nc.dram_tensor("shm_moe", [S, D], F32, kind="Internal",
                            addr_space="Shared")
    shm_moe = shmK_t.ap()
    shf = nc.dram_tensor("shf", [2, 16], mybir.dt.int32, kind="Internal",
                         addr_space="Shared").ap()
    if dbg:
        for nm, shp, dt in [
                ("dbg_xnTm", [128, DT * MT2], BF16),
                ("dbg_wrt", [128, 16 * E], F32),
                ("dbg_int", [128, (IB // 128) * MT2], BF16),
                ("dbg_xnT", [128, DT * S], BF16),
                ("dbg_xn1T", [128, DT * TB], BF16),
                ("dbg_qn", [128, DT * TB], BF16),
                ("dbg_kn", [128, DT * TB], BF16),
                ("dbg_vtb", [128, DT * (TB + 32)], BF16),
                ("dbg_beta", [4, TB], F32),
                ("dbg_br", [128, 5 * DT * TB], BF16),
                ("dbg_stats", [24, 4 * TB], BF16),
                ("dbg_hg", [128, DT * TB], BF16),
                ("dbg_wgt", [24, TB], BF16),
                ("dbg_fsd", [128, DT * TB], BF16),
                ("dbg_M", [128, DT * D], BF16),
                ("dbg_dmx", [128, 24 * 128], BF16),
                ("dbg_bb", [128, CH], BF16)]:
            ap[nm] = nc.dram_tensor(nm, shp, dt, kind="ExternalOutput").ap()

    sb = {}

    def A(name, shape, dtype):
        sb[name] = nc.alloc_sbuf_tensor(name, shape, dtype)
        return sb[name]

    A("cst", [128, 16], F32)
    A("idn", [128, 128], BF16)
    A("mask", [128, 4 * 512], BF16)
    A("ones1", [1, 128], BF16)
    A("ones1f", [1, 128], F32)
    A("rb_b", [128, E], F32)
    A("red", [128, 64], F32)
    A("rw", [128, DT * E], BF16)
    A("wrt", [128, 16 * E], F32)
    A("pt", [128, 8 * 128], BF16)
    A("ctxq", [128, D], BF16)
    A("ctxT", [128, DT * 128], BF16)
    A("idnf", [128, 128], F32)
    A("cft", [1, 16], mybir.dt.int32)
    A("flt", [1, 16], mybir.dt.int32)
    A("tiny", [1, 1], F32)
    A("glt_r", [1, 32], F32)
    A("alf", [128, 1], F32)
    A("tls", [128, 3 * DT * 3], BF16)
    A("wgt", [128, 24], BF16)
    A("onw_s", [128, 2], F32)
    A("rowbf", [4, TB], BF16)
    A("bb", [128, CH], BF16)

    BASE = 31 * KB
    assert nc.bytes_allocated(sb["cst"].space) <= BASE, \
        nc.bytes_allocated(sb["cst"].space)
    OA = BASE
    OB = BASE + 64 * KB
    OC = BASE + 128 * KB

    def AT(name, shape, dtype, off):
        sb[name] = nc.alloc_sbuf_tensor_at(name, shape, dtype, offset=off)
        return sb[name]

    AT("xnT", [128, DT * S], BF16, OA)
    AT("ktA", [128, DT * S], BF16, OA)
    AT("xnTm", [128, DT * MT2], BF16, OA)
    AT("wib", [128, DT * IB], BF16, OA + 32 * KB)
    AT("wob", [128, (IB // 128) * D], BF16, OA + 48 * KB)
    AT("intT", [128, (IB // 128) * MT2], BF16, OC)
    AT("vt", [128, TT * D], BF16, OB)
    AT("xw", [128, 4 * D], F32, OB)
    AT("xnw", [128, 4 * D], BF16, OB + 16 * KB)
    AT("lnscr", [128, D], F32, OB + 24 * KB)
    AT("stg", [128, 4 * 512], BF16, OB + 28 * KB)
    AT("acc", [128, 16 * D], F32, OB)
    AT("wq", [128, DT * D], BF16, OC)
    AT("wk", [128, DT * D], BF16, OC + 16 * KB)
    AT("wv", [128, DT * D], BF16, OC + 32 * KB)
    AT("wo", [128, DT * D], BF16, OC + 48 * KB)
    AT("Frf", [128, DT * D], F32, OA)
    AT("Fif", [128, DT * D], F32, OA + 32 * KB)
    AT("Wrf", [128, DT * D], F32, OC)
    AT("Wif", [128, DT * D], F32, OC + 32 * KB)
    AT("fcl", [128, DT * 128], F32, OB + 32 * KB)
    AT("gcl", [128, DT * 128], F32, OB + 36 * KB)
    AT("ngc", [128, DT * 128], F32, OB + 40 * KB)
    AT("arc", [128, D], F32, OB + 44 * KB)
    AT("brc", [128, D], F32, OB + 48 * KB)
    AT("atr", [128, DT * 128], F32, OB + 52 * KB)
    AT("btr", [128, DT * 128], F32, OB + 56 * KB)
    AT("mou", [128, D], F32, OB + 60 * KB)
    AT("scrow", [128, S], F32, OC)
    AT("prow", [128, S], BF16, OC + 16 * KB)
    AT("qblk", [128, DT * 512], BF16, OC + 24 * KB)
    AT("qblk2", [128, DT * 512], BF16, OC + 32 * KB)
    AT("xres", [128, D], F32, OC + 40 * KB)
    AT("y1s", [128, D], F32, OC + 44 * KB)

    OW = BASE
    AT("qw_s", [128, DT * D], BF16, OW)
    AT("kw_s", [128, DT * D], BF16, OW + 16 * KB)
    AT("ow_s", [128, DT * D], BF16, OW + 32 * KB)
    AT("vwh", [128, DT * D], BF16, OW + 48 * KB)
    AT("M_s", [128, DT * D], BF16, OW + 64 * KB)
    AT("M_l", [128, DT * D], BF16, OW + 80 * KB)
    bump = [OW + 96 * KB]

    def AS(name, shape, dtype):
        es = 4 if dtype == F32 else (2 if dtype == BF16 else 4)
        nbytes = shape[1] * es
        import math as _m
        for d in shape[2:]:
            nbytes *= d
        off = (bump[0] + 31) // 32 * 32
        sb[name] = nc.alloc_sbuf_tensor_at(name, shape, dtype, offset=off)
        bump[0] = off + nbytes
        return sb[name]

    AS("msk", [128, 3 * 128 + 47 + 95 + 7 + 512 + 3072], BF16)
    AS("bw_s", [128, DT * H], BF16)
    AS("qcw_s", [128, DT * 4], F32)
    AS("kcw_s", [128, DT * 4], F32)
    AS("vcw_s", [128, DT * 4], F32)
    AS("fir_s", [128, DT * NTAP], F32)
    AS("b1_s", [128, DT], F32)
    AS("w2_s", [128, DT * 24], BF16)
    AS("b2_b", [128, 24], F32)
    AS("it_b", [128, 24], F32)
    AS("fb_b", [128, D], BF16)
    AS("w1m_s", [24, D], BF16)
    AS("w1s_s", [24, D], BF16)
    AS("w1a_s", [24, D], BF16)
    AS("w1l_s", [24, D], BF16)
    AS("stm", [24, TB], BF16)
    AS("sts", [24, TB], BF16)
    AS("sta", [24, TB], BF16)
    AS("stl", [24, TB], BF16)
    AS("stf", [24, TB], F32)
    AS("s_sb", [128, H * 512], BF16)
    AS("wgf", [128, 24], F32)
    AS("wTt", [24, TB], BF16)
    AS("bcv", [128, 2 * 512], BF16)
    AS("sq1", [128, 256], BF16)
    AS("sq2", [128, 256], BF16)
    AS("beta", [4, TB], F32)
    AS("qfac", [4, TB], F32)
    AS("kfac", [4, TB], F32)
    AS("gtmp", [128, 2 * TB], BF16)
    OX = (bump[0] + 1023) // 1024 * 1024
    AT("Mstg", [128, DT * D // 2], F32, OX)
    AT("xn1T", [128, DT * TB], BF16, OX)
    AT("raw", [128, DT * (TB + 4)], BF16, OX + 4 * KB)
    AT("hgT", [128, DT * TB], BF16, OX + 4 * KB)
    AT("qn", [128, DT * TB], BF16, OX + 9 * KB)
    AT("fsd", [128, DT * TB], BF16, OX + 9 * KB)
    AT("kn", [128, DT * TB], BF16, OX + 13 * KB)
    AT("vtb", [128, DT * (TB + 32)], BF16, OX + 17 * KB)
    AT("brs", [128, 5 * DT * TB], BF16, OX + 22 * KB)
    AT("dmx", [128, 24 * 128], BF16, OX + 42 * KB)
    AT("x1t", [128, D], F32, OX + 48 * KB)
    AT("outs", [128, D], F32, OX + 52 * KB)
    AT("x2t", [128, D], F32, OX + 52 * KB)
    assert OX + 56 * KB <= 223 * KB, (OX, bump[0])

    MSK_A, MSK_AT, MSK_U = 0, 128, 256
    IND_M, IND_2, IND_H = 384, 384 + 47, 384 + 47 + 95
    SEL = 384 + 47 + 95 + 7
    SEL24 = SEL + 512

    def msk(off, w):
        return sb["msk"][:, off:off + w]

    BR_F1, BR_F3, BR_F7, BR_F31, BR_DL = range(5)
    BR_SRC = [BR_F1, BR_F3, BR_F7, BR_F31, BR_DL, None]

    def brs_ap(br, a, c0, w):
        base = (br * DT + a) * TB
        return sb["brs"][:, base + c0: base + c0 + w]

    def vt_ap(a, c0, w):
        return sb["vtb"][:, a * (TB + 32) + 32 + c0:
                         a * (TB + 32) + 32 + c0 + w]

    DM = {}
    _dm = [0]

    def dm_alloc(name, n=1):
        DM[name] = _dm[0]
        _dm[0] += n

    for nm, n in [("ktok", 2), ("kbtok", 2), ("vbtok", 2), ("kbT", 2),
                  ("vbT", 2), ("A", 1), ("ATr", 1), ("X", 1), ("XT", 1),
                  ("P", 1), ("P2", 1), ("Tt", 1), ("wn", 2), ("uh", 2),
                  ("at", 1)]:
        dm_alloc(nm, n)
    assert _dm[0] <= 24, _dm

    def dm(name, i=0, w=1):
        c = (DM[name] + i) * 128
        return sb["dmx"][:, c:c + w * 128]

    p512a = nc.alloc_psum_tensor("p512a", [128, 512], F32)
    p512b = nc.alloc_psum_tensor("p512b", [128, 512], F32)
    pS = nc.alloc_psum_tensor("pS", [128, 512], F32)
    pE = nc.alloc_psum_tensor("pE", [128, 512], F32)
    pF = nc.alloc_psum_tensor("pF", [128, 512], F32)
    pG = nc.alloc_psum_tensor("pG", [128, 512], F32)
    pH = nc.alloc_psum_tensor("pH", [128, 512], F32)
    pT = nc.alloc_psum_tensor("pT", [128, 512], BF16)
    pTa = pT
    pTb = pT
    pSh = [pS, pF, pG, pH]

    stack = contextlib.ExitStack()
    ctx = Ctx(nc)
    for s_ in ["dg", "ds", "pe", "ac", "ve", "gp"]:
        ctx.sem(s_, stack)
    pf = stack.enter_context(nc.semaphore("pf"))
    px = stack.enter_context(nc.semaphore("px"))
    blk = stack.enter_context(nc.Block())

    state = {"psum_free": {}}

    def pfree(p):
        return state["psum_free"].get(p.name, ("pe", 0))

    def set_pfree(p, tok):
        state["psum_free"][p.name] = tok

    def cstc(i):
        return sb["cst"][:, i:i + 1]

    @blk.gpsimd
    def _(g):
        ctx.inc(g.dma_start(
            out=sb["rw"][:].rearrange("p (a o) -> p a o", a=DT),
            in_=ap["router_wT"].rearrange("(a p) o -> p a o", p=128)), "dg", 16)
        ctx.inc(g.dma_start(out=sb["idn"][:], in_=ap["ident"]), "dg", 16)
    t_wload0 = ctx.tok("dg")

    @blk.sync
    def _(s):
        ctx.inc(s.dma_start(out=sb["cst"][:], in_=ap["consts"]), "ds", 16)
        ctx.inc(s.dma_start(
            out=sb["mask"][:].rearrange("p (a o) -> p a o", a=4),
            in_=ap["masks"].rearrange("a p o -> p a o")), "ds", 16)
        ctx.inc(s.dma_start(out=sb["red"][0:1, 0:E], in_=ap["router_b"]),
                "ds", 16)
        ctx.inc(s.dma_start(out=sb["cft"][:], in_=ap["cfg"]), "ds", 16)
        ctx.inc(s.dma_start(out=sb["alf"][:], in_=ap["alpha"]), "ds", 16)
    t_cload0 = ctx.tok("ds")
    t_cload = t_cload0

    @blk.sync
    def _(s):
        for nm, dst in [("Ffr", "Frf"), ("Ffi", "Fif"), ("wfr", "Wrf"),
                        ("wfi", "Wif")]:
            ctx.inc(s.dma_start(
                out=sb[dst][:].rearrange("p (a o) -> p a o", a=DT),
                in_=ap[nm].rearrange("(a p) o -> p a o", p=128)), "ds", 16)
        ctx.inc(s.dma_start(out=sb["idnf"][:], in_=ap["ident"]), "ds", 16)
    t_mload = ctx.tok("ds")

    def mm_mrow(lhs1, rhs1, lhs2, rhs2, out_s, scale_ap=None, dep=None):
        for nb in range(2):
            p = p512a if nb == 0 else p512b

            @blk.tensor
            def _(t, nb=nb, p=p):
                ctx.wait(t, "tensor", *(dep or [t_mload]), pfree(p))
                for k in range(DT):
                    t.matmul(p[:],
                             sb[lhs1][:, k * 128:(k + 1) * 128],
                             sb[rhs1][:, k * D + nb * 512:
                                      k * D + nb * 512 + 512],
                             start=(k == 0), stop=False)
                for k in range(DT):
                    last = t.matmul(
                        p[:], sb[lhs2][:, k * 128:(k + 1) * 128],
                        sb[rhs2][:, k * D + nb * 512: k * D + nb * 512 + 512],
                        start=False, stop=(k == DT - 1))
                ctx.inc(last, "pe")
            tk = ctx.tok("pe")

            @blk.scalar
            def _(sc, nb=nb, p=p, tk=tk):
                ctx.wait(sc, "scalar", tk)
                if scale_ap is not None:
                    ctx.inc(sc.activation(sb[out_s][:, nb * 512:
                                                    nb * 512 + 512],
                                          p[:], AF.Copy, scale=scale_ap),
                            "ac")
                else:
                    ctx.inc(sc.copy(sb[out_s][:, nb * 512: nb * 512 + 512],
                                    p[:]), "ac")
            set_pfree(p, ctx.tok("ac"))

    t_m_prev = ("ds", 0)
    for mi in range(4):
        @blk.sync
        def _(s, mi=mi, t_m_prev=t_m_prev):
            ctx.wait(s, "sync", t_m_prev, ctx.tok("ac"))
            for nm, dst in [("fcol", "fcl"), ("gcol", "gcl"),
                            ("ngcol", "ngc")]:
                ctx.inc(s.dma_start(
                    out=sb[dst][:].rearrange("p (a o) -> p a o", a=DT),
                    in_=ap[nm].rearrange("(a p) o -> p a o", p=128)
                    [:, :, mi * 128:(mi + 1) * 128]), "ds", 16)
        t_sl = ctx.tok("ds")

        mm_mrow("fcl", "Wrf", "ngc", "Wif", "arc",
                dep=[t_mload, t_sl])
        mm_mrow("fcl", "Wif", "gcl", "Wrf", "brc",
                dep=[t_mload, t_sl])
        t_ab = ctx.tok("ac")

        for src, dst in [("arc", "atr"), ("brc", "btr")]:
            for half in range(2):
                @blk.tensor
                def _(t, src=src, half=half, t_ab=t_ab):
                    ctx.wait(t, "tensor", t_ab, pfree(pS))
                    for q in range(4):
                        a = half * 4 + q
                        last = t.transpose(pS[:, ts(q)],
                                           sb[src][:, ts(a)], sb["idnf"][:])
                    ctx.inc(last, "pe")
                tk = ctx.tok("pe")

                @blk.scalar
                def _(sc, dst=dst, half=half, tk=tk):
                    ctx.wait(sc, "scalar", tk)
                    for q in range(4):
                        a = half * 4 + q
                        last = sc.copy(sb[dst][:, ts(a)], pS[:, ts(q)])
                    ctx.inc(last, "ac")
                set_pfree(pS, ctx.tok("ac"))
        t_abt = ctx.tok("ac")

        @blk.tensor
        def _(t, t_abt=t_abt):
            ctx.wait(t, "tensor", t_abt)
        mm_mrow("atr", "Frf", "btr", "Fif", "mou",
                scale_ap=cstc(C_INV1024), dep=[t_mload])
        t_msc = ctx.tok("ac")

        @blk.sync
        def _(s, mi=mi, t_msc=t_msc):
            ctx.wait(s, "sync", t_msc)
            ctx.inc(s.dma_start(out=priv_M[mi * 128:(mi + 1) * 128, :],
                                in_=sb["mou"][:]), "ds", 16)
        t_m_prev = ctx.tok("ds")
    t_m_all = t_m_prev
    t_mdone = t_m_all

    @blk.vector
    def _(v):
        ctx.wait(v, "vector", t_cload)
        v.memset(sb["ones1"][:], 1.0)
        v.memset(sb["ones1f"][:], 1.0)
        ctx.inc(v.tensor_copy(sb["stg"][0:1, 0:E], sb["red"][0:1, 0:E]), "ve")
    t_ones = ctx.tok("ve")
    t_ones0 = t_ones

    @blk.tensor
    def _(t):
        ctx.wait(t, "tensor", t_ones, t_m_all)
        ctx.inc(t.matmul(pS[:, 0:E], sb["ones1"][:], sb["stg"][0:1, 0:E],
                         start=True, stop=True), "pe")
    t_rbm = ctx.tok("pe")

    @blk.scalar
    def _(sc):
        ctx.wait(sc, "scalar", t_rbm)
        ctx.inc(sc.copy(sb["rb_b"][:], pS[:, 0:E]), "ac")
    set_pfree(pS, ctx.tok("ac"))
    t_wload = t_wload0

    def emit_ln_wave(x_in_view, deps):
        @blk.sync
        def _(s):
            ctx.wait(s, "sync", *deps, state.get("m_guard", ("ds", 0)))
            ctx.inc(s.dma_start(out=sb["xw"][:].rearrange(
                "p (j d) -> p j d", j=4), in_=x_in_view), "ds", 16)
        t_x = ctx.tok("ds")

        @blk.scalar
        def _(sc):
            ctx.wait(sc, "scalar", t_cload, t_x, *deps)
            for j in range(4):
                sc.activation(sb["lnscr"][:], sb["xw"][:, ts(j, D)],
                              AF.Square, accum_out=sb["red"][:, j:j + 1])
            sc.drain()
            ctx.inc(sc.copy(sb["red"][:, 4:8], sb["red"][:, 0:4]), "ac")
        t_s2 = ctx.tok("ac")

        @blk.vector
        def _(v):
            ctx.wait(v, "vector", t_x, *deps)
            for j in range(4):
                v.tensor_scalar(sb["xnw"][:, ts(j, D)], sb["xw"][:, ts(j, D)],
                                1.0, 0.0, ALU.mult, ALU.add,
                                accum_out=sb["red"][:, 8 + j:9 + j])
            ctx.wait(v, "vector", t_s2)
            v.tensor_scalar(sb["red"][:, 16:20], sb["red"][:, 8:12],
                            1.0 / D, None, ALU.mult)
            v.drain()
            v.tensor_tensor(sb["red"][:, 24:28], sb["red"][:, 16:20],
                            sb["red"][:, 16:20], ALU.mult)
            v.tensor_scalar(sb["red"][:, 4:8], sb["red"][:, 4:8],
                            1.0 / D, 1e-5, ALU.mult, ALU.add)
            v.drain()
            ctx.inc(v.tensor_sub(sb["red"][:, 24:28], sb["red"][:, 4:8],
                                 sb["red"][:, 24:28]), "ve")
        t_var = ctx.tok("ve")

        @blk.scalar
        def _(sc):
            ctx.wait(sc, "scalar", t_var)
            ctx.inc(sc.activation(sb["red"][:, 32:36], sb["red"][:, 24:28],
                                  AF.Sqrt), "ac")
        t_sd = ctx.tok("ac")

        @blk.vector
        def _(v):
            ctx.wait(v, "vector", t_sd)
            v.reciprocal(sb["red"][:, 32:36], sb["red"][:, 32:36])
            v.drain()
            for j in range(4):
                last = v.tensor_scalar(sb["xnw"][:, ts(j, D)],
                                       sb["xw"][:, ts(j, D)],
                                       sb["red"][:, 16 + j:17 + j],
                                       sb["red"][:, 32 + j:33 + j],
                                       ALU.subtract, ALU.mult)
            ctx.inc(last, "ve")
        return ctx.tok("ve")

    def emit_transpose_wave(src, src_col0, n_tiles, dst, dst_T, dst_tcol,
                            deps):
        for j in range(n_tiles):
            for half, p in enumerate([pTa, pTb]):
                @blk.tensor
                def _(t, j=j, half=half, p=p):
                    ctx.wait(t, "tensor", t_wload, pfree(p), *deps)
                    for q in range(4):
                        a = half * 4 + q
                        last = t.transpose(
                            p[:, ts(q)],
                            src[:, src_col0 + j * D + a * 128:
                                src_col0 + j * D + a * 128 + 128],
                            sb["idn"][:])
                    ctx.inc(last, "pe")
                t_tr = ctx.tok("pe")

                @blk.scalar
                def _(sc, j=j, half=half, p=p, t_tr=t_tr):
                    ctx.wait(sc, "scalar", t_tr)
                    tc = dst_tcol + j * 128
                    for q in range(4):
                        a = half * 4 + q
                        last = sc.copy(
                            dst[:, a * dst_T + tc: a * dst_T + tc + 128],
                            p[:, ts(q)])
                    ctx.inc(last, "ac")
                set_pfree(p, ctx.tok("ac"))
        return ctx.tok("ac")

    def emit_mm(lhsT_of, rhs_of, n_m, n_n, n_k, evac, deps=(), n_free=512):
        for mi in range(n_m):
            for ni in range(n_n):
                p = p512a if (mi * n_n + ni) % 2 == 0 else p512b

                @blk.tensor
                def _(t, mi=mi, ni=ni, p=p):
                    ctx.wait(t, "tensor", t_wload, pfree(p), *deps)
                    for k in range(n_k):
                        last = t.matmul(p[:, 0:n_free], lhsT_of(k, mi),
                                        rhs_of(k, ni), start=(k == 0),
                                        stop=(k == n_k - 1))
                    ctx.inc(last, "pe")
                set_pfree(p, evac(mi, ni, p, ctx.tok("pe")))

    state["m_guard"] = t_mdone
    t_prev = ctx.tok("ac")
    xmoe_v = ap["x_moe"].rearrange("(j p) d -> p j d", p=128)
    for w in range(MT2 // 512):
        t_ln = emit_ln_wave(xmoe_v[:, 4 * w:4 * w + 4, :], [t_prev])
        t_prev = emit_transpose_wave(sb["xnw"], 0, 4, sb["xnTm"], MT2,
                                     w * 512, [t_ln])
    t_xnTm = t_prev

    for tt2 in range(MT2 // 128):
        @blk.tensor
        def _(t, tt2=tt2):
            ctx.wait(t, "tensor", t_xnTm, pfree(pS))
            for k in range(DT):
                last = t.matmul(
                    pS[:, 0:E],
                    sb["xnTm"][:, k * MT2 + tt2 * 128: k * MT2 + tt2 * 128 + 128],
                    sb["rw"][:, ts(tt2 * 0 + k, E)], start=(k == 0),
                    stop=(k == DT - 1))
            ctx.inc(last, "pe")
        t_lg = ctx.tok("pe")

        @blk.vector
        def _(v, t_lg=t_lg):
            ctx.wait(v, "vector", t_lg)
            v.tensor_tensor(sb["red"][:, 40:40 + E], pS[:, 0:E],
                            sb["rb_b"][:], ALU.add)
            v.drain()
            ctx.inc(v.reduce_max(sb["red"][:, 48:49], sb["red"][:, 40:40 + E],
                                 axis=AX.X, negate=True), "ve")
        set_pfree(pS, ctx.tok("ve"))
        t_mx = ctx.tok("ve")

        @blk.scalar
        def _(sc, t_mx=t_mx):
            ctx.wait(sc, "scalar", t_mx)
            sc.activation(sb["red"][:, 40:40 + E],
                          sb["red"][:, 40:40 + E], AF.Exp,
                          bias=sb["red"][:, 48:49],
                          accum_out=sb["red"][:, 49:50])
            sc.drain()
            ctx.inc(sc.copy(sb["red"][:, 50:51], sb["red"][:, 49:50]), "ac")
        t_exp = ctx.tok("ac")

        @blk.vector
        def _(v, tt2=tt2, t_exp=t_exp):
            ctx.wait(v, "vector", t_exp)
            v.reciprocal(sb["red"][:, 50:51], sb["red"][:, 50:51])
            v.drain()
            ctx.inc(v.tensor_scalar(sb["wrt"][:, ts(tt2, E)],
                                    sb["red"][:, 40:40 + E],
                                    sb["red"][:, 50:51], None, ALU.mult), "ve")
    t_wrt = ctx.tok("ve")

    if dbg:
        @blk.sync
        def _(s):
            ctx.wait(s, "sync", t_xnTm, t_wrt)
            ctx.inc(s.dma_start(out=ap["dbg_xnTm"], in_=sb["xnTm"][:]),
                    "ds", 16)
            ctx.inc(s.dma_start(out=ap["dbg_wrt"], in_=sb["wrt"][:]),
                    "ds", 16)

    @blk.gpsimd
    def _(g):
        ctx.wait(g, "gpsimd", t_xnTm)
        ctx.inc(g.memset(sb["acc"][:], 0.0), "gp")
    t_accz = ctx.tok("gp")

    t_int_used = t_m_all
    for e in range(E):
        for ib in range(I_FF // IB):
            @blk.gpsimd
            def _(g, e=e, ib=ib, t_int_used=t_int_used):
                ctx.wait(g, "gpsimd", t_int_used)
                ctx.inc(g.dma_start(
                    out=sb["wib"][:].rearrange("p (a i) -> p a i", a=DT),
                    in_=ap["ei"][e].rearrange("(a p) i -> p a i", p=128)
                    [:, :, ib * IB:(ib + 1) * IB]), "dg", 16)
                ctx.inc(g.dma_start(
                    out=sb["wob"][:].rearrange("p (a o) -> p a o",
                                               a=IB // 128),
                    in_=ap["eo"][e].rearrange("(a p) o -> p a o", p=128)
                    [:, 8 * ib:8 * ib + 8, :]), "dg", 16)
            t_w = ctx.tok("dg")

            def ev_int(mi, ni, p, tok):
                @blk.scalar
                def _(sc, mi=mi, ni=ni, p=p, tok=tok):
                    ctx.wait(sc, "scalar", tok)
                    ctx.inc(sc.activation(
                        sb["intT"][:, mi * MT2 + ni * 512:
                                   mi * MT2 + ni * 512 + 512],
                        p[:], AF.Relu), "ac")
                return ctx.tok("ac")

            emit_mm(
                lhsT_of=lambda k, mi: sb["wib"][:, k * IB + mi * 128:
                                                k * IB + mi * 128 + 128],
                rhs_of=lambda k, ni: sb["xnTm"][:, k * MT2 + ni * 512:
                                               k * MT2 + ni * 512 + 512],
                n_m=IB // 128, n_n=MT2 // 512, n_k=DT, evac=ev_int,
                deps=[t_w, t_xnTm])
            t_int = ctx.tok("ac")

            def ev_out(mi, ni, p, tok, e=e):
                @blk.vector
                def _(v, mi=mi, ni=ni, p=p, tok=tok, e=e):
                    ctx.wait(v, "vector", tok, t_wrt, t_accz)
                    ctx.inc(v.scalar_tensor_tensor(
                        sb["acc"][:, mi * D + ni * 512:
                                  mi * D + ni * 512 + 512],
                        p[:], sb["wrt"][:, mi * E + e: mi * E + e + 1],
                        sb["acc"][:, mi * D + ni * 512:
                                  mi * D + ni * 512 + 512],
                        ALU.mult, ALU.add), "ve")
                return ctx.tok("ve")

            if dbg and e == 0 and ib == 0:
                @blk.sync
                def _(s, t_int=t_int):
                    ctx.wait(s, "sync", t_int)
                    ctx.inc(s.dma_start(out=ap["dbg_int"], in_=sb["intT"][:]),
                            "ds", 16)

            emit_mm(
                lhsT_of=lambda k, mi: sb["intT"][:, k * MT2 + mi * 128:
                                                 k * MT2 + mi * 128 + 128],
                rhs_of=lambda k, ni: sb["wob"][:, k * D + ni * 512:
                                               k * D + ni * 512 + 512],
                n_m=MT2 // 128, n_n=D // 512, n_k=IB // 128, evac=ev_out,
                deps=[t_int])
            t_int_used = ctx.tok("pe")
    t_acc_last = ctx.tok("ve")

    @blk.sync
    def _(s):
        ctx.wait(s, "sync", t_acc_last)
        ctx.inc(s.dma_start(
            out=priv_moe.rearrange("(a p) o -> p a o", p=128),
            in_=sb["acc"][:].rearrange("p (a o) -> p a o", a=16)), "ds", 16)
    t_moe_priv = ctx.tok("ds")

    @blk.gpsimd
    def _(g):
        ctx.wait(g, "gpsimd", ctx.tok("pe"))
        for nm, dst in [("wqT", "wq"), ("wkT", "wk"), ("wvT", "wv"),
                        ("woT", "wo")]:
            ctx.inc(g.dma_start(
                out=sb[dst][:].rearrange("p (a o) -> p a o", a=DT),
                in_=ap[nm].rearrange("(a p) o -> p a o", p=128)), "dg", 16)
    t_wload = ctx.tok("dg")

    @blk.gpsimd
    def _(g):
        from concourse.ap import AP as _AP
        ctx.wait(g, "gpsimd", t_moe_priv, t_m_all, t_cload0)
        slot = g.value_load(sb["cft"][0:1, 0:1])
        dstM = shmM_t.ap().rearrange("a n -> (a n)")[0:512 * D]
        ctx.inc(g.dma_start(
            out=_AP(dstM.tensor, slot * (512 * D), dstM.ap),
            in_=privM_t.ap().rearrange("a n -> (a n)")), "dg", 16)
        dstK = shmK_t.ap().rearrange("a n -> (a n)")[0:MT2 * D]
        ctx.inc(g.dma_start(
            out=_AP(dstK.tensor, slot * (MT2 * D), dstK.ap),
            in_=privK_t.ap().rearrange("a n -> (a n)")), "dg", 16)
        t_pay = ctx.tok("dg")
        ctx.wait(g, "gpsimd", t_pay)
        dstF = shf[0:1, 0:1]
        ctx.inc(g.dma_start(
            out=_AP(dstF.tensor, slot * 16, dstF.ap),
            in_=sb["cft"][0:1, 1:2]), "dg", 16)
    t_flag = ctx.tok("dg")

    for b, (x_in, y_out) in enumerate([(ap["x0"], sc_ya),
                                       (ap["x1"], sc_yb)]):
        dep_A = [t_int_used, t_moe_priv] if b == 0 else ctx.all_toks()
        if b == 1:
            @blk.gpsimd
            def _(g):
                ctx.wait(g, "gpsimd", *dep_A)
                for nm, dst in [("wqT", "wq"), ("wkT", "wk"), ("wvT", "wv")]:
                    ctx.inc(g.dma_start(
                        out=sb[dst][:].rearrange("p (a o) -> p a o", a=DT),
                        in_=ap[nm].rearrange("(a p) o -> p a o", p=128)),
                        "dg", 16)
            dep_A = [ctx.tok("dg")]

        x_v = x_in.rearrange("(j p) d -> p j d", p=128)
        t_prev = ctx.tok("ac")
        for w in range(S // 512):
            t_ln = emit_ln_wave(x_v[:, 4 * w:4 * w + 4, :], [t_prev] + dep_A)
            t_prev = emit_transpose_wave(sb["xnw"], 0, 4, sb["xnT"], S,
                                         w * 512, [t_ln])
        t_xnT = t_prev
        if dbg and b == 0:
            @blk.sync
            def _(s):
                ctx.wait(s, "sync", t_xnT)
                ctx.inc(s.dma_start(out=ap["dbg_xnT"], in_=sb["xnT"][:]),
                        "ds", 16)

        stg_free = [ctx.tok("ds")] * 4
        stg_i = [0]

        def ev_spill(col_of):
            def ev(mi, ni, p, tok):
                sl = stg_i[0] % 4
                stg_i[0] += 1

                @blk.scalar
                def _(sc, sl=sl, p=p, tok=tok):
                    ctx.wait(sc, "scalar", tok, stg_free[sl])
                    ctx.inc(sc.copy(sb["stg"][:, ts(sl, 512)], p[:]), "ac")
                t_e = ctx.tok("ac")

                @blk.sync
                def _(s, sl=sl, mi=mi, ni=ni, t_e=t_e):
                    ctx.wait(s, "sync", t_e)
                    ctx.inc(s.dma_start(out=col_of(mi, ni),
                                        in_=sb["stg"][:, ts(sl, 512)]),
                            "ds", 16)
                stg_free[sl] = ctx.tok("ds")
                return t_e
            return ev

        for wname, dst in [("wk", sc_k), ("wq", sc_q)]:
            emit_mm(
                lhsT_of=lambda k, mi, wname=wname: sb[wname][
                    :, k * D + mi * 128: k * D + mi * 128 + 128],
                rhs_of=lambda k, ni: sb["xnT"][:, k * S + ni * 512:
                                               k * S + ni * 512 + 512],
                n_m=DT, n_n=S // 512, n_k=DT,
                evac=ev_spill(lambda mi, ni, dst=dst: dst[
                    :, mi * S + ni * 512: mi * S + ni * 512 + 512]),
                deps=[t_xnT])
        emit_mm(
            lhsT_of=lambda k, mi: sb["xnT"][:, k * S + mi * 128:
                                            k * S + mi * 128 + 128],
            rhs_of=lambda k, ni: sb["wv"][:, k * D + ni * 512:
                                          k * D + ni * 512 + 512],
            n_m=TT, n_n=D // 512, n_k=DT,
            evac=ev_spill(lambda mi, ni: sc_v[ts(mi),
                                              ni * 512:(ni + 1) * 512]),
            deps=[t_xnT])
        t_proj_pe = ctx.tok("pe")
        t_proj_st = ctx.tok("ds")

        @blk.sync
        def _(s):
            ctx.wait(s, "sync", t_proj_pe, t_proj_st, ctx.tok("ve"),
                     ctx.tok("ac"))
            ctx.inc(s.dma_start(out=sb["ktA"][:], in_=sc_k), "ds", 16)
            ctx.inc(s.dma_start(
                out=sb["vt"][:].rearrange("p (a o) -> p a o", a=TT),
                in_=sc_v.rearrange("(a p) o -> p a o", p=128)), "ds", 16)
        t_kv = ctx.tok("ds")

        qb_bufs = [sb["qblk"], sb["qblk2"]]
        qb_tok = [("ds", 0), ("ds", 0)]
        qb_used = [("pe", 0), ("pe", 0)]
        t_exp_prev = ("ac", 0)
        t_prowread_prev = ("pe", 0)
        t_y1_ve = ("ve", 0)
        t_y1_st = ("ds", 0)
        for qb in range(S // 512):
            bi = qb % 2

            @blk.sync
            def _(s, qb=qb, bi=bi):
                ctx.wait(s, "sync", t_proj_st, qb_used[bi], t_kv)
                ctx.inc(s.dma_start(
                    out=qb_bufs[bi][:].rearrange("p (a t) -> p a t", a=DT),
                    in_=sc_q.rearrange("p (a t) -> p a t", a=DT)
                    [:, :, qb * 512:(qb + 1) * 512]), "ds", 16)
            qb_tok[bi] = ctx.tok("ds")

            for qq in range(4):
                qt = qb * 4 + qq
                nkb = qt // 4 + 1
                par = qt % 2
                cF = 16 if par == 0 else 20
                for kb2 in range(nkb):
                    p = p512a if kb2 % 2 == 0 else p512b

                    @blk.tensor
                    def _(t, qq=qq, kb2=kb2, p=p, bi=bi):
                        ctx.wait(t, "tensor", qb_tok[bi], t_kv, pfree(p))
                        for k in range(DT):
                            last = t.matmul(
                                p[:],
                                qb_bufs[bi][:, k * 512 + qq * 128:
                                            k * 512 + qq * 128 + 128],
                                sb["ktA"][:, k * S + kb2 * 512:
                                          k * S + kb2 * 512 + 512],
                                start=(k == 0), stop=(k == DT - 1))
                        ctx.inc(last, "pe")
                    t_sc = ctx.tok("pe")

                    @blk.vector
                    def _(v, qt=qt, kb2=kb2, p=p, t_sc=t_sc, nkb=nkb):
                        ctx.wait(v, "vector", t_sc, t_cload, t_exp_prev)
                        if kb2 == nkb - 1:
                            v.tensor_tensor(sb["scrow"][:, ts(kb2, 512)],
                                            p[:],
                                            sb["mask"][:, ts(qt % 4, 512)],
                                            ALU.add)
                        else:
                            v.tensor_copy(sb["scrow"][:, ts(kb2, 512)], p[:])
                        ctx.inc(v.reduce_max(
                            sb["red"][:, kb2:kb2 + 1],
                            p[:], axis=AX.X), "ve")
                    set_pfree(p, ctx.tok("ve"))

                @blk.vector
                def _(v, nkb=nkb, cF=cF):
                    v.drain()
                    if nkb > 1:
                        v.reduce_max(sb["red"][:, cF:cF + 1],
                                     sb["red"][:, 0:nkb], axis=AX.X)
                    else:
                        v.tensor_copy(sb["red"][:, cF:cF + 1],
                                      sb["red"][:, 0:1])
                    v.drain()
                    ctx.inc(v.tensor_scalar(sb["red"][:, cF + 1:cF + 2],
                                            sb["red"][:, cF:cF + 1],
                                            -1.0 / 32, None, ALU.mult), "ve")
                t_max = ctx.tok("ve")

                @blk.scalar
                def _(sc, nkb=nkb, cF=cF, t_max=t_max,
                      t_prowread_prev=t_prowread_prev):
                    ctx.wait(sc, "scalar", t_max, t_prowread_prev)
                    cP = 32 + 8 * (cF == 20)
                    for kb2 in range(nkb):
                        sc.activation(
                            sb["prow"][:, ts(kb2, 512)],
                            sb["scrow"][:, ts(kb2, 512)], AF.Exp,
                            bias=sb["red"][:, cF + 1:cF + 2],
                            scale=cstc(C_INV32),
                            accum_out=sb["red"][:, 24 + kb2:25 + kb2])
                    sc.drain()
                    ctx.inc(sc.copy(sb["red"][:, cP:cP + nkb],
                                    sb["red"][:, 24:24 + nkb]), "ac")
                t_exp_prev = ctx.tok("ac")
                t_exp = t_exp_prev

                @blk.vector
                def _(v, nkb=nkb, cF=cF, t_exp=t_exp):
                    ctx.wait(v, "vector", t_exp)
                    cP = 32 + 8 * (cF == 20)
                    v.reduce_sum(sb["red"][:, cF + 2:cF + 3],
                                 sb["red"][:, cP:cP + nkb], axis=AX.X)
                    v.drain()
                    ctx.inc(v.reciprocal(sb["red"][:, cF + 3:cF + 4],
                                         sb["red"][:, cF + 2:cF + 3]), "ve")
                t_linv = ctx.tok("ve")

                nkt = nkb * 4
                for g0 in range(0, nkt, 4):
                    p = pTa if (g0 // 4) % 2 == 0 else pTb
                    sl0 = ((g0 // 4) % 2) * 4

                    @blk.tensor
                    def _(t, g0=g0, p=p, t_exp=t_exp, nkt=nkt):
                        ctx.wait(t, "tensor", t_exp, pfree(p))
                        for q in range(min(4, nkt - g0)):
                            last = t.transpose(p[:, ts(q)],
                                               sb["prow"][:, ts(g0 + q)],
                                               sb["idn"][:])
                        ctx.inc(last, "pe")
                    t_tr = ctx.tok("pe")

                    @blk.scalar
                    def _(sc, g0=g0, p=p, t_tr=t_tr, sl0=sl0, nkt=nkt):
                        ctx.wait(sc, "scalar", t_tr)
                        for q in range(min(4, nkt - g0)):
                            last = sc.copy(sb["pt"][:, ts(sl0 + q)],
                                           p[:, ts(q)])
                        ctx.inc(last, "ac")
                    t_pt = ctx.tok("ac")
                    set_pfree(p, t_pt)

                    @blk.tensor
                    def _(t, g0=g0, t_pt=t_pt, sl0=sl0, nkt=nkt):
                        ctx.wait(t, "tensor", t_pt,
                                 *([pfree(pG), pfree(pH)] if g0 == 0
                                   else [("pe", 0)]))
                        for q in range(min(4, nkt - g0)):
                            kt2 = g0 + q
                            for oh in range(2):
                                last = t.matmul(
                                    (pG, pH)[oh][:],
                                    sb["pt"][:, ts(sl0 + q)],
                                    sb["vt"][:, kt2 * D + oh * 512:
                                             kt2 * D + oh * 512 + 512],
                                    start=(kt2 == 0), stop=(kt2 == nkt - 1))
                        ctx.inc(last, "pe")
                t_pv = ctx.tok("pe")
                t_prowread_prev = t_pv

                @blk.scalar
                def _(sc, t_pv=t_pv, t_linv=t_linv, cF=cF):
                    ctx.wait(sc, "scalar", t_pv, t_linv)
                    sc.activation(sb["ctxq"][:, 0:512], pG[:], AF.Copy,
                                  scale=sb["red"][:, cF + 3:cF + 4])
                    ctx.inc(sc.activation(
                        sb["ctxq"][:, 512:1024], pH[:], AF.Copy,
                        scale=sb["red"][:, cF + 3:cF + 4]), "ac")
                set_pfree(pG, ctx.tok("ac"))
                set_pfree(pH, ctx.tok("ac"))
                t_ctx = ctx.tok("ac")
                if dbg and b == 0 and qt == 31:
                    @blk.sync
                    def _(s, t_exp=t_exp, t_ctx=t_ctx):
                        ctx.wait(s, "sync", t_exp, t_ctx)
                        ctx.inc(s.dma_start(out=ap["dbg_scrow"],
                                            in_=sb["scrow"][:]), "ds", 16)
                        ctx.inc(s.dma_start(out=ap["dbg_prow"],
                                            in_=sb["prow"][:]), "ds", 16)
                        ctx.inc(s.dma_start(out=ap["dbg_ctx"],
                                            in_=sb["ctxq"][:]), "ds", 16)
                t_ctxT = emit_transpose_wave(sb["ctxq"], 0, 1, sb["ctxT"],
                                             128, 0, [t_ctx])

                @blk.sync
                def _(s, qt=qt, t_y1_ve=t_y1_ve):
                    ctx.wait(s, "sync", t_y1_ve)
                    ctx.inc(s.dma_start(out=sb["xres"][:],
                                        in_=x_in[ts(qt), :]), "ds", 16)
                t_xr = ctx.tok("ds")

                for oh in range(2):
                    p = p512a if oh == 0 else p512b

                    @blk.tensor
                    def _(t, oh=oh, p=p, t_ctxT=t_ctxT):
                        ctx.wait(t, "tensor", t_ctxT, pfree(p))
                        for k in range(DT):
                            last = t.matmul(
                                p[:], sb["ctxT"][:, ts(k)],
                                sb["wo"][:, k * D + oh * 512:
                                         k * D + oh * 512 + 512],
                                start=(k == 0), stop=(k == DT - 1))
                        ctx.inc(last, "pe")
                    t_o = ctx.tok("pe")

                    @blk.vector
                    def _(v, oh=oh, p=p, t_o=t_o, t_xr=t_xr,
                          t_y1_st=t_y1_st):
                        ctx.wait(v, "vector", t_o, t_xr, t_y1_st)
                        ctx.inc(v.tensor_tensor(
                            sb["y1s"][:, ts(oh, 512)], p[:],
                            sb["xres"][:, ts(oh, 512)], ALU.add), "ve")
                    set_pfree(p, ctx.tok("ve"))
                t_y1_ve = ctx.tok("ve")

                @blk.sync
                def _(s, qt=qt, t_y1_ve=t_y1_ve):
                    ctx.wait(s, "sync", t_y1_ve)
                    ctx.inc(s.dma_start(out=y_out[ts(qt), :],
                                        in_=sb["y1s"][:]), "ds", 16)
                t_y1_st = ctx.tok("ds")
            qb_used[bi] = ctx.tok("pe")

    barrier_toks = ctx.all_toks()

    @blk.gpsimd
    def _(g):
        ctx.wait(g, "gpsimd", *barrier_toks)
        g.memset(sb["flt"][:], 0)
        g.memset(sb["flt"][0:1, 3:4], 1)
        dreg = g.to_reg(g.value_load(sb["flt"][0:1, 3:4]))
        for i in range(NPOLLS):
            with g.If_cmp(dreg, 0, comp_op="IS_NE"):
                g.memset(sb["brs"][:], 0.0)
                g.memset(sb["brs"][:], 0.0)
                g.dma_start(out=sb["flt"][0:1, 0:1], in_=shf[0:1, 0:1]
                            ).then_inc(pf, 16)
                g.dma_start(out=sb["flt"][0:1, 1:2], in_=shf[1:2, 0:1]
                            ).then_inc(pf, 16)
                g.wait_ge(pf, 32 * (i + 1))
                g.tensor_tensor(sb["flt"][0:1, 2:3], sb["flt"][0:1, 0:1],
                                sb["flt"][0:1, 1:2], ALU.add)
                g.memset(sb["tiny"][:], 0.0)
                g.tensor_tensor(sb["flt"][0:1, 3:4], sb["flt"][0:1, 2:3],
                                sb["cft"][0:1, 3:4], ALU.subtract)
                g.memset(sb["tiny"][:], 0.0)
                g.reg_load(dreg, sb["flt"][0:1, 3:4])
        with g.If_cmp(dreg, 0, comp_op="IS_EQ"):
            g.memset(sb["tiny"][:], 1.0).then_inc(px, 1)
        g.wait_ge(px, 1)

    for eng in ["tensor", "scalar", "vector", "sync"]:
        @blk.__getattribute__(eng)
        def _(e, eng=eng):
            ctx.wait(e, eng, *barrier_toks)

    @blk.gpsimd
    def _(g):
        for nm, dst in [("qw", "qw_s"), ("kw", "kw_s"),
                        ("ow", "ow_s")]:
            ctx.inc(g.dma_start(
                out=sb[dst][:].rearrange("p (a o) -> p a o", a=DT),
                in_=ap[nm].rearrange("(a p) o -> p a o", p=128)), "dg", 16)
        ctx.inc(g.dma_start(
            out=sb["bw_s"][:].rearrange("p (a o) -> p a o", a=DT),
            in_=ap["bw"].rearrange("(a p) o -> p a o", p=128)), "dg", 16)
        ctx.inc(g.dma_start(
            out=sb["w2_s"][:].rearrange("p (a o) -> p a o", a=DT),
            in_=ap["w2"].rearrange("(a p) o -> p a o", p=128)), "dg", 16)
        for nm in ["w1m", "w1s", "w1a", "w1l"]:
            ctx.inc(g.dma_start(out=sb[nm + "_s"][:], in_=ap[nm]), "dg", 16)
        ctx.inc(g.dma_start(out=sb["msk"][:], in_=ap["masksb"]), "dg", 16)
    t_wload = ctx.tok("dg")

    @blk.sync
    def _(s):
        for nm in ["qcw", "kcw", "vcw"]:
            ctx.inc(s.dma_start(
                out=sb[nm + "_s"][:].rearrange("p (a o) -> p a o", a=DT),
                in_=ap[nm].rearrange("(a p) o -> p a o", p=128)), "ds", 16)
        ctx.inc(s.dma_start(
            out=sb["fir_s"][:].rearrange("p (a o) -> p a o", a=DT),
            in_=ap["firw"].rearrange("(a p) o -> p a o", p=128)), "ds", 16)
        ctx.inc(s.dma_start(out=sb["b1_s"][:], in_=ap["b1"]), "ds", 16)
        ctx.inc(s.dma_start(out=sb["onw_s"][:], in_=ap["onw"]), "ds", 16)
        ctx.inc(s.dma_start(out=sb["red"][0:1, 0:24], in_=ap["b2"]), "ds", 16)
        ctx.inc(s.dma_start(out=sb["glt_r"][0:1, 0:H], in_=ap["glt"]),
                "ds", 16)
    t_cload = ctx.tok("ds")

    @blk.gpsimd
    def _(g):
        ctx.inc(g.dma_start(out=sb["fb_b"][0:1, :], in_=ap["fbias"]),
                "dg", 16)
    t_fbrow = ctx.tok("dg")

    t_ones = t_ones0

    @blk.vector
    def _(v):
        ctx.wait(v, "vector", t_cload)
        g = sb["glt_r"]
        v.scalar_tensor_tensor(g[0:1, 4:8], g[0:1, 0:H], -1.0, g[0:1, 0:H],
                               ALU.mult, ALU.max)
        ctx.inc(v.tensor_scalar_max(g[0:1, 8:12], g[0:1, 0:H], 0.0), "ve")
    t_g1 = ctx.tok("ve")

    @blk.scalar
    def _(sc):
        ctx.wait(sc, "scalar", t_g1)
        g = sb["glt_r"]
        sc.activation(g[0:1, 12:16], g[0:1, 4:8], AF.Exp,
                      scale=sb["cst"][0:1, LC_NEG1:LC_NEG1 + 1])
        sc.drain()
        ctx.inc(sc.activation(g[0:1, 16:20], g[0:1, 12:16], AF.Ln,
                              bias=sb["cst"][0:1, LC_ONE:LC_ONE + 1]), "ac")
    t_g2 = ctx.tok("ac")

    @blk.vector
    def _(v):
        ctx.wait(v, "vector", t_g2)
        g = sb["glt_r"]
        v.tensor_add(g[0:1, 20:24], g[0:1, 8:12], g[0:1, 16:20])
        v.drain()
        v.tensor_scalar_add(g[0:1, 20:24], g[0:1, 20:24], 0.5)
        v.drain()
        v.reciprocal(g[0:1, 24:28], g[0:1, 20:24])
        v.drain()
        ctx.inc(v.tensor_copy(
            g[0:1, 0:24].rearrange("q (a b) -> q a b", a=4),
            g[0:1, 24:28].unsqueeze(2).broadcast_to([1, 4, 6])),
            "ve")
    t_it = ctx.tok("ve")

    def emit_bcast_row_f32(row_ap, dst_ap, n, deps):
        @blk.tensor
        def _(t):
            ctx.wait(t, "tensor", t_ones, pfree(p512b), *deps)
            ctx.inc(t.matmul(p512b[:, 0:n], sb["ones1f"][:], row_ap,
                             start=True, stop=True), "pe")
        tk = ctx.tok("pe")

        @blk.scalar
        def _(sc):
            ctx.wait(sc, "scalar", tk)
            ctx.inc(sc.copy(dst_ap, p512b[:, 0:n]), "ac")
        set_pfree(p512b, ctx.tok("ac"))
        return ctx.tok("ac")

    emit_bcast_row_f32(sb["red"][0:1, 0:24], sb["b2_b"][:], 24, [t_cload])
    emit_bcast_row_f32(sb["glt_r"][0:1, 0:24], sb["it_b"][:], 24, [t_it])
    def emit_bcast_row_bf(row_ap, dst_ap, n, deps):
        @blk.tensor
        def _(t):
            ctx.wait(t, "tensor", t_ones, pfree(p512b), *deps)
            ctx.inc(t.matmul(p512b[:, 0:n], sb["ones1"][:], row_ap,
                             start=True, stop=True), "pe")
        tk = ctx.tok("pe")

        @blk.scalar
        def _(sc):
            ctx.wait(sc, "scalar", tk)
            ctx.inc(sc.copy(dst_ap, p512b[:, 0:n]), "ac")
        set_pfree(p512b, ctx.tok("ac"))
        return ctx.tok("ac")

    emit_bcast_row_bf(sb["fb_b"][0:1, 0:512], sb["fb_b"][:, 0:512], 512,
                      [t_fbrow])
    t_pre = emit_bcast_row_bf(sb["fb_b"][0:1, 512:1024],
                              sb["fb_b"][:, 512:1024], 512, [t_fbrow])

    @blk.gpsimd
    def _(g):
        ctx.inc(g.dma_start(
            out=sb["M_s"][:].rearrange("p (a o) -> p a o", a=DT),
            in_=shm_M.rearrange("(a p) o -> p a o", p=128)), "dg", 16)
    t_Ms = ctx.tok("dg")
    t_wload2 = t_wload
    t_ml_prev = ("ve", 0)
    for half in range(2):
        @blk.sync
        def _(s, half=half, t_ml_prev=t_ml_prev):
            if half == 0:
                s.wait_ge(px, 1)
            ctx.wait(s, "sync", t_ml_prev)
            ctx.inc(s.dma_start(
                out=sb["Mstg"][:].rearrange("p (a o) -> p a o", a=DT // 2),
                in_=shm_M.rearrange("(a p) o -> p a o", p=128)
                [:, half * 4:half * 4 + 4, :]), "ds", 16)
        t_mst = ctx.tok("ds")

        @blk.vector
        def _(v, half=half, t_mst=t_mst):
            ctx.wait(v, "vector", t_mst, t_Ms)
            for a4 in range(DT // 2):
                a = half * 4 + a4
                last = v.tensor_sub(sb["M_l"][:, a * D:(a + 1) * D],
                                    sb["Mstg"][:, a4 * D:(a4 + 1) * D],
                                    sb["M_s"][:, a * D:(a + 1) * D])
            ctx.inc(last, "ve")
        t_ml_prev = ctx.tok("ve")
    t_M = t_ml_prev
    if dbg:
        @blk.sync
        def _(s):
            ctx.wait(s, "sync", t_M)
            ctx.inc(s.dma_start(out=ap["dbg_M"], in_=sb["M_s"][:]), "ds", 16)

    def emit_head_bcast(src_rows, h, c0, w, dst_bf, deps, copy_rows=True):
        if copy_rows:
            @blk.vector
            def _(v):
                ctx.wait(v, "vector", *deps,
                         state.get("t_rowbf_used", ("pe", 0)))
                v.drain()
                ctx.inc(v.tensor_copy(sb["rowbf"][:], src_rows), "ve")
            state["t_rowbf"] = ctx.tok("ve")
        tk0 = state["t_rowbf"]

        @blk.tensor
        def _(t):
            ctx.wait(t, "tensor", tk0, pfree(p512b), t_wload, *deps)
            ctx.inc(t.matmul(p512b[:, 0:w],
                             msk(SEL + h * 128, 128)[0:4, :],
                             sb["rowbf"][0:4, c0:c0 + w],
                             start=True, stop=True), "pe")
        tk = ctx.tok("pe")
        state["t_rowbf_used"] = tk

        @blk.scalar
        def _(sc):
            ctx.wait(sc, "scalar", tk)
            ctx.inc(sc.copy(dst_bf, p512b[:, 0:w]), "ac")
        set_pfree(p512b, ctx.tok("ac"))
        return ctx.tok("ac")

    t_alf = t_cload0
    for b in range(2):
        x_in = sc_ya if b == 0 else sc_yb
        out_t = ap["outa"] if b == 0 else ap["outb"]
        barrier = ctx.all_toks()

        @blk.vector
        def _(v):
            ctx.wait(v, "vector", *barrier)
            v.memset(sb["tls"][:], 0.0)
            ctx.inc(v.memset(sb["vtb"][:].rearrange(
                "p (a o) -> p a o", a=DT)[:, :, 0:32], 0.0), "ve")
        t_tails = ctx.tok("ve")
        state["t_beta_used"] = t_tails

        for ib in range(NBLK):
            t0 = ib * TB

            for j in range(TB // 128):
                tt = t0 // 128 + j

                @blk.sync
                def _(s, tt=tt):
                    ctx.wait(s, "sync", state.get("t_x1_used", ("ve", 0)),
                             t_M)
                    ctx.inc(s.dma_start(out=sb["x1t"][:],
                                        in_=x_in[ts(tt), :]), "ds", 16)
                    if b == 0:
                        ctx.inc(s.dma_start(out=sb["x2t"][:],
                                            in_=shm_moe[ts(tt), :]),
                                "ds", 16)
                t_x = ctx.tok("ds")

                @blk.vector
                def _(v, t_x=t_x):
                    ctx.wait(v, "vector", t_x, t_tails, t_alf)
                    if b == 0:
                        v.scalar_tensor_tensor(
                            sb["x1t"][:], sb["x2t"][:], sb["alf"][:, 0:1],
                            sb["x1t"][:], ALU.mult, ALU.add)
                        v.drain()
                    v.tensor_scalar(sb["x2t"][:], sb["x1t"][:], 1.0, 0.0,
                                    ALU.mult, ALU.add,
                                    accum_out=sb["red"][:, 8:9])
                    ctx.inc(v.drain(), "ve")
                t_s1 = ctx.tok("ve")

                @blk.scalar
                def _(sc, t_s1=t_s1):
                    ctx.wait(sc, "scalar", t_s1, t_cload)
                    sc.activation(sb["x2t"][:], sb["x1t"][:], AF.Square,
                                  accum_out=sb["red"][:, 0:1])
                    sc.drain()
                    ctx.inc(sc.copy(sb["red"][:, 4:5], sb["red"][:, 0:1]),
                            "ac")
                t_s2 = ctx.tok("ac")

                @blk.vector
                def _(v, t_s2=t_s2):
                    ctx.wait(v, "vector", t_s2)
                    v.tensor_scalar(sb["red"][:, 16:17], sb["red"][:, 8:9],
                                    1.0 / D, None, ALU.mult)
                    v.drain()
                    v.tensor_tensor(sb["red"][:, 24:25], sb["red"][:, 16:17],
                                    sb["red"][:, 16:17], ALU.mult)
                    v.tensor_scalar(sb["red"][:, 4:5], sb["red"][:, 4:5],
                                    1.0 / D, 1e-5, ALU.mult, ALU.add)
                    v.drain()
                    ctx.inc(v.tensor_sub(sb["red"][:, 24:25],
                                         sb["red"][:, 4:5],
                                         sb["red"][:, 24:25]), "ve")
                t_var = ctx.tok("ve")

                @blk.scalar
                def _(sc, t_var=t_var):
                    ctx.wait(sc, "scalar", t_var)
                    ctx.inc(sc.activation(sb["red"][:, 32:33],
                                          sb["red"][:, 24:25], AF.Sqrt), "ac")
                t_sd = ctx.tok("ac")

                @blk.vector
                def _(v, t_sd=t_sd):
                    ctx.wait(v, "vector", t_sd,
                             state.get("t_bcv_used", ("pe", 0)))
                    v.reciprocal(sb["red"][:, 33:34], sb["red"][:, 32:33])
                    v.drain()
                    v.tensor_scalar(sb["bcv"][:], sb["x1t"][:],
                                    sb["red"][:, 16:17],
                                    sb["red"][:, 33:34],
                                    ALU.subtract, ALU.mult)
                    ctx.inc(v.drain(), "ve")
                t_xn = ctx.tok("ve")
                state["t_x1_used"] = t_xn

                for half in range(2):
                    @blk.tensor
                    def _(t, j=j, half=half, t_xn=t_xn):
                        ctx.wait(t, "tensor", t_xn, pfree(pT), t_wload,
                                 state.get("t_xnT_used", ("ac", 0)))
                        for q in range(4):
                            a = half * 4 + q
                            last = t.transpose(pT[:, ts(q)],
                                               sb["bcv"][:, ts(a)],
                                               sb["idn"][:])
                        ctx.inc(last, "pe")
                    t_tr = ctx.tok("pe")
                    state["t_bcv_used"] = t_tr

                    @blk.scalar
                    def _(sc, j=j, half=half, t_tr=t_tr):
                        ctx.wait(sc, "scalar", t_tr)
                        for q in range(4):
                            a = half * 4 + q
                            last = sc.copy(
                                sb["xn1T"][:, a * TB + j * 128:
                                           a * TB + j * 128 + 128],
                                pT[:, ts(q)])
                        ctx.inc(last, "ac")
                    set_pfree(pT, ctx.tok("ac"))
            t_xn1T = ctx.tok("ac")

            @blk.gpsimd
            def _(g):
                ctx.wait(g, "gpsimd", state.get("t_vwh_used", ("pe", 0)))
                ctx.inc(g.dma_start(
                    out=sb["vwh"][:].rearrange("p (a o) -> p a o", a=DT),
                    in_=ap["vw"].rearrange("(a p) o -> p a o", p=128)),
                    "dg", 16)
            t_vw_ld = ctx.tok("dg")

            def emit_proj_conv(w_s, cw_s, dst, dst_off, dstW, deps, tix=0):
                for mi in range(DT):
                    p = p512a if mi % 2 == 0 else p512b

                    @blk.tensor
                    def _(t, mi=mi, p=p):
                        ctx.wait(t, "tensor", t_wload2, pfree(p), *deps)
                        for k in range(DT):
                            last = t.matmul(
                                p[:, 0:TB],
                                sb[w_s][:, k * D + mi * 128:
                                        k * D + mi * 128 + 128],
                                sb["xn1T"][:, k * TB:(k + 1) * TB],
                                start=(k == 0), stop=(k == DT - 1))
                        ctx.inc(last, "pe")
                    tk = ctx.tok("pe")

                    @blk.scalar
                    def _(sc, mi=mi, p=p, tk=tk):
                        ctx.wait(sc, "scalar", tk,
                                 state.get("t_raw_used", ("ve", 0)))
                        ctx.inc(sc.copy(
                            sb["raw"][:, mi * (TB + 4) + 3:
                                      mi * (TB + 4) + 3 + TB], p[:, 0:TB]),
                            "ac")
                    set_pfree(p, ctx.tok("ac"))
                t_raw = ctx.tok("ac")

                @blk.vector
                def _(v, t_raw=t_raw, tix=tix):
                    ctx.wait(v, "vector", t_raw, t_tails,
                             state.get("t_sil_used", ("ac", 0)))
                    for a in range(DT):
                        rb = a * (TB + 4)
                        v.tensor_copy(sb["raw"][:, rb:rb + 3],
                                      sb["tls"][:, (tix * DT + a) * 3:
                                                (tix * DT + a) * 3 + 3])
                    v.drain()
                    for tap in range(4):
                        for a in range(DT):
                            rb = a * (TB + 4)
                            src = sb["raw"][:, rb + tap: rb + tap + TB]
                            wcol = sb[cw_s][:, a * 4 + tap: a * 4 + tap + 1]
                            if tap == 0:
                                v.tensor_scalar(
                                    dst[:, a * dstW + dst_off:
                                        a * dstW + dst_off + TB],
                                    src, wcol, None, ALU.mult)
                            else:
                                v.scalar_tensor_tensor(
                                    dst[:, a * dstW + dst_off:
                                        a * dstW + dst_off + TB],
                                    src, wcol,
                                    dst[:, a * dstW + dst_off:
                                        a * dstW + dst_off + TB],
                                    ALU.mult, ALU.add)
                    v.drain()
                    for a in range(DT):
                        rb = a * (TB + 4)
                        v.tensor_copy(sb["tls"][:, (tix * DT + a) * 3:
                                                (tix * DT + a) * 3 + 3],
                                      sb["raw"][:, rb + TB:rb + TB + 3])
                    ctx.inc(v.drain(), "ve")
                t_conv = ctx.tok("ve")
                state["t_raw_used"] = t_conv

                @blk.scalar
                def _(sc, t_conv=t_conv):
                    ctx.wait(sc, "scalar", t_conv)
                    for a in range(DT):
                        last = sc.activation(
                            dst[:, a * dstW + dst_off:
                                a * dstW + dst_off + TB],
                            dst[:, a * dstW + dst_off:
                                a * dstW + dst_off + TB], AF.Silu)
                    ctx.inc(last, "ac")
                state["t_sil_used"] = ctx.tok("ac")
                return ctx.tok("ac")

            t_q = emit_proj_conv("qw_s", "qcw_s", sb["qn"][:], 0, TB,
                                 [t_xn1T, state.get("t_qn_used", ("pe", 0))],
                                 tix=0)
            t_k = emit_proj_conv("kw_s", "kcw_s", sb["kn"][:], 0, TB,
                                 [t_xn1T, state.get("t_kn_used", ("pe", 0))],
                                 tix=1)
            t_v = emit_proj_conv("vwh", "vcw_s", sb["vtb"][:], 32, TB + 32,
                                 [t_xn1T, t_vw_ld,
                                  state.get("t_vt_used", ("pe", 0)),
                                  state.get("t_vtl", ("ve", 0))], tix=2)

            @blk.gpsimd
            def _(g, t_v=t_v):
                ctx.wait(g, "gpsimd", ctx.tok("pe"))
                ctx.inc(g.dma_start(
                    out=sb["vwh"][:].rearrange("p (a o) -> p a o", a=DT),
                    in_=ap["w1h"].rearrange("(a p) o -> p a o", p=128)),
                    "dg", 16)
            t_w1h_ld = ctx.tok("dg")

            @blk.tensor
            def _(t):
                ctx.wait(t, "tensor", t_xn1T, pfree(pE), t_wload,
                         state.get("t_beta_used", ("ve", 0)))
                for k in range(DT):
                    last = t.matmul(pE[0:H, 0:TB],
                                    sb["bw_s"][:, ts(k, H)],
                                    sb["xn1T"][:, k * TB:(k + 1) * TB],
                                    start=(k == 0), stop=(k == DT - 1))
                ctx.inc(last, "pe")
            t_bl = ctx.tok("pe")

            @blk.scalar
            def _(sc, t_bl=t_bl):
                ctx.wait(sc, "scalar", t_bl)
                ctx.inc(sc.activation(sb["beta"][:], pE[0:H, 0:TB],
                                      AF.Sigmoid), "ac")
            set_pfree(pE, ctx.tok("ac"))
            t_beta = ctx.tok("ac")

            def emit_l2fac(src, dst_rows, tok_in):
                for a in range(DT):
                    sqb = sb["sq1"] if a % 2 == 0 else sb["sq2"]

                    @blk.scalar
                    def _(sc, a=a, sqb=sqb):
                        ctx.wait(sc, "scalar", tok_in,
                                 state.get(f"t_sqf{a % 2}", ("pe", 0)))
                        ctx.inc(sc.activation(
                            sqb[:], src[:, a * TB:(a + 1) * TB], AF.Square),
                            "ac")
                    t_sq = ctx.tok("ac")

                    @blk.tensor
                    def _(t, a=a, sqb=sqb, t_sq=t_sq):
                        ctx.wait(t, "tensor", t_sq, t_wload,
                                 pfree(p512a) if a == 0 else ("pe", 0))
                        h = a // 2
                        ctx.inc(t.matmul(
                            p512a[0:H, 0:TB],
                            msk(IND_H + 3 - h, H), sqb[:],
                            start=(a == 0), stop=(a == DT - 1)), "pe")
                    state[f"t_sqf{a % 2}"] = ctx.tok("pe")
                t_sum = ctx.tok("pe")

                @blk.scalar
                def _(sc, t_sum=t_sum):
                    ctx.wait(sc, "scalar", t_sum)
                    ctx.inc(sc.activation(dst_rows, p512a[0:H, 0:TB],
                                          AF.Sqrt,
                                          bias=sb["cst"][0:H, LC_EPS6:
                                                         LC_EPS6 + 1]), "ac")
                set_pfree(p512a, ctx.tok("ac"))
                t_r = ctx.tok("ac")

                @blk.vector
                def _(v, t_r=t_r):
                    ctx.wait(v, "vector", t_r)
                    v.reciprocal(dst_rows, dst_rows)
                    ctx.inc(v.drain(), "ve")
                return ctx.tok("ve")

            t_qf = emit_l2fac(sb["qn"][:], sb["qfac"][:], t_q)
            t_kf = emit_l2fac(sb["kn"][:], sb["kfac"][:], t_k)

            def emit_apply_rows(dstbuf, rows, tok_rows, tok_data):
                last = ("ve", 0)
                for h in range(H):
                    t_b = emit_head_bcast(rows, h, 0, TB,
                                          sb["bcv"][:, 0:TB],
                                          [tok_rows, tok_data,
                                           state.get("t_bcv_used", ("pe", 0))])
                    state["t_bcv_used"] = ("pe", 0)

                    @blk.vector
                    def _(v, h=h, t_b=t_b):
                        ctx.wait(v, "vector", t_b)
                        for a in (2 * h, 2 * h + 1):
                            v.tensor_tensor(
                                dstbuf[:, a * TB:(a + 1) * TB],
                                dstbuf[:, a * TB:(a + 1) * TB],
                                sb["bcv"][:, 0:TB], ALU.mult)
                            v.drain()
                        ctx.inc(v.drain(), "ve")
                    state["t_bcv_used"] = ctx.tok("ve")
                    last = ctx.tok("ve")
                return last

            t_qn = emit_apply_rows(sb["qn"][:], sb["qfac"][:], t_qf, t_q)
            t_kn = emit_apply_rows(sb["kn"][:], sb["kfac"][:], t_kf, t_k)

            @blk.vector
            def _(v, t_v=t_v):
                ctx.wait(v, "vector", t_v,
                         state.get("t_brs_used", ("pe", 0)))
                col = 0
                for fi, kw_ in enumerate(FIRS[:3]):
                    brj = [BR_F1, BR_F3, BR_F7][fi]
                    for tap in range(kw_):
                        for a in range(DT):
                            src = sb["vtb"][:, a * (TB + 32) + 32 - (kw_ - 1)
                                           + tap: a * (TB + 32) + 32 -
                                           (kw_ - 1) + tap + TB]
                            wcol = sb["fir_s"][:, a * NTAP + col + tap:
                                               a * NTAP + col + tap + 1]
                            dstap = brs_ap(brj, a, 0, TB)
                            if tap == 0:
                                v.tensor_scalar(dstap, src, wcol, None,
                                                ALU.mult)
                            else:
                                v.scalar_tensor_tensor(dstap, src, wcol,
                                                       dstap, ALU.mult,
                                                       ALU.add)
                    col += kw_
                ctx.inc(v.drain(), "ve")
            t_fir_a = ctx.tok("ve")

            @blk.gpsimd
            def _(g, t_v=t_v):
                ctx.wait(g, "gpsimd", t_v,
                         state.get("t_brs_used", ("pe", 0)))
                col = 11
                for tap in range(31):
                    for a in range(DT):
                        src = sb["vtb"][:, a * (TB + 32) + 2 + tap:
                                       a * (TB + 32) + 2 + tap + TB]
                        wb = sb["fir_s"][:, a * NTAP + col + tap:
                                         a * NTAP + col + tap + 1] \
                            .broadcast_to([128, TB])
                        dstap = brs_ap(BR_F31, a, 0, TB)
                        if tap == 0:
                            g.tensor_tensor(dstap, src, wb, ALU.mult)
                        else:
                            tmp = sb["gtmp"][:, (a % 2) * TB:
                                             (a % 2) * TB + TB]
                            g.tensor_tensor(tmp, src, wb, ALU.mult)
                            g.tensor_tensor(dstap, dstap, tmp, ALU.add)
                ctx.inc(g.drain(), "gp")
            t_fir_b = ctx.tok("gp")

            for cc in range(TB // CH):
                gchunk = ib * (TB // CH) + cc
                first = gchunk == 0
                ccol = cc * CH

                for h in range(H):
                    a0, a1 = 2 * h, 2 * h + 1
                    ktok = dm("ktok", 0, 2)
                    kbtok = dm("kbtok", 0, 2)
                    vbtok = dm("vbtok", 0, 2)
                    ssb = sb["s_sb"][:, h * 512:(h + 1) * 512]

                    def kns(a):
                        return sb["kn"][:, a * TB + ccol: a * TB + ccol + CH]

                    def qns(a):
                        return sb["qn"][:, a * TB + ccol: a * TB + ccol + CH]

                    t_bb = emit_head_bcast(
                        sb["beta"][:], h, ccol, CH, sb["bb"][:],
                        [t_beta, state.get("t_bb_used", ("ve", 0))])

                    @blk.vector
                    def _(v, h=h, a0=a0, a1=a1, t_bb=t_bb, t_kn=t_kn,
                          t_v=t_v):
                        ctx.wait(v, "vector", t_bb, t_kn, t_v,
                                 state.get("t_dm_used", ("pe", 0)))
                        for i, a in enumerate((a0, a1)):
                            v.tensor_tensor(dm("kbT", i), kns(a),
                                            sb["bb"][:], ALU.mult)
                            v.tensor_tensor(dm("vbT", i),
                                            vt_ap(a, ccol, CH),
                                            sb["bb"][:], ALU.mult)
                        ctx.inc(v.drain(), "ve")
                    t_kbv = ctx.tok("ve")
                    state["t_bb_used"] = t_kbv

                    @blk.tensor
                    def _(t, a0=a0, a1=a1, t_kbv=t_kbv, t_qn=t_qn):
                        ctx.wait(t, "tensor", t_kbv, t_qn, pfree(p512a))
                        t.matmul(p512a[:, 0:128], dm("kbT", 0), kns(a0),
                                 start=True, stop=False)
                        t.matmul(p512a[:, 0:128], dm("kbT", 1), kns(a1),
                                 start=False, stop=True)
                        t.matmul(p512a[:, 128:256], kns(a0), dm("kbT", 0),
                                 start=True, stop=False)
                        ctx.inc(t.matmul(p512a[:, 128:256], kns(a1),
                                         dm("kbT", 1), start=False,
                                         stop=True), "pe")
                    t_Araw = ctx.tok("pe")

                    @blk.tensor
                    def _(t, a0=a0, a1=a1, t_kbv=t_kbv):
                        ctx.wait(t, "tensor", t_kbv, pfree(pT))
                        t.transpose(pT[:, 0:128], kns(a0), sb["idn"][:])
                        t.transpose(pT[:, 128:256], kns(a1), sb["idn"][:])
                        t.transpose(pT[:, 256:384], dm("kbT", 0),
                                    sb["idn"][:])
                        ctx.inc(t.transpose(pT[:, 384:512], dm("kbT", 1),
                                            sb["idn"][:]), "pe")
                    t_tr1 = ctx.tok("pe")

                    @blk.scalar
                    def _(sc, t_tr1=t_tr1):
                        ctx.wait(sc, "scalar", t_tr1)
                        sc.copy(ktok, pT[:, 0:256])
                        ctx.inc(sc.copy(kbtok, pT[:, 256:512]), "ac")
                    set_pfree(pT, ctx.tok("ac"))

                    @blk.tensor
                    def _(t, t_kbv=t_kbv):
                        ctx.wait(t, "tensor", pfree(pT))
                        t.transpose(pT[:, 0:128], dm("vbT", 0), sb["idn"][:])
                        ctx.inc(t.transpose(pT[:, 128:256], dm("vbT", 1),
                                            sb["idn"][:]), "pe")
                    t_tr2 = ctx.tok("pe")
                    state["t_dm_used"] = t_tr2

                    @blk.scalar
                    def _(sc, t_tr2=t_tr2):
                        ctx.wait(sc, "scalar", t_tr2)
                        ctx.inc(sc.copy(vbtok, pT[:, 0:256]), "ac")
                    set_pfree(pT, ctx.tok("ac"))
                    t_toks = ctx.tok("ac")

                    @blk.vector
                    def _(v, t_Araw=t_Araw):
                        ctx.wait(v, "vector", t_Araw)
                        v.tensor_tensor(dm("A"), p512a[:, 0:128],
                                        msk(MSK_A, 128), ALU.mult)
                        v.tensor_tensor(dm("ATr"), p512a[:, 128:256],
                                        msk(MSK_AT, 128), ALU.mult)
                        v.drain()
                        ctx.inc(v.tensor_tensor(dm("P"), dm("A"),
                                                sb["idn"][:], ALU.add), "ve")
                    set_pfree(p512a, ctx.tok("ve"))
                    t_P0 = ctx.tok("ve")

                    t_lvl = t_P0
                    for lvl in range(6):
                        @blk.tensor
                        def _(t, t_lvl=t_lvl):
                            ctx.wait(t, "tensor", t_lvl, pfree(p512b))
                            t.matmul(p512b[:, 0:128], dm("ATr"), dm("A"),
                                     start=True, stop=True)
                            ctx.inc(t.matmul(p512b[:, 128:256], dm("A"),
                                             dm("ATr"), start=True,
                                             stop=True), "pe")
                        t_mm = ctx.tok("pe")

                        @blk.vector
                        def _(v, t_mm=t_mm):
                            ctx.wait(v, "vector", t_mm)
                            v.tensor_copy(dm("X"), p512b[:, 0:128])
                            ctx.inc(v.tensor_copy(dm("XT"),
                                                  p512b[:, 128:256]), "ve")
                        t_x = ctx.tok("ve")

                        @blk.tensor
                        def _(t, t_x=t_x):
                            ctx.wait(t, "tensor", t_x)
                            ctx.inc(t.matmul(p512b[:, 256:384], dm("XT"),
                                             dm("P"), start=True,
                                             stop=True), "pe")
                        t_pm = ctx.tok("pe")

                        @blk.vector
                        def _(v, t_pm=t_pm):
                            ctx.wait(v, "vector", t_pm)
                            ctx.inc(v.tensor_tensor(
                                dm("P2"), p512b[:, 256:384], dm("P"),
                                ALU.add), "ve")
                        set_pfree(p512b, ctx.tok("ve"))
                        t_lvl = ctx.tok("ve")
                        DM["A"], DM["X"] = DM["X"], DM["A"]
                        DM["ATr"], DM["XT"] = DM["XT"], DM["ATr"]
                        DM["P"], DM["P2"] = DM["P2"], DM["P"]
                    t_chain = t_lvl

                    @blk.tensor
                    def _(t, t_chain=t_chain):
                        ctx.wait(t, "tensor", t_chain, pfree(pT))
                        ctx.inc(t.transpose(pT[:, 0:128], dm("P"),
                                            sb["idn"][:]), "pe")
                    t_ttr = ctx.tok("pe")

                    @blk.scalar
                    def _(sc, t_ttr=t_ttr):
                        ctx.wait(sc, "scalar", t_ttr)
                        ctx.inc(sc.copy(dm("Tt"), pT[:, 0:128]), "ac")
                    set_pfree(pT, ctx.tok("ac"))
                    t_Tt = ctx.tok("ac")

                    if not first:
                        @blk.tensor
                        def _(t, t_Tt=t_Tt, t_toks=t_toks):
                            ctx.wait(t, "tensor", t_Tt, t_toks,
                                     pfree(p512b))
                            t.matmul(p512b[:, 0:128], kbtok[:, 0:128],
                                     dm("Tt"), start=True, stop=True)
                            ctx.inc(t.matmul(p512b[:, 128:256],
                                             kbtok[:, 128:256], dm("Tt"),
                                             start=True, stop=True), "pe")
                        t_w = ctx.tok("pe")

                        @blk.scalar
                        def _(sc, t_w=t_w):
                            ctx.wait(sc, "scalar", t_w)
                            sc.activation(dm("wn", 0), p512b[:, 0:128],
                                          AF.Copy, scale=cstc(LC_NEG1))
                            ctx.inc(sc.activation(dm("wn", 1),
                                                  p512b[:, 128:256],
                                                  AF.Copy,
                                                  scale=cstc(LC_NEG1)), "ac")
                        set_pfree(p512b, ctx.tok("ac"))
                        t_wn = ctx.tok("ac")
                    else:
                        t_wn = t_Tt

                    @blk.tensor
                    def _(t, t_Tt=t_Tt, t_toks=t_toks, t_wn=t_wn,
                          first=first, h=h):
                        ctx.wait(t, "tensor", t_Tt, t_toks, t_wn, pfree(pE))
                        t.matmul(pE[:, 0:256], dm("Tt"), vbtok,
                                 start=True, stop=first)
                        if not first:
                            t.matmul(pE[:, 0:256], dm("wn", 0),
                                     ssb[:, 0:256], start=False, stop=False)
                            ctx.inc(t.matmul(pE[:, 0:256], dm("wn", 1),
                                             ssb[:, 256:512], start=False,
                                             stop=True), "pe")
                        else:
                            ctx.inc(t.matmul(pE[:, 256:384], sb["idn"][:],
                                             sb["idn"][:], start=True,
                                             stop=True), "pe")
                    t_uh_mm = ctx.tok("pe")

                    @blk.scalar
                    def _(sc, t_uh_mm=t_uh_mm):
                        ctx.wait(sc, "scalar", t_uh_mm)
                        ctx.inc(sc.copy(dm("uh", 0, 2), pE[:, 0:256]), "ac")
                    t_uh = ctx.tok("ac")
                    set_pfree(pE, t_uh)

                    @blk.tensor
                    def _(t, a0=a0, a1=a1, t_qn=t_qn, t_toks=t_toks):
                        ctx.wait(t, "tensor", t_qn, pfree(p512b))
                        t.matmul(p512b[:, 0:128], kns(a0), qns(a0),
                                 start=True, stop=False)
                        ctx.inc(t.matmul(p512b[:, 0:128], kns(a1), qns(a1),
                                         start=False, stop=True), "pe")
                    t_at_mm = ctx.tok("pe")

                    @blk.vector
                    def _(v, t_at_mm=t_at_mm):
                        ctx.wait(v, "vector", t_at_mm)
                        ctx.inc(v.tensor_tensor(dm("at"), p512b[:, 0:128],
                                                msk(MSK_U, 128), ALU.mult),
                                "ve")
                    set_pfree(p512b, ctx.tok("ve"))
                    t_at = ctx.tok("ve")

                    for dt2 in range(2):
                        @blk.tensor
                        def _(t, h=h, dt2=dt2, a0=a0, a1=a1, t_uh=t_uh,
                              t_at=t_at, first=first):
                            ctx.wait(t, "tensor", t_uh, t_at, pfree(pE))
                            if not first:
                                t.matmul(pE[:, 256 + dt2 * 128:
                                            384 + dt2 * 128],
                                         ssb[:, 0 * 256 + dt2 * 128:
                                             0 * 256 + dt2 * 128 + 128],
                                         qns(a0), start=True, stop=False)
                                t.matmul(pE[:, 256 + dt2 * 128:
                                            384 + dt2 * 128],
                                         ssb[:, 1 * 256 + dt2 * 128:
                                             1 * 256 + dt2 * 128 + 128],
                                         qns(a1), start=False, stop=False)
                                ctx.inc(t.matmul(
                                    pE[:, 256 + dt2 * 128:384 + dt2 * 128],
                                    dm("uh", dt2), dm("at"),
                                    start=False, stop=True), "pe")
                            else:
                                ctx.inc(t.matmul(
                                    pE[:, 256 + dt2 * 128:384 + dt2 * 128],
                                    dm("uh", dt2), dm("at"),
                                    start=True, stop=True), "pe")
                        t_o = ctx.tok("pe")

                        @blk.scalar
                        def _(sc, h=h, dt2=dt2, t_o=t_o):
                            ctx.wait(sc, "scalar", t_o)
                            ctx.inc(sc.copy(
                                brs_ap(BR_DL, 2 * h + dt2, ccol, CH),
                                pE[:, 256 + dt2 * 128:384 + dt2 * 128]),
                                "ac")
                    set_pfree(pE, ctx.tok("ac"))

                    @blk.tensor
                    def _(t, h=h, t_uh=t_uh, first=first,
                          gchunk=gchunk):
                        ctx.wait(t, "tensor", t_uh,
                                 state.get(f"t_scp{h}", ("ac", 0)))
                        for i in range(2):
                            last = t.matmul(
                                pSh[h][:, i * 256:(i + 1) * 256],
                                ktok[:, i * 128:(i + 1) * 128],
                                dm("uh", 0, 2),
                                start=(first and i == 0),
                                stop=(gchunk == NCH - 1 and i == 1))
                        ctx.inc(last, "pe")
                    t_su = ctx.tok("pe")

                    @blk.scalar
                    def _(sc, h=h, t_su=t_su):
                        ctx.wait(sc, "scalar", t_su)
                        ctx.inc(sc.copy(ssb, pSh[h][:]), "ac")
                    state[f"t_scp{h}"] = ctx.tok("ac")
                    if dbg and b == 0 and gchunk == 0 and h == 0:
                        @blk.sync
                        def _(s):
                            ctx.wait(s, "sync", ctx.tok("ac"), ctx.tok("ve"))
                            ctx.inc(s.dma_start(out=ap["dbg_dmx"],
                                                in_=sb["dmx"][:]), "ds", 16)
                            ctx.inc(s.dma_start(out=ap["dbg_bb"],
                                                in_=sb["bb"][:]), "ds", 16)
            t_delta = ctx.tok("ac")

            def branch_tile(s, a, w=TB):
                if BR_SRC[s] is None:
                    return vt_ap(a, 0, w)
                return brs_ap(BR_SRC[s], a, 0, w)

            t_br_ready = [t_fir_a, t_fir_a, t_fir_a, t_fir_b, t_delta, t_v]

            @blk.tensor
            def _(t):
                ctx.wait(t, "tensor", pfree(p512a), t_wload, *t_br_ready)
                n = 0
                for s_ in range(6):
                    for a in range(DT):
                        r = s_ * 4 + a // 2
                        last = t.matmul(p512a[0:24, 0:TB],
                                        msk(IND_M + 23 - r, 24),
                                        branch_tile(s_, a),
                                        start=(n == 0), stop=(n == 47))
                        n += 1
                ctx.inc(last, "pe")
            t_s1k = ctx.tok("pe")

            @blk.scalar
            def _(sc, t_s1k=t_s1k):
                ctx.wait(sc, "scalar", t_s1k)
                ctx.inc(sc.copy(sb["stm"][:], p512a[0:24, 0:TB]), "ac")
            t_stm = ctx.tok("ac")

            n = [0]
            for s_ in range(6):
                for a in range(DT):
                    sqb = sb["sq1"] if n[0] % 2 == 0 else sb["sq2"]

                    @blk.scalar
                    def _(sc, s_=s_, a=a, sqb=sqb):
                        ctx.wait(sc, "scalar",
                                 state.get(f"t_sqs{n[0] % 2}", ("pe", 0)),
                                 *t_br_ready)
                        ctx.inc(sc.activation(sqb[:], branch_tile(s_, a),
                                              AF.Square), "ac")
                    t_sq = ctx.tok("ac")

                    @blk.tensor
                    def _(t, s_=s_, a=a, sqb=sqb, t_sq=t_sq, n0=n[0],
                          t_stm=t_stm):
                        r = s_ * 4 + a // 2
                        ctx.wait(t, "tensor", t_sq,
                                 pfree(p512b) if n0 == 0 else ("pe", 0),
                                 t_stm if n0 == 0 else ("pe", 0))
                        ctx.inc(t.matmul(p512b[0:64, 0:TB],
                                         msk(IND_2 + 31 - r, 64), sqb[:],
                                         start=(n0 == 0), stop=(n0 == 47)),
                                "pe")
                    state[f"t_sqs{n[0] % 2}"] = ctx.tok("pe")
                    n[0] += 1
            t_s2k = ctx.tok("pe")

            @blk.scalar
            def _(sc, t_s2k=t_s2k):
                ctx.wait(sc, "scalar", t_s2k)
                ctx.inc(sc.activation(sb["stl"][:], p512b[32:56, 0:TB],
                                      AF.Sqrt), "ac")
            t_stl = ctx.tok("ac")

            @blk.vector
            def _(v, t_s2k=t_s2k, t_stm=t_stm):
                ctx.wait(v, "vector", t_s2k, t_stm)
                v.tensor_tensor(sb["stf"][:], sb["stm"][:], sb["stm"][:],
                                ALU.mult)
                v.drain()
                ctx.inc(v.scalar_tensor_tensor(
                    sb["stf"][:], sb["stf"][:], -256.0 / 255.0,
                    p512b[0:24, 0:TB], ALU.mult, ALU.add), "ve")
            set_pfree(p512b, ctx.tok("ve"))
            t_var2 = ctx.tok("ve")

            @blk.scalar
            def _(sc, t_var2=t_var2):
                ctx.wait(sc, "scalar", t_var2)
                ctx.inc(sc.activation(sb["sts"][:], sb["stf"][:], AF.Sqrt),
                        "ac")
            t_sts = ctx.tok("ac")

            n = [0]
            for s_ in range(6):
                for a in range(DT):
                    sqb = sb["sq1"] if n[0] % 2 == 0 else sb["sq2"]

                    @blk.vector
                    def _(v, s_=s_, a=a, sqb=sqb):
                        ctx.wait(v, "vector",
                                 state.get(f"t_sqa{n[0] % 2}", ("pe", 0)),
                                 *t_br_ready)
                        ctx.inc(v.scalar_tensor_tensor(
                            sqb[:], branch_tile(s_, a), -1.0,
                            branch_tile(s_, a), ALU.mult, ALU.max), "ve")
                    t_ab = ctx.tok("ve")

                    @blk.tensor
                    def _(t, s_=s_, a=a, sqb=sqb, t_ab=t_ab, n0=n[0]):
                        r = s_ * 4 + a // 2
                        ctx.wait(t, "tensor", t_ab,
                                 pfree(p512a) if n0 == 0 else ("pe", 0))
                        ctx.inc(t.matmul(p512a[0:24, 0:TB],
                                         msk(IND_M + 23 - r, 24), sqb[:],
                                         start=(n0 == 0), stop=(n0 == 47)),
                                "pe")
                    state[f"t_sqa{n[0] % 2}"] = ctx.tok("pe")
                    n[0] += 1
            t_sak = ctx.tok("pe")

            @blk.scalar
            def _(sc, t_sak=t_sak):
                ctx.wait(sc, "scalar", t_sak)
                ctx.inc(sc.copy(sb["sta"][:], p512a[0:24, 0:TB]), "ac")
            set_pfree(p512a, ctx.tok("ac"))
            t_sta = ctx.tok("ac")

            if dbg and ib == 1 and b == 0:
                @blk.sync
                def _(s):
                    ctx.wait(s, "sync", t_stm, t_sts, t_sta, t_stl)
                    ctx.inc(s.dma_start(out=ap["dbg_stats"][:, 0:TB],
                                        in_=sb["stm"][:]), "ds", 16)
                    ctx.inc(s.dma_start(out=ap["dbg_stats"][:, TB:2 * TB],
                                        in_=sb["sts"][:]), "ds", 16)
                    ctx.inc(s.dma_start(out=ap["dbg_stats"][:, 2 * TB:3 * TB],
                                        in_=sb["sta"][:]), "ds", 16)
                    ctx.inc(s.dma_start(out=ap["dbg_stats"][:, 3 * TB:4 * TB],
                                        in_=sb["stl"][:]), "ds", 16)

            for gt in range(DT):
                p = p512a if gt % 2 == 0 else p512b

                @blk.tensor
                def _(t, gt=gt, p=p):
                    ctx.wait(t, "tensor", pfree(p), t_xn1T, t_stm, t_sts,
                             t_sta, t_stl, t_wload2, t_w1h_ld)
                    for k in range(DT):
                        t.matmul(p[:, 0:TB],
                                 sb["vwh"][:, k * D + gt * 128:
                                           k * D + gt * 128 + 128],
                                 sb["xn1T"][:, k * TB:(k + 1) * TB],
                                 start=(k == 0), stop=False)
                    t.matmul(p[:, 0:TB], sb["w1m_s"][:, gt * 128:
                                                     gt * 128 + 128],
                             sb["stm"][:], start=False, stop=False)
                    t.matmul(p[:, 0:TB], sb["w1s_s"][:, gt * 128:
                                                     gt * 128 + 128],
                             sb["sts"][:], start=False, stop=False)
                    t.matmul(p[:, 0:TB], sb["w1a_s"][:, gt * 128:
                                                     gt * 128 + 128],
                             sb["sta"][:], start=False, stop=False)
                    ctx.inc(t.matmul(p[:, 0:TB],
                                     sb["w1l_s"][:, gt * 128: gt * 128 + 128],
                                     sb["stl"][:], start=False, stop=True),
                            "pe")
                tk = ctx.tok("pe")

                @blk.scalar
                def _(sc, gt=gt, p=p, tk=tk):
                    ctx.wait(sc, "scalar", tk, t_v,
                             state.get("t_hg_used", ("pe", 0)))
                    ctx.inc(sc.activation(
                        sb["hgT"][:, gt * TB:(gt + 1) * TB], p[:, 0:TB],
                        AF.Gelu, bias=sb["b1_s"][:, gt:gt + 1]), "ac")
                set_pfree(p, ctx.tok("ac"))
            t_hg = ctx.tok("ac")

            for j in range(TB // 128):
                @blk.tensor
                def _(t, j=j, t_hg=t_hg):
                    ctx.wait(t, "tensor", t_hg, pfree(pE))
                    for k in range(DT):
                        last = t.matmul(
                            pE[:, 0:24],
                            sb["hgT"][:, k * TB + j * 128:
                                      k * TB + j * 128 + 128],
                            sb["w2_s"][:, ts(k, 24)],
                            start=(k == 0), stop=(k == DT - 1))
                    ctx.inc(last, "pe")
                t_lg = ctx.tok("pe")

                @blk.vector
                def _(v, j=j, t_lg=t_lg):
                    ctx.wait(v, "vector", t_lg, t_cload,
                             state.get("t_wgt_used", ("pe", 0)))
                    v.tensor_tensor(sb["wgf"][:], pE[:, 0:24], sb["b2_b"][:],
                                    ALU.add)
                    v.drain()
                    v.tensor_tensor(sb["wgf"][:], sb["wgf"][:], sb["it_b"][:],
                                    ALU.mult)
                    v.drain()
                    v.reduce_max(sb["red"][:, 40:44],
                                 sb["wgf"][:].rearrange(
                                     "p (a s) -> p a s", a=4), axis=AX.X)
                    v.drain()
                    ctx.inc(v.tensor_tensor(
                        sb["wgf"][:].rearrange("p (a s) -> p a s", a=4),
                        sb["wgf"][:].rearrange("p (a s) -> p a s", a=4),
                        sb["red"][:, 40:44].unsqueeze(2)
                        .broadcast_to([128, 4, 6]), ALU.subtract), "ve")
                set_pfree(pE, ctx.tok("ve"))
                t_lgx = ctx.tok("ve")

                @blk.scalar
                def _(sc, t_lgx=t_lgx):
                    ctx.wait(sc, "scalar", t_lgx)
                    ctx.inc(sc.activation(sb["wgf"][:], sb["wgf"][:],
                                          AF.Exp), "ac")
                t_we = ctx.tok("ac")

                @blk.vector
                def _(v, t_we=t_we):
                    ctx.wait(v, "vector", t_we)
                    v.reduce_sum(sb["red"][:, 44:48],
                                 sb["wgf"][:].rearrange(
                                     "p (a s) -> p a s", a=4), axis=AX.X)
                    v.drain()
                    v.reciprocal(sb["red"][:, 44:48], sb["red"][:, 44:48])
                    v.drain()
                    ctx.inc(v.tensor_tensor(
                        sb["wgt"][:].rearrange("p (a s) -> p a s", a=4),
                        sb["wgf"][:].rearrange("p (a s) -> p a s", a=4),
                        sb["red"][:, 44:48].unsqueeze(2)
                        .broadcast_to([128, 4, 6]), ALU.mult), "ve")
                t_wg = ctx.tok("ve")

                @blk.tensor
                def _(t, j=j, t_wg=t_wg):
                    ctx.wait(t, "tensor", t_wg, pfree(pT))
                    ctx.inc(t.transpose(pT[0:24, 0:128], sb["wgt"][:],
                                        sb["idn"][:]), "pe")
                t_wtr = ctx.tok("pe")
                state["t_wgt_used"] = t_wtr

                @blk.scalar
                def _(sc, j=j, t_wtr=t_wtr):
                    ctx.wait(sc, "scalar", t_wtr)
                    ctx.inc(sc.copy(sb["wTt"][0:24, j * 128:j * 128 + 128],
                                    pT[0:24, 0:128]), "ac")
                set_pfree(pT, ctx.tok("ac"))
            t_wT = ctx.tok("ac")

            for s_ in range(6):
                for h in range(H):
                    r = h * 6 + s_
                    a0, a1 = 2 * h, 2 * h + 1

                    @blk.tensor
                    def _(t, r=r, t_wT=t_wT):
                        ctx.wait(t, "tensor", t_wT, pfree(p512b), t_wload)
                        ctx.inc(t.matmul(p512b[:, 0:TB],
                                         msk(SEL24 + r * 128, 128)[0:24, :],
                                         sb["wTt"][0:24, 0:TB],
                                         start=True, stop=True), "pe")
                    t_bw = ctx.tok("pe")

                    @blk.scalar
                    def _(sc, t_bw=t_bw):
                        ctx.wait(sc, "scalar", t_bw,
                                 state.get("t_bcw_used", ("ve", 0)))
                        ctx.inc(sc.copy(sb["bcv"][:, 512:512 + TB],
                                        p512b[:, 0:TB]), "ac")
                    set_pfree(p512b, ctx.tok("ac"))
                    t_bc = ctx.tok("ac")

                    eng = "vector" if (s_ * H + h) % 2 == 0 else "gpsimd"

                    def mixop(v, s_=s_, a0=a0, a1=a1, t_bc=t_bc, eng=eng):
                        ctx.wait(v, eng, t_bc, t_delta, t_fir_a, t_fir_b,
                                 t_v, state.get("t_fsd_used", ("pe", 0)))
                        if s_ == 0:
                            v.tensor_tensor(
                                sb["fsd"][:, a0 * TB:(a0 + 1) * TB],
                                branch_tile(s_, a0),
                                sb["bcv"][:, 512:512 + TB], ALU.mult)
                            v.tensor_tensor(
                                sb["fsd"][:, a1 * TB:(a1 + 1) * TB],
                                branch_tile(s_, a1),
                                sb["bcv"][:, 512:512 + TB], ALU.mult)
                        else:
                            tmp0 = sb["sq1"] if eng == "vector" else sb["sq2"]
                            v.tensor_tensor(tmp0[:], branch_tile(s_, a0),
                                            sb["bcv"][:, 512:512 + TB],
                                            ALU.mult)
                            v.tensor_tensor(sb["bcv"][:, 0:TB],
                                            branch_tile(s_, a1),
                                            sb["bcv"][:, 512:512 + TB],
                                            ALU.mult)
                            v.tensor_tensor(
                                sb["fsd"][:, a0 * TB:(a0 + 1) * TB],
                                sb["fsd"][:, a0 * TB:(a0 + 1) * TB],
                                tmp0[:], ALU.add)
                            v.tensor_tensor(
                                sb["fsd"][:, a1 * TB:(a1 + 1) * TB],
                                sb["fsd"][:, a1 * TB:(a1 + 1) * TB],
                                sb["bcv"][:, 0:TB], ALU.add)
                        tk = v.drain()
                        ctx.inc(tk, "ve" if eng == "vector" else "gp")

                    if eng == "vector":
                        @blk.vector
                        def _(v, f=mixop):
                            f(v)
                    else:
                        @blk.gpsimd
                        def _(g, f=mixop):
                            f(g)
                    state["t_bcw_used"] = (ctx.tok("ve") if eng == "vector"
                                           else ctx.tok("gp"))
                    state["t_fsd_used"] = state["t_bcw_used"]
            t_mixv = ctx.tok("ve")
            t_mixg = ctx.tok("gp")

            for a in range(DT):
                sqb = sb["sq1"] if a % 2 == 0 else sb["sq2"]

                @blk.scalar
                def _(sc, a=a, sqb=sqb, t_mixv=t_mixv, t_mixg=t_mixg):
                    ctx.wait(sc, "scalar", t_mixv, t_mixg,
                             state.get(f"t_sqr{a % 2}", ("pe", 0)))
                    ctx.inc(sc.activation(
                        sqb[:], sb["fsd"][:, a * TB:(a + 1) * TB],
                        AF.Square), "ac")
                t_sq = ctx.tok("ac")

                @blk.tensor
                def _(t, a=a, sqb=sqb, t_sq=t_sq):
                    ctx.wait(t, "tensor", t_sq,
                             pfree(p512a) if a == 0 else ("pe", 0))
                    h = a // 2
                    ctx.inc(t.matmul(p512a[0:H, 0:TB],
                                     msk(IND_H + 3 - h, H), sqb[:],
                                     start=(a == 0), stop=(a == DT - 1)),
                            "pe")
                state[f"t_sqr{a % 2}"] = ctx.tok("pe")
            t_rsum = ctx.tok("pe")

            @blk.scalar
            def _(sc, t_rsum=t_rsum):
                ctx.wait(sc, "scalar", t_rsum)
                ctx.inc(sc.activation(
                    sb["qfac"][:], p512a[0:H, 0:TB], AF.Sqrt,
                    bias=sb["cst"][0:H, LC_EPS5:LC_EPS5 + 1],
                    scale=sb["cst"][0:H, LC_INV256:LC_INV256 + 1]), "ac")
            set_pfree(p512a, ctx.tok("ac"))
            t_rr = ctx.tok("ac")

            @blk.vector
            def _(v, t_rr=t_rr):
                ctx.wait(v, "vector", t_rr)
                v.reciprocal(sb["qfac"][:], sb["qfac"][:])
                ctx.inc(v.drain(), "ve")
            t_rinv = ctx.tok("ve")

            for h in range(H):
                t_b = emit_head_bcast(sb["qfac"][:], h, 0, TB,
                                      sb["bcv"][:, 0:TB],
                                      [t_rinv,
                                       state.get("t_bcv_used", ("pe", 0))])

                @blk.vector
                def _(v, h=h, t_b=t_b):
                    ctx.wait(v, "vector", t_b, t_cload)
                    for a in (2 * h, 2 * h + 1):
                        v.scalar_tensor_tensor(
                            sb["fsd"][:, a * TB:(a + 1) * TB],
                            sb["fsd"][:, a * TB:(a + 1) * TB],
                            sb["onw_s"][:, a % 2:a % 2 + 1],
                            sb["bcv"][:, 0:TB], ALU.mult, ALU.mult)
                        v.drain()
                    ctx.inc(v.drain(), "ve")
                state["t_bcv_used"] = ctx.tok("ve")
            t_fsdn = ctx.tok("ve")

            if dbg and ib == 1 and b == 0:
                @blk.sync
                def _(s):
                    ctx.wait(s, "sync", t_xn1T, t_qn, t_kn, t_v, t_beta,
                             t_delta, t_hg, t_wT, t_fsdn, t_fir_a, t_fir_b)
                    for nm, buf in [("dbg_xn1T", "xn1T"), ("dbg_qn", "qn"),
                                    ("dbg_kn", "kn"), ("dbg_vtb", "vt"),
                                    ("dbg_br", "brs"), ("dbg_hg", "hgT"),
                                    ("dbg_fsd", "fsd")]:
                        ctx.inc(s.dma_start(out=ap[nm], in_=sb[buf][:]),
                                "ds", 16)
                    ctx.inc(s.dma_start(out=ap["dbg_beta"], in_=sb["beta"][:]),
                            "ds", 16)
                    ctx.inc(s.dma_start(out=ap["dbg_wgt"], in_=sb["wTt"][:]),
                            "ds", 16)

            for j in range(TB // 128):
                tt = t0 // 128 + j

                @blk.sync
                def _(s, tt=tt):
                    ctx.wait(s, "sync", state.get("t_out_used", ("ve", 0)),
                             state.get("t_x1_used", ("ve", 0)))
                    ctx.inc(s.dma_start(out=sb["x1t"][:],
                                        in_=x_in[ts(tt), :]), "ds", 16)
                    if b == 0:
                        ctx.inc(s.dma_start(out=sb["x2t"][:],
                                            in_=shm_moe[ts(tt), :]),
                                "ds", 16)
                t_xo = ctx.tok("ds")
                state["t_x1_used"] = t_xo

                @blk.vector
                def _(v, t_xo=t_xo):
                    ctx.wait(v, "vector", t_xo, t_alf)
                    if b == 0:
                        v.scalar_tensor_tensor(
                            sb["x1t"][:], sb["x2t"][:], sb["alf"][:, 0:1],
                            sb["x1t"][:], ALU.mult, ALU.add)
                    ctx.inc(v.drain(), "ve")
                t_x1r = ctx.tok("ve")

                for oh in range(2):
                    p = p512a if oh == 0 else p512b

                    @blk.tensor
                    def _(t, j=j, oh=oh, p=p, t_fsdn=t_fsdn, t_xn1T=t_xn1T):
                        ctx.wait(t, "tensor", t_fsdn, t_xn1T, t_M, pfree(p),
                                 t_wload2)
                        for k in range(DT):
                            t.matmul(p[:],
                                     sb["fsd"][:, k * TB + j * 128:
                                               k * TB + j * 128 + 128],
                                     sb["ow_s"][:, k * D + oh * 512:
                                                k * D + oh * 512 + 512],
                                     start=(k == 0), stop=False)
                        for k in range(DT):
                            t.matmul(
                                p[:],
                                sb["xn1T"][:, k * TB + j * 128:
                                           k * TB + j * 128 + 128],
                                sb["M_s"][:, k * D + oh * 512:
                                          k * D + oh * 512 + 512],
                                start=False, stop=False)
                        for k in range(DT):
                            last = t.matmul(
                                p[:],
                                sb["xn1T"][:, k * TB + j * 128:
                                           k * TB + j * 128 + 128],
                                sb["M_l"][:, k * D + oh * 512:
                                          k * D + oh * 512 + 512],
                                start=False, stop=(k == DT - 1))
                        ctx.inc(last, "pe")
                    t_om = ctx.tok("pe")

                    @blk.vector
                    def _(v, oh=oh, p=p, t_om=t_om, t_x1r=t_x1r):
                        ctx.wait(v, "vector", t_om, t_x1r, t_cload,
                                 state.get("t_st_done", ("ds", 0)))
                        v.tensor_tensor(sb["outs"][:, ts(oh, 512)], p[:],
                                        sb["fb_b"][:, ts(oh, 512)], ALU.add)
                        v.drain()
                        ctx.inc(v.tensor_tensor(
                            sb["outs"][:, ts(oh, 512)],
                            sb["outs"][:, ts(oh, 512)],
                            sb["x1t"][:, ts(oh, 512)], ALU.add), "ve")
                    set_pfree(p, ctx.tok("ve"))
                t_oe = ctx.tok("ve")
                state["t_x1_used"] = t_oe

                @blk.sync
                def _(s, tt=tt, t_oe=t_oe):
                    ctx.wait(s, "sync", t_oe)
                    ctx.inc(s.dma_start(out=out_t[ts(tt), :],
                                        in_=sb["outs"][:]), "ds", 16)
                state["t_st_done"] = ctx.tok("ds")
                state["t_out_used"] = ctx.tok("ve")
            state["t_xnT_used"] = ctx.tok("pe")
            state["t_vwh_used"] = ctx.tok("pe")
            state["t_hg_used"] = ctx.tok("pe")
            state["t_raw_used"] = ctx.tok("pe")
            state["t_qn_used"] = ctx.tok("pe")
            state["t_kn_used"] = ctx.tok("pe")
            state["t_vt_used"] = ctx.tok("pe")
            state["t_brs_used"] = ctx.tok("pe")

            @blk.vector
            def _(v):
                ctx.wait(v, "vector", ctx.tok("pe"))
                for a in range(DT):
                    vb0 = a * (TB + 32)
                    v.tensor_copy(sb["vtb"][:, vb0:vb0 + 32],
                                  sb["vtb"][:, vb0 + TB:vb0 + TB + 32])
                ctx.inc(v.drain(), "ve")
            state["t_vtl"] = ctx.tok("ve")

    @blk.sync
    def _(s):
        ctx.wait(s, "sync", ctx.tok("ds"))

    stack.close()
    return nc


FB = [0, 2, 1, 3, 4, 5, 6, 7]
_NONCE = [0]


def fused_in_maps(inputs):
    import time
    x = np.ascontiguousarray(inputs["L0_attn_in_features"], np.float32)
    wqT = np.ascontiguousarray(np.asarray(inputs["L0_attn_w_q"]).T)
    wkT = np.ascontiguousarray(np.asarray(inputs["L0_attn_w_k"]).T)
    wvT = np.ascontiguousarray(np.asarray(inputs["L0_attn_w_v"]).T)
    woT = np.ascontiguousarray(np.asarray(inputs["L0_attn_w_o"]).T)
    rwT = np.ascontiguousarray(np.asarray(inputs["L0_ffn_router_w"]).T)
    rb = np.ascontiguousarray(np.asarray(inputs["L0_ffn_router_b"])).reshape(1, E)
    ei = np.ascontiguousarray(inputs["L0_ffn_experts_inter"], np.float32)
    eo = np.ascontiguousarray(inputs["L0_ffn_experts_out"], np.float32)
    masks = _masks_np()
    ident = np.eye(128, dtype=np.float32)
    xflat = x.reshape(-1, D)
    idx = np.arange(D)
    F = np.exp(-2j * np.pi * np.outer(idx, idx) / D)
    Ffr = np.ascontiguousarray(np.real(F).astype(np.float32))
    Ffi = np.ascontiguousarray(np.imag(F).astype(np.float32))
    wfr = np.ascontiguousarray(np.asarray(inputs["L1_ffn_w_freq_real"],
                                          np.float32))
    wfi = np.ascontiguousarray(np.asarray(inputs["L1_ffn_w_freq_imag"],
                                          np.float32))
    l1c = _l1_common_maps(inputs)
    _NONCE[0] = max(_NONCE[0] + 1,
                    int(time.time_ns() // 1000) & 0x3FFFFFFF) | 1
    nonce = _NONCE[0]
    maps = []
    for c in range(8):
        slot = c & 1
        rs = slice(512 * slot, 512 * (slot + 1))
        m = dict(l1c)
        m.update({
            "x0": np.ascontiguousarray(x[FB[c]]),
            "x1": np.ascontiguousarray(x[c + 8]),
            "x_moe": (np.ascontiguousarray(
                xflat[2048 * c:2048 * (c + 1)]) if c < 4
                else np.zeros((MT2, D), np.float32)),
            "wqT": wqT, "wkT": wkT, "wvT": wvT, "woT": woT,
            "router_wT": rwT, "router_b": rb, "ei": ei, "eo": eo,
            "ident": ident, "masks": masks,
            "consts": _l1_consts16_np(),
            "wfr": wfr, "wfi": wfi, "Ffr": Ffr, "Ffi": Ffi,
            "fcol": np.ascontiguousarray(Ffr[:, rs]),
            "gcol": np.ascontiguousarray(Ffi[:, rs]),
            "ngcol": np.ascontiguousarray(-Ffi[:, rs]),
            "alpha": np.full((128, 1), 1.0 if c in (0, 2) else 0.0,
                             np.float32),
            "cfg": np.array([[slot, nonce, 0, 2 * nonce] + [0] * 12],
                            np.int32),
        })
        maps.append(m)
    return maps


def _l1_consts16_np():
    c = np.zeros((128, 16), np.float32)
    for i, val in enumerate(L1_CONSTS):
        c[:, i] = val
    return c


def _l1_common_maps(inputs):
    n = np.asarray
    qc = np.ascontiguousarray(n(inputs["L1_attn_qc_w"]).reshape(D, 4))
    kc = np.ascontiguousarray(n(inputs["L1_attn_kc_w"]).reshape(D, 4))
    vc = np.ascontiguousarray(n(inputs["L1_attn_vc_w"]).reshape(D, 4))
    firw = np.concatenate([n(inputs[f"L1_attn_fir_w{k}"]).reshape(D, k)
                           for k in FIRS], axis=1)
    w1 = n(inputs["L1_attn_mlp_w1"])
    w1h = np.ascontiguousarray(w1[:D])
    stats_rows = {}
    for kind in range(4):
        rows = np.zeros((24, D), np.float32)
        for s_ in range(6):
            for h in range(H):
                rows[s_ * 4 + h] = w1[D + h * 24 + s_ * 4 + kind]
        stats_rows[kind] = np.ascontiguousarray(rows)
    base = {
        "qw": n(inputs["L1_attn_q_w"]), "kw": n(inputs["L1_attn_k_w"]),
        "vw": n(inputs["L1_attn_v_w"]), "bw": n(inputs["L1_attn_b_w"]),
        "ow": n(inputs["L1_attn_o_w"]),
        "qcw": qc, "kcw": kc, "vcw": vc, "firw": firw,
        "w1h": w1h, "w1m": stats_rows[0], "w1s": stats_rows[1],
        "w1a": stats_rows[2], "w1l": stats_rows[3],
        "b1": np.ascontiguousarray(
            n(inputs["L1_attn_mlp_b1"]).reshape(DT, 128).T),
        "w2": n(inputs["L1_attn_mlp_w2"]),
        "b2": n(inputs["L1_attn_mlp_b2"]).reshape(1, 24),
        "glt": n(inputs["L1_attn_gate_log_temp"]).reshape(1, H),
        "onw": np.ascontiguousarray(
            n(inputs["L1_attn_onorm_w"]).reshape(2, 128).T),
        "fbias": n(inputs["L1_ffn_bias"]).reshape(1, D),
        "masksb": l1_masks_np(),
    }
    base = {k: np.ascontiguousarray(np.asarray(v, base[k].dtype
                                               if hasattr(v, 'dtype')
                                               else np.float32))
            for k, v in base.items()}
    return base


FUSED = True


def kernel(**inputs):
    if FUSED:
        if "fused" not in _CACHE:
            _CACHE["fused"] = build_fused()
        maps = fused_in_maps(inputs)
        res = run_bass_kernel_spmd(_CACHE["fused"], maps,
                                   core_ids=list(range(8)))
        out = np.zeros((16, S, D), np.float32)
        for c in range(8):
            out[FB[c]] = res.results[c]["outa"]
            out[c + 8] = res.results[c]["outb"]
        return out
    return _kernel_2launch(**inputs)
